# version 02
# baseline (speedup 1.0000x reference)
import numpy as np
from contextlib import ExitStack, contextmanager

import concourse.bass as bass
import concourse.mybir as mybir
from concourse.bass_utils import run_bass_kernel_spmd

F32 = mybir.dt.float32
BF16 = mybir.dt.bfloat16
AF = mybir.ActivationFunctionType
ALU = mybir.AluOpType
AX = mybir.AxisListType

T = 2048
D = 1024
DC = D // 128
NT = T // 128
NQ = T // 512


NCONST = 64


class Prog:
    def __init__(self, detect_races=True):
        self.nc = bass.Bass("TRN2", target_bir_lowering=False,
                            detect_race_conditions=detect_races)
        self.es = ExitStack()
        self._stack = [self.es]
        self._sems = {}
        self._dma_sems = set()
        self._uid = 0
        self._cvals = []
        self._cdram = self.nc.dram_tensor("fconsts", [128, NCONST], F32,
                                          kind="ExternalInput").ap()
        self._csb = self.sbuf([128, NCONST], F32, "fconsts")
        self.load(self._csb[:], self._cdram[:], "cst")
        for e in (self.nc.tensor, self.nc.vector, self.nc.scalar,
                  self.nc.gpsimd, self.nc.sync):
            self.wait(e, "cst", 16)

    def constf(self, val):
        val = float(val)
        if val not in self._cvals:
            assert len(self._cvals) < NCONST
            self._cvals.append(val)
            idx = self._cvals.index(val)
            self.nc.const_aps.register(F32, val, self._csb[:, idx:idx + 1])
        return val

    def const_array(self):
        arr = np.zeros((128, NCONST), np.float32)
        arr[:, :len(self._cvals)] = np.asarray(self._cvals, np.float32)[None, :]
        return arr

    def name(self, prefix):
        self._uid += 1
        return f"{prefix}{self._uid}"

    def sbuf(self, shape, dt, name="t"):
        return self._stack[-1].enter_context(
            self.nc.sbuf_tensor(self.name(name), list(shape), dt))

    def psum(self, shape, dt=F32, name="p"):
        return self._stack[-1].enter_context(
            self.nc.psum_tensor(self.name(name), list(shape), dt))

    @contextmanager
    def scope(self):
        es = ExitStack()
        self._stack.append(es)
        try:
            yield
        finally:
            self.barrier()
            es.close()
            self._stack.pop()

    def _sem(self, sname):
        if sname not in self._sems:
            h = self.es.enter_context(self.nc.semaphore(self.name("s" + sname)))
            self._sems[sname] = [h, 0]
        return self._sems[sname]

    def inc(self, instr, sname, n=1):
        s = self._sem(sname)
        s[1] += n
        instr.then_inc(s[0], n)
        return s[1]

    def tail(self, sname):
        return self._sem(sname)[1]

    def wait(self, eng, sname, val=None):
        s = self._sem(sname)
        v = s[1] if val is None else val
        if v > 0:
            eng.wait_ge(s[0], v)

    def barrier(self):
        nc = self.nc
        engs = [nc.tensor, nc.vector, nc.scalar, nc.sync, nc.gpsimd]
        s = self._sem("bar")
        base = s[1]
        dmas = [(self._sems[n][0], self._sems[n][1])
                for n in self._dma_sems
                if n in self._sems and self._sems[n][1] > 0]
        for e in engs:
            for h, v in dmas:
                e.wait_ge(h, v)
            e.drain()
            self.inc(e.nop(), "bar", 1)
        for e in engs:
            e.wait_ge(s[0], base + len(engs))

    def load(self, out_ap, in_ap, sname="ld"):
        self._dma_sems.add(sname)
        return self.inc(self.nc.sync.dma_start(out=out_ap, in_=in_ap), sname, 16)

    def loadc(self, out_ap, in_ap, sname="ldg"):
        self._dma_sems.add(sname)
        return self.inc(self.nc.gpsimd.dma_start(out=out_ap, in_=in_ap), sname, 16)

    def store(self, out_ap, in_ap, sname="st"):
        self._dma_sems.add(sname)
        return self.inc(self.nc.sync.dma_start(out=out_ap, in_=in_ap), sname, 16)


def load_w(p, w_dram, K, M, sname="ldg"):
    wsb = p.sbuf([128, (K // 128) * M], BF16, "w")
    tok = None
    for kc in range(K // 128):
        tok = p.loadc(wsb[:, kc * M:(kc + 1) * M],
                      w_dram[kc * 128:(kc + 1) * 128, :], sname)
    return wsb, tok


def w_tile(wsb, M, kc, j):
    return wsb[:, kc * M + j * 128: kc * M + (j + 1) * 128]


def ln_to_xT(p, x_dram, xT, identb, eps=1e-5, gain=None, bias_=None,
             nchunk=DC, tag="ln"):
    nc = p.nc
    tag = "ln"
    s_ld, s_mu, s_xc, s_t, s_rstd, s_xn, s_mm, s_ev = (
        tag + k for k in ("ld", "mu", "xc", "t", "rstd", "xn", "mm", "ev"))
    B = {s: p.tail(s) for s in (s_xc, s_t, s_rstd, s_xn, s_mm, s_ev,
                                s_xn + "m", s_rstd + "w")}
    p._lnscope = p.scope()
    p._lnscope.__enter__()
    xt = [p.sbuf([128, D], F32, "lnx"), p.sbuf([128, D], F32, "lnx")]
    xc = p.sbuf([128, D], F32, "lnxc")
    tmid = p.sbuf([128, D], F32, "lnmid")
    xnb = p.sbuf([128, D], BF16, "lnxn")
    sq = p.sbuf([128, D], F32, "lnsq")
    s1 = p.sbuf([128, 1], F32, "s1")
    s2 = p.sbuf([128, 1], F32, "s2")
    mu = p.sbuf([128, 1], F32, "mu")
    lnv = p.sbuf([128, 1], F32, "lnv")
    rstd = p.sbuf([128, 1], F32, "rstd")
    tp = p.psum([128, D], BF16, "tp")
    npass = 2 if gain is not None else 1

    def one_pass(src_buf, dst_buf, k):
        nc.vector.drain()
        p.inc(nc.vector.reduce_sum(out=s1[:], in_=src_buf[:], axis=AX.X), s_mu)
        nc.vector.drain()
        nc.vector.tensor_scalar_mul(mu[:], s1[:], 1.0 / D)
        nc.vector.drain()
        p.inc(nc.vector.tensor_single_scalar(out=xc[:], in_=src_buf[:],
                                             scalar=mu[:], op=ALU.subtract), s_xc)
        p.wait(nc.scalar, s_xc, B[s_xc] + k + 1)
        p.wait(nc.scalar, s_rstd + "w", B[s_rstd + "w"] + k)
        p.inc(nc.scalar.activation(out=sq[:], in_=xc[:], func=AF.Square,
                                   accum_out=s2[:]), s_t)
        p.wait(nc.scalar, s_t)
        nc.scalar.activation(out=lnv[:], in_=s2[:], func=AF.Ln,
                             scale=1.0 / D, bias=p.constf(eps))
        nc.scalar.drain()
        p.inc(nc.scalar.activation(out=rstd[:], in_=lnv[:], func=AF.Exp,
                                   scale=-0.5), s_rstd)
        p.wait(nc.vector, s_rstd, B[s_rstd] + k + 1)
        nc.vector.drain()
        i = nc.vector.tensor_scalar_mul(dst_buf[:], xc[:], rstd[:])
        p.inc(i, s_rstd + "w")
        return i

    for tt in range(NT):
        buf = xt[tt % 2]
        par = s_ld + str(tt % 2)
        p.wait(nc.sync, s_xn, B[s_xn] + tt - 1)
        lt = p.load(buf[:], x_dram[tt * 128:(tt + 1) * 128, :], par)
        p.wait(nc.vector, par, lt)
        p.wait(nc.vector, s_mm, B[s_mm] + tt * nchunk)
        if gain is None:
            i5 = one_pass(buf, tmid, tt)
            nc.vector.drain()
            i5 = nc.vector.tensor_copy(out=xnb[:], in_=tmid[:])
        else:
            one_pass(buf, tmid, 2 * tt)
            one_pass(tmid, xc, 2 * tt + 1)
            nc.vector.drain()
            nc.vector.tensor_tensor(out=xnb[:], in0=xc[:], in1=gain[:],
                                    op=ALU.mult)
            nc.vector.drain()
            i5 = nc.vector.tensor_tensor(out=xnb[:], in0=xnb[:], in1=bias_[:],
                                         op=ALU.add)
        p.inc(i5, s_xn)
        p.wait(nc.tensor, s_xn, B[s_xn] + tt + 1)
        p.wait(nc.tensor, s_ev, B[s_ev] + tt)
        it = None
        for j in range(nchunk):
            it = nc.tensor.transpose(tp[:, j * 128:(j + 1) * 128],
                                     xnb[:, j * 128:(j + 1) * 128], identb[:])
        p.inc(it, s_mm, nchunk)
        p.wait(nc.vector, s_mm, B[s_mm] + (tt + 1) * nchunk)
        dst = xT[:, :].rearrange("p (c t) -> p c t", c=nchunk)[
            :, :, tt * 128:(tt + 1) * 128]
        srcv = tp[:, :].rearrange("p (c t) -> p c t", c=nchunk)
        p.inc(nc.vector.tensor_copy(out=dst, in_=srcv), s_ev)
    p._lnscope.__exit__(None, None, None)
    return p.tail(s_ev)


def mm_ff(p, wsb, K, M, rhsT, evac, rhs_T=T, tag="mm"):
    nc = p.nc
    KC = K // 128
    nq = rhs_T // 512
    s_mm, s_ev = tag + "m", tag + "e"
    ps = [p.psum([128, 512], F32, "mmps"), p.psum([128, 512], F32, "mmps")]
    slot_tok = [0, 0]
    si = 0
    for j in range(M // 128):
        for n in range(nq):
            if slot_tok[si]:
                p.wait(nc.tensor, s_ev, slot_tok[si])
            im = None
            for kc in range(KC):
                im = nc.tensor.matmul(
                    ps[si][:], w_tile(wsb, M, kc, j),
                    rhsT[:, kc * rhs_T + n * 512:kc * rhs_T + (n + 1) * 512],
                    start=(kc == 0), stop=(kc == KC - 1))
            mtok = p.inc(im, s_mm)
            ie = evac(j, n, ps[si], mtok)
            slot_tok[si] = p.inc(ie, s_ev)
            si ^= 1
    return p.tail(s_ev)


def attention_c(p, qT, kT, vT, ctxT_out, H, dh, scale, identb, identq=None,
                shift=12.0, tag=None):
    tag = "at"
    """identq: [128, dh] identity stack (eye(dh) tiled) for dh<128 transposes."""
    """Scheme-C attention. qT/kT/vT/ctxT_out: sbuf bf16 [128, (H*dh/128)*T]."""
    nc = p.nc
    KCH = T // 128
    hpp = 128 // dh
    s_vt, s_ve, s_sm, s_pe, s_cm, s_su, s_rs, s_bm, s_be, s_ce = (
        tag + k for k in ("vt", "ve", "sm", "pe", "cm", "su", "rs", "bm", "be",
                          "ce"))
    with p.scope():
        PTb = [p.sbuf([128, KCH * 512], BF16, "PT") for _ in range(hpp)]
        Vp = [p.sbuf([128, KCH * dh], BF16, "Vp") for _ in range(hpp)]
        vtp = p.psum([128, 512], BF16, "vtp")
        sps = [p.psum([128, 512], F32, "sps"), p.psum([128, 512], F32, "sps")]
        cps = p.psum([128, 512], F32, "cps")
        sm = [p.psum([1, 512], F32, "sm") for _ in range(hpp)]
        rs = [p.sbuf([1, 512], F32, "rs") for _ in range(hpp)]
        bcp = p.psum([128, 512], F32, "bcp")
        bcs = p.sbuf([128, 512], F32, "bcs")
        ones1 = p.sbuf([1, 128], F32, "ones1")
        onesk = p.sbuf([128, 1], BF16, "onesk")
        nc.vector.memset(ones1[:], 1.0)
        p.inc(nc.vector.memset(onesk[:], 1.0), tag + "init")
        p.wait(nc.tensor, tag + "init")

        vslots = 512 // dh
        for hp in range(H // hpp):
            heads = [hp * hpp + i for i in range(hpp)]
            for i, h in enumerate(heads):
                c0, r0 = (h * dh) // 128, (h * dh) % 128
                for g in range(KCH // vslots):
                    p.wait(nc.tensor, s_ve, p.tail(s_ve))
                    it = None
                    for s in range(vslots):
                        kc = g * vslots + s
                        idq = identb if dh == 128 else identq
                        it = nc.tensor.transpose(
                            vtp[:, s * dh:(s + 1) * dh],
                            vT[r0:r0 + dh,
                               c0 * T + kc * 128: c0 * T + (kc + 1) * 128],
                            idq[r0:r0 + dh, 0:dh])
                    vt = p.inc(it, s_vt)
                    p.wait(nc.vector, s_vt, vt)
                    p.inc(nc.vector.tensor_copy(
                        out=Vp[i][:, g * 512:(g + 1) * 512], in_=vtp[:]), s_ve)
            for n in range(NQ):
                for i, h in enumerate(heads):
                    c0, r0 = (h * dh) // 128, (h * dh) % 128
                    p.wait(nc.scalar, s_su, p.tail(s_su))
                    si = 0
                    for kc in range(KCH):
                        if p.tail(s_pe) >= 2:
                            p.wait(nc.tensor, s_pe, p.tail(s_pe) - 1)
                        im = nc.tensor.matmul(
                            sps[si][:],
                            kT[r0:r0 + dh,
                               c0 * T + kc * 128: c0 * T + (kc + 1) * 128],
                            qT[r0:r0 + dh,
                               c0 * T + n * 512: c0 * T + (n + 1) * 512],
                            start=True, stop=True)
                        tm = p.inc(im, s_sm)
                        p.wait(nc.scalar, s_sm, tm)
                        p.inc(nc.scalar.activation(
                            out=PTb[i][:, kc * 512:(kc + 1) * 512],
                            in_=sps[si][:], func=AF.Exp, scale=float(scale),
                            bias=p.constf(-shift)), s_pe)
                        si ^= 1
                    p.wait(nc.tensor, s_pe, p.tail(s_pe))
                    p.wait(nc.tensor, s_ce, p.tail(s_ce))
                    im = None
                    for kc in range(KCH):
                        im = nc.tensor.matmul(
                            cps[i * dh:(i + 1) * dh, :],
                            Vp[i][:, kc * dh:(kc + 1) * dh],
                            PTb[i][:, kc * 512:(kc + 1) * 512],
                            start=(kc == 0), stop=(kc == KCH - 1))
                    p.inc(im, s_cm)
                    p.wait(nc.tensor, s_rs, p.tail(s_rs))
                    im = None
                    for kc in range(KCH):
                        im = nc.tensor.matmul(
                            sm[i][:], onesk[:, 0:1],
                            PTb[i][:, kc * 512:(kc + 1) * 512],
                            start=(kc == 0), stop=(kc == KCH - 1))
                    sut = p.inc(im, s_su)
                    p.wait(nc.vector, s_su, sut)
                    p.wait(nc.vector, s_bm, p.tail(s_bm))
                    p.inc(nc.vector.reciprocal(out=rs[i][:], in_=sm[i][:]), s_rs)
                p.wait(nc.tensor, s_rs, p.tail(s_rs))
                p.wait(nc.tensor, s_be, p.tail(s_be))
                ib = None
                for i in range(hpp):
                    ib = nc.tensor.matmul(bcp[i * dh:(i + 1) * dh, :],
                                          ones1[0:1, 0:dh], rs[i][:],
                                          start=True, stop=True)
                bt = p.inc(ib, s_bm)
                p.wait(nc.vector, s_bm, bt)
                p.inc(nc.vector.tensor_copy(out=bcs[:], in_=bcp[:]), s_be)
                nc.vector.drain()
                p.wait(nc.vector, s_cm, p.tail(s_cm))
                c0 = (heads[0] * dh) // 128
                p.inc(nc.vector.tensor_tensor(
                    out=ctxT_out[:, c0 * T + n * 512: c0 * T + (n + 1) * 512],
                    in0=cps[:], in1=bcs[:], op=ALU.mult), s_ce)


def residual_store(p, srcT, x_dram, out_dram, identf, scale_src=1.0, tag="res"):
    nc = p.nc
    s_ld, s_mm, s_ev, s_st = "lnld0", "lnmm", "lnev", "lnst"
    B = {s: p.tail(s) for s in (s_mm, s_ev, s_st)}
    rp = p.psum([128, 1024], F32, "rp")
    xtile = p.sbuf([128, D], F32, "rx")
    otile = p.sbuf([128, D], F32, "ro")
    for tt in range(NT):
        p.wait(nc.sync, s_ev, B[s_ev] + tt)
        lt = p.load(xtile[:], x_dram[tt * 128:(tt + 1) * 128, :], s_ld)
        p.wait(nc.tensor, s_ev, B[s_ev] + tt)
        it = None
        for j in range(DC):
            it = nc.tensor.transpose(
                rp[:, j * 128:(j + 1) * 128],
                srcT[:, j * T + tt * 128: j * T + (tt + 1) * 128], identf[:])
        mt = p.inc(it, s_mm)
        p.wait(nc.vector, s_mm, mt)
        p.wait(nc.vector, s_ld, lt)
        p.wait(nc.vector, s_st, B[s_st] + tt * 16)
        ia = nc.vector.scalar_tensor_tensor(out=otile[:], in0=rp[:],
                                            scalar=float(scale_src), in1=xtile[:],
                                            op0=ALU.mult, op1=ALU.add)
        ev = p.inc(ia, s_ev)
        p.wait(nc.sync, s_ev, ev)
        p.store(out_dram[tt * 128:(tt + 1) * 128, :], otile[:], s_st)
    p.wait(nc.sync, s_st)


def sub_l0attn(p, x_dram, out_dram, w):
    nc = p.nc
    H, dh = 16, 64
    with p.scope():
        identb = p.sbuf([128, 128], BF16, "idb")
        identf = p.sbuf([128, 128], F32, "idf")
        identq = p.sbuf([128, 64], BF16, "idq")
        p.loadc(identb[:], w["ident"][:])
        p.load(identf[:], w["ident"][:])
        p.loadc(identq[:], w["ident64s"][:])
        p.barrier()

        ctxT = p.sbuf([128, 8 * T], BF16, "ctxT")
        with p.scope():
            qkvT = p.sbuf([128, 24 * T], BF16, "qkvT")
            with p.scope():
                xT = p.sbuf([128, DC * T], BF16, "xT")
                gainb = p.sbuf([128, D], BF16, "gain")
                biasb = p.sbuf([128, D], BF16, "bias")
                p.loadc(gainb[:], w["ln_gamma_bc"][:])
                p.loadc(biasb[:], w["ln_beta_bc"][:])
                p.barrier()
                ln_to_xT(p, x_dram, xT, identb, eps=1e-5, gain=gainb, bias_=biasb)
                p.barrier()

                def evac_qkv(half):
                    def f(j, n, ps, mtok):
                        jj = j + 12 * half
                        p.wait(nc.scalar, "mmm", mtok)
                        return nc.scalar.activation(
                            out=qkvT[:, jj * T + n * 512:jj * T + (n + 1) * 512],
                            in_=ps[:], func=AF.Copy)
                    return f
                for half in range(2):
                    with p.scope():
                        wq, _ = load_w(p, w["w_qkv"][:, half * 1536:(half + 1) * 1536],
                                       1024, 1536)
                        p.barrier()
                        mm_ff(p, wq, 1024, 1536, xT, evac_qkv(half))

            qT = qkvT[:, 0 * T: 8 * T]
            kT = qkvT[:, 8 * T: 16 * T]
            vT = qkvT[:, 16 * T: 24 * T]

            with p.scope():
                pib = p.sbuf([128, 128], BF16, "pib")
                cosF = p.sbuf([128, T], BF16, "cosF")
                sinF = p.sbuf([128, T], BF16, "sinF")
                p.loadc(pib[:], w["rope_pi"][:])
                p.loadc(cosF[:], w["rope_cos"][:])
                p.loadc(sinF[:], w["rope_sin"][:])
                p.barrier()
                swp = p.psum([128, 512], F32, "swp")
                sws = p.sbuf([128, 512], BF16, "sws")
                tmp = p.sbuf([128, 512], BF16, "ropet")
                for base in (0, 8 * T):
                    for c in range(8):
                        for n in range(NQ):
                            sl = slice(base + c * T + n * 512,
                                       base + c * T + (n + 1) * 512)
                            p.wait(nc.tensor, "rpsw", p.tail("rpsw"))
                            im = nc.tensor.matmul(swp[:], pib[:], qkvT[:, sl],
                                                  start=True, stop=True)
                            mt = p.inc(im, "rpmm")
                            p.wait(nc.scalar, "rpmm", mt)
                            p.wait(nc.scalar, "rpev", p.tail("rpev"))
                            isw = nc.scalar.activation(out=sws[:], in_=swp[:],
                                                       func=AF.Copy)
                            swt = p.inc(isw, "rpsw")
                            p.wait(nc.vector, "rpsw", swt)
                            nc.vector.tensor_tensor(
                                out=tmp[:], in0=sws[:],
                                in1=sinF[:, n * 512:(n + 1) * 512], op=ALU.mult)
                            nc.vector.tensor_tensor(
                                out=qkvT[:, sl], in0=qkvT[:, sl],
                                in1=cosF[:, n * 512:(n + 1) * 512], op=ALU.mult)
                            nc.vector.drain()
                            p.inc(nc.vector.tensor_tensor(
                                out=qkvT[:, sl], in0=qkvT[:, sl], in1=tmp[:],
                                op=ALU.add), "rpev")

            attention_c(p, qT, kT, vT, ctxT, H, dh, dh ** -0.5, identb, identq)

        with p.scope():
            oT = p.sbuf([128, DC * T], F32, "oT")
            wo, _ = load_w(p, w["w_out"], 1024, 1024)
            bout = p.sbuf([128, DC], F32, "bout")
            for j in range(DC):
                p.load(bout[:, j:j + 1], w["b_out"][j * 128:(j + 1) * 128, :], "ldb")
            p.barrier()

            def evac_o(j, n, ps, mtok):
                p.wait(nc.vector, "mmm", mtok)
                p.wait(nc.vector, "ldb", p.tail("ldb"))
                return nc.vector.tensor_scalar_add(
                    oT[:, j * T + n * 512:j * T + (n + 1) * 512],
                    ps[:], bout[:, j:j + 1])
            mm_ff(p, wo, 1024, 1024, ctxT, evac_o)
            p.barrier()
            residual_store(p, oT, x_dram, out_dram, identf, scale_src=1.0)


def rope_tables(n=T, head_dim=64):
    inv_freq = 1.0 / 10000.0 ** (np.arange(0, head_dim, 2, dtype=np.float64) / head_dim)
    pos = np.arange(n, dtype=np.float64)
    ang = pos[:, None] * inv_freq[None, :]
    sin, cos = np.sin(ang), np.cos(ang)
    cosF = np.zeros((128, n), np.float32)
    sinF = np.zeros((128, n), np.float32)
    for r in range(128):
        i = r % 64
        f = i % 32
        cosF[r] = cos[:, f]
        sinF[r] = sin[:, f] if i >= 32 else -sin[:, f]
    pi = np.zeros((128, 128), np.float32)
    for r in range(128):
        i = r % 64
        partner = r + 32 if i < 32 else r - 32
        pi[partner, r] = 1.0
    return cosF, sinF, pi


def l0attn_consts():
    cosF, sinF, pi = rope_tables()
    return {
        "ident": np.eye(128, dtype=np.float32),
        "ident64s": np.ascontiguousarray(np.tile(np.eye(64, dtype=np.float32), (2, 1))),
        "rope_pi": pi, "rope_cos": cosF, "rope_sin": sinF,
    }


L0F_EDGES = {0: (0, 0, 0), 1: (2, 1, 2), 2: (5, 3, 4),
             3: (9, 6, 8), 4: (14, 10, 12), 5: (19, 15, 17)}
L0F_INS = {0: (-1, -1, -1), 1: (0, -1, 0), 2: (1, -1, 0),
           3: (2, -1, 1), 4: (3, -1, 1), 5: (4, 0, 2)}


def ln_residual_store(p, srcT, x_dram, out_dram, identf, gain, bias_,
                      eps, scale_src, tag="lrs"):
    nc = p.nc
    s_ld, s_mm, s_xc, s_rstd, s_ev, s_st = (
        "lnld0", "lnmm", "lnxc", "lnrstd", "lnev", "lnst")
    B = {s: p.tail(s) for s in (s_mm, s_xc, s_rstd, s_ev, s_st,
                                s_rstd + "w")}
    rp = p.psum([128, 1024], F32, "rp")
    xtile = p.sbuf([128, D], F32, "rx")
    xc = p.sbuf([128, D], F32, "rxc")
    sq = p.sbuf([128, D], F32, "rsq")
    otile = p.sbuf([128, D], F32, "ro")
    s1 = p.sbuf([128, 1], F32, "s1")
    s2 = p.sbuf([128, 1], F32, "s2")
    mu = p.sbuf([128, 1], F32, "mu")
    lnv = p.sbuf([128, 1], F32, "lnv")
    rstd = p.sbuf([128, 1], F32, "rstd")
    for tt in range(NT):
        p.wait(nc.sync, s_ev, B[s_ev] + tt)
        lt = p.load(xtile[:], x_dram[tt * 128:(tt + 1) * 128, :], s_ld)
        p.wait(nc.tensor, s_xc, B[s_xc] + tt)
        it = None
        for j in range(DC):
            it = nc.tensor.transpose(
                rp[:, j * 128:(j + 1) * 128],
                srcT[:, j * T + tt * 128: j * T + (tt + 1) * 128], identf[:])
        mt = p.inc(it, s_mm)
        p.wait(nc.vector, s_mm, mt)
        nc.vector.reduce_sum(out=s1[:], in_=rp[:], axis=AX.X)
        nc.vector.drain()
        nc.vector.tensor_scalar_mul(mu[:], s1[:], 1.0 / D)
        nc.vector.drain()
        p.inc(nc.vector.tensor_single_scalar(out=xc[:], in_=rp[:],
                                             scalar=mu[:], op=ALU.subtract), s_xc)
        p.wait(nc.scalar, s_xc, B[s_xc] + tt + 1)
        p.wait(nc.scalar, s_rstd + "w", B[s_rstd + "w"] + tt)
        p.inc(nc.scalar.activation(out=sq[:], in_=xc[:], func=AF.Square,
                                   accum_out=s2[:]), "lnt")
        p.wait(nc.scalar, "lnt")
        nc.scalar.activation(out=lnv[:], in_=s2[:], func=AF.Ln,
                             scale=1.0 / D, bias=p.constf(eps))
        nc.scalar.drain()
        p.inc(nc.scalar.activation(out=rstd[:], in_=lnv[:], func=AF.Exp,
                                   scale=-0.5), s_rstd)
        p.wait(nc.vector, s_rstd, B[s_rstd] + tt + 1)
        p.wait(nc.vector, s_ld, lt)
        p.wait(nc.vector, s_st, B[s_st] + tt * 16)
        nc.vector.drain()
        nc.vector.tensor_scalar_mul(otile[:], xc[:], rstd[:])
        nc.vector.drain()
        nc.vector.tensor_tensor(out=otile[:], in0=otile[:], in1=gain[:], op=ALU.mult)
        nc.vector.drain()
        p.inc(nc.vector.tensor_tensor(out=otile[:], in0=otile[:], in1=bias_[:],
                                      op=ALU.add), s_rstd + "w")
        nc.vector.drain()
        iv = nc.vector.scalar_tensor_tensor(out=otile[:], in0=otile[:],
                                            scalar=float(scale_src), in1=xtile[:],
                                            op0=ALU.mult, op1=ALU.add)
        ev = p.inc(iv, s_ev)
        p.wait(nc.sync, s_ev, ev)
        p.store(out_dram[tt * 128:(tt + 1) * 128, :], otile[:], s_st)
    p.wait(nc.sync, s_st)


def sub_l0ffn(p, x_dram, out_dram, w, tag):
    nc = p.nc
    H, dh = 8, 128

    def Wsl(name, e):
        return w[name][e * 1024:(e + 1) * 1024, :]

    with p.scope():
        identb = p.sbuf([128, 128], BF16, "idb")
        identf = p.sbuf([128, 128], F32, "idf")
        ewb = p.sbuf([128, 18], F32, "ewb")
        awb = p.sbuf([128, 6], F32, "awb")
        p.loadc(identb[:], w["ident"][:])
        p.load(identf[:], w["ident"][:])
        p.load(ewb[:], w["ew_bc"][:])
        p.load(awb[:], w["actw_bc"][:])
        nodes_d = {c: nc.dram_tensor(p.name(f"{tag}nd{c}"), [128, DC * T], BF16,
                                     kind="Internal").ap()
                   for c in range(-1, 6)}
        p.barrier()

        with p.scope():
            xT = p.sbuf([128, DC * T], BF16, "xT")
            ln_to_xT(p, x_dram, xT, identb, eps=1e-5, gain=None, tag=tag + "ln")
            p.barrier()
            p.store(nodes_d[-1][:], xT[:], tag + "nst")
            p.wait(nc.sync, tag + "nst")
        p.barrier()

        TMP = p.sbuf([128, DC * T], BF16, "sTMP")
        Q = p.sbuf([128, DC * T], BF16, "sQ")
        K = p.sbuf([128, DC * T], BF16, "sK")
        MV = TMP
        CTX = K

        def edge_mm(e, src_c, dest, ew_idx=None):
            with p.scope():
                ldt = p.load(TMP[:], nodes_d[src_c][:], "ldg")
                wsb, _ = load_w(p, Wsl("We", e), 1024, 1024)
                beb = p.sbuf([128, DC], F32, "beb")
                for j in range(DC):
                    p.load(beb[:, j:j + 1],
                           w["be"][e * 1024 + j * 128: e * 1024 + (j + 1) * 128, :],
                           "ld")
                p.barrier()

                def evac(j, n, ps, mtok):
                    p.wait(nc.vector, "mmm", mtok)
                    sl = slice(j * T + n * 512, j * T + (n + 1) * 512)
                    if ew_idx is None:
                        return nc.vector.tensor_scalar_add(dest[:, sl], ps[:],
                                                           beb[:, j:j + 1])
                    return nc.vector.tensor_scalar(
                        out=dest[:, sl], in0=ps[:], scalar1=beb[:, j:j + 1],
                        scalar2=ewb[:, ew_idx:ew_idx + 1], op0=ALU.add,
                        op1=ALU.mult)
                mm_ff(p, wsb, 1024, 1024, TMP, evac)

        def proj_mm(wname, c, src, dest, ew_idx):
            with p.scope():
                wsb, _ = load_w(p, Wsl(wname, c), 1024, 1024)
                p.barrier()

                def evac(j, n, ps, mtok):
                    p.wait(nc.vector, "mmm", mtok)
                    sl = slice(j * T + n * 512, j * T + (n + 1) * 512)
                    return nc.vector.tensor_scalar_mul(
                        dest[:, sl], ps[:], ewb[:, ew_idx:ew_idx + 1])
                mm_ff(p, wsb, 1024, 1024, src, evac)

        for c in range(6):
            eq, ek, ev = L0F_EDGES[c]
            iq, ik, iv_ = L0F_INS[c]
            ew_q, ew_k, ew_v = 3 * c + 0, 3 * c + 1, 3 * c + 2
            last = (c == 5)
            with p.scope():
                MQ = p.sbuf([128, DC * T], BF16, "sMQ")
                MK = p.sbuf([128, DC * T], BF16, "sMK")
                edge_mm(eq, iq, Q)
                p.barrier()
                proj_mm("Wq", c, Q, MQ, ew_q)
                p.barrier()
                if (ek, ik) == (eq, iq):
                    ksrc = Q
                else:
                    edge_mm(ek, ik, K)
                    p.barrier()
                    ksrc = K
                proj_mm("Wk", c, ksrc, MK, ew_k)
                p.barrier()
                if (ev, iv_) == (eq, iq):
                    vsrc = Q
                elif (ev, iv_) == (ek, ik):
                    vsrc = ksrc
                else:
                    edge_mm(ev, iv_, K)
                    p.barrier()
                    vsrc = K
                proj_mm("Wv", c, vsrc, MV, ew_v)
                p.barrier()
                attention_c(p, MQ, MK, MV, CTX, H, dh, dh ** -0.5, identb,
                            tag=tag + f"a{c}")
                if not last:
                    with p.scope():
                        wsb, _ = load_w(p, Wsl("Wo", c), 1024, 1024)
                        tsc = p.sbuf([128, 512], BF16, "tsc")
                        p.barrier()

                        def evac(j, n, ps, mtok):
                            sl = slice(j * T + n * 512, j * T + (n + 1) * 512)
                            p.wait(nc.vector, "mmm", mtok)
                            nc.vector.tensor_scalar_mul(tsc[:], ps[:],
                                                        awb[:, c:c + 1])
                            nc.vector.drain()
                            return nc.vector.scalar_tensor_tensor(
                                out=MK[:, sl], in0=Q[:, sl],
                                scalar=ewb[:, ew_q:ew_q + 1],
                                in1=tsc[:], op0=ALU.mult, op1=ALU.add)
                        mm_ff(p, wsb, 1024, 1024, CTX, evac)
                        p.barrier()
                        p.store(nodes_d[c][:], MK[:], tag + "nst")
                        p.wait(nc.sync, tag + "nst")
            if last:
                with p.scope():
                    wsb, _ = load_w(p, Wsl("Wo", c), 1024, 1024)
                    n5f = p.sbuf([128, DC * T], F32, "n5f")
                    tsc = p.sbuf([128, 512], F32, "tsc")
                    p.barrier()

                    def evac5(j, n, ps, mtok):
                        sl = slice(j * T + n * 512, j * T + (n + 1) * 512)
                        p.wait(nc.vector, "mmm", mtok)
                        nc.vector.tensor_scalar_mul(tsc[:], ps[:], awb[:, c:c + 1])
                        nc.vector.drain()
                        return nc.vector.scalar_tensor_tensor(
                            out=n5f[:, sl], in0=Q[:, sl],
                            scalar=ewb[:, ew_q:ew_q + 1],
                            in1=tsc[:], op0=ALU.mult, op1=ALU.add)
                    mm_ff(p, wsb, 1024, 1024, CTX, evac5)
                    p.barrier()
                    gf = p.sbuf([128, D], F32, "gf")
                    bf = p.sbuf([128, D], F32, "bf")
                    p.load(gf[:], w["lng_bc"][:])
                    p.load(bf[:], w["lnb_bc"][:])
                    p.barrier()
                    ln_residual_store(p, n5f, x_dram, out_dram, identf,
                                      gf, bf, 1e-6, 0.5, tag=tag + "lrs")
            p.barrier()


def sub_l1ffn(p, x_dram, out_dram, w, tag=None):
    nc = p.nc
    tag = "mo"
    E, HD = 8, 512
    B = {s: p.tail(tag + s) for s in ("gsc", "gnm", "gex", "yz")}
    with p.scope():
        identb = p.sbuf([128, 128], BF16, "idb")
        p.loadc(identb[:], w["ident"][:])
        p.barrier()
        xT = p.sbuf([128, DC * T], BF16, "xT")
        ln_to_xT(p, x_dram, xT, identb, eps=1e-5, gain=None, tag=tag + "ln")
        p.barrier()
        if "dbg_xT" in w:
            xc32 = p.sbuf([128, T], F32, "dbgx")
            for cc in range(DC):
                p.wait(nc.vector, tag + "dbg", cc * 16)
                p.inc(nc.vector.tensor_copy(out=xc32[:], in_=xT[:, cc * T:(cc + 1) * T]), tag + "dx")
                p.wait(nc.sync, tag + "dx", cc + 1)
                p.store(w["dbg_xT"][cc * 128:(cc + 1) * 128, :], xc32[:], tag + "dbg")
            p.barrier()

        scale_sb = p.sbuf([128, NT * E], F32, "gsc")
        with p.scope():
            gwb = p.sbuf([128, E * E], BF16, "gw")
            for kc in range(DC):
                p.loadc(gwb[:, kc * E:(kc + 1) * E],
                        w["gate_w"][kc * 128:(kc + 1) * 128, :], "ldg")
            gps = p.psum([128, E], F32, "gps")
            lg = p.sbuf([128, E], F32, "lg")
            ex = p.sbuf([128, E], F32, "ex")
            sm = p.sbuf([128, E], F32, "sm")
            nmx = p.sbuf([128, 1], F32, "nmx")
            ssum = p.sbuf([128, 1], F32, "ssum")
            rsum = p.sbuf([128, 1], F32, "rs")
            m1 = p.sbuf([128, 1], F32, "m1")
            m2 = p.sbuf([128, 1], F32, "m2")
            msk = p.sbuf([128, E], F32, "msk")
            msk2 = p.sbuf([128, E], F32, "msk2")
            dbgg = p.sbuf([128, 32], F32, "dbgg") if "dbg_gate" in w else None
            dbglg = p.sbuf([128, NT * 8], F32, "dbglg") if "dbg_lg" in w else None
            p.barrier()
            if "dbg_gw" in w:
                gwf = p.sbuf([128, 64], F32, "gwf")
                p.inc(nc.vector.tensor_copy(out=gwf[:], in_=gwb[:]), tag + "dgw")
                p.wait(nc.sync, tag + "dgw")
                p.store(w["dbg_gw"][:], gwf[:], tag + "dbg")
                p.barrier()
            for tt in range(NT):
                p.wait(nc.tensor, tag + "gsc", B["gsc"] + tt)
                im = None
                for kc in range(DC):
                    im = nc.tensor.matmul(gps[:], xT[:, kc * T + tt * 128:
                                                     kc * T + (tt + 1) * 128],
                                          gwb[:, kc * E:(kc + 1) * E],
                                          start=(kc == 0), stop=(kc == DC - 1))
                gm = p.inc(im, tag + "gm")
                p.wait(nc.vector, tag + "gm", gm)
                nc.vector.tensor_copy(out=lg[:], in_=gps[:])
                if "dbg_lg" in w:
                    nc.vector.tensor_copy(out=dbglg[:, tt * 8:(tt + 1) * 8], in_=gps[:])
                nc.vector.drain()
                nc.vector.reduce_max(out=nmx[:], in_=lg[:], axis=AX.X, negate=True)
                nc.vector.drain()
                p.inc(nc.vector.engine_nop(), tag + "gnm")
                p.wait(nc.scalar, tag + "gnm", B["gnm"] + tt + 1)
                p.inc(nc.scalar.activation(out=ex[:], in_=lg[:], func=AF.Exp,
                                           bias=nmx[:]), tag + "gex")
                p.wait(nc.vector, tag + "gex", B["gex"] + tt + 1)
                nc.vector.reduce_sum(out=ssum[:], in_=ex[:], axis=AX.X)
                nc.vector.drain()
                nc.vector.reciprocal(out=rsum[:], in_=ssum[:])
                nc.vector.drain()
                nc.vector.tensor_scalar_mul(sm[:], ex[:], rsum[:])
                nc.vector.drain()
                if "dbg_gate" in w and tt == 0:
                    nc.vector.tensor_copy(out=dbgg[:, 0:8], in_=lg[:])
                    nc.vector.tensor_copy(out=dbgg[:, 8:16], in_=ex[:])
                    nc.vector.tensor_copy(out=dbgg[:, 16:17], in_=ssum[:])
                    nc.vector.tensor_copy(out=dbgg[:, 17:18], in_=rsum[:])
                    nc.vector.tensor_copy(out=dbgg[:, 24:32], in_=sm[:])
                    nc.vector.tensor_copy(out=dbgg[:, 18:19], in_=nmx[:])
                nc.vector.reduce_max(out=m1[:], in_=sm[:], axis=AX.X)
                nc.vector.drain()
                nc.vector.tensor_single_scalar(out=msk[:], in_=sm[:], scalar=m1[:],
                                               op=ALU.is_ge)
                nc.vector.drain()
                nc.vector.scalar_tensor_tensor(out=msk2[:], in0=msk[:], scalar=-1.0,
                                               in1=sm[:], op0=ALU.mult,
                                               op1=ALU.mult)
                nc.vector.drain()
                nc.vector.tensor_add(msk2[:], msk2[:], sm[:])
                nc.vector.drain()
                nc.vector.reduce_max(out=m2[:], in_=msk2[:], axis=AX.X)
                nc.vector.drain()
                nc.vector.tensor_single_scalar(out=msk2[:], in_=msk2[:],
                                               scalar=m2[:], op=ALU.is_ge)
                nc.vector.drain()
                nc.vector.tensor_add(msk[:], msk[:], msk2[:])
                nc.vector.drain()
                p.inc(nc.vector.tensor_mul(scale_sb[:, tt * E:(tt + 1) * E],
                                           sm[:], msk[:]), tag + "gsc")
            if "dbg_gate" in w:
                p.store(w["dbg_gate"][:], dbgg[:], tag + "dbg")
            if "dbg_lg" in w:
                p.store(w["dbg_lg"][:], dbglg[:], tag + "dbg")
        p.barrier()
        if "dbg_scale" in w:
            p.store(w["dbg_scale"][:], scale_sb[:], tag + "dbg")
            p.wait(nc.sync, tag + "dbg")

        y = p.sbuf([128, NT * D], F32, "yacc")
        p.inc(nc.vector.memset(y[:], 0.0), tag + "yz")
        sa = p.sbuf([128, HD], F32, "sa")
        hb = p.sbuf([128, HD], F32, "hb")
        hh = p.sbuf([128, HD], BF16, "hh")
        hTs = p.sbuf([128, HD], BF16, "hTs")
        aps = p.psum([128, HD], F32, "aps")
        bps = p.psum([128, HD], F32, "bps")
        tps = p.psum([128, HD], BF16, "tps")
        yps = p.psum([128, D], F32, "yps")
        for e in range(E):
            with p.scope():
                w1b, _ = load_w(p, w["w1"][e * 1024:(e + 1) * 1024, :], 1024, HD)
                w3b, _ = load_w(p, w["w3"][e * 1024:(e + 1) * 1024, :], 1024, HD)
                w2b, _ = load_w(p, w["w2"][e * HD:(e + 1) * HD, :], HD, D)
                p.barrier()
                for tt in range(NT):
                    sc = scale_sb[:, tt * E + e: tt * E + e + 1]
                    p.wait(nc.tensor, "mov", p.tail("mov"))
                    ia = ib = None
                    for kc in range(DC):
                        lhs = xT[:, kc * T + tt * 128: kc * T + (tt + 1) * 128]
                        ia = nc.tensor.matmul(aps[:], lhs, w1b[:, kc * HD:(kc + 1) * HD],
                                              start=(kc == 0), stop=(kc == DC - 1))
                    for kc in range(DC):
                        lhs = xT[:, kc * T + tt * 128: kc * T + (tt + 1) * 128]
                        ib = nc.tensor.matmul(bps[:], lhs, w3b[:, kc * HD:(kc + 1) * HD],
                                              start=(kc == 0), stop=(kc == DC - 1))
                    abm = p.inc(ib, "mot")
                    p.wait(nc.scalar, "mot", abm)
                    p.inc(nc.scalar.activation(out=sa[:], in_=aps[:], func=AF.Silu,
                                               scale=sc), tag + "hsa")
                    p.wait(nc.vector, "mot", abm)
                    nc.vector.tensor_scalar_mul(hb[:], bps[:], sc)
                    nc.vector.drain()
                    p.wait(nc.vector, tag + "hsa", p.tail(tag + "hsa"))
                    ih = nc.vector.tensor_mul(hh[:], sa[:], hb[:])
                    hse = p.inc(ih, "mov")
                    p.wait(nc.tensor, "mov", hse)
                    p.wait(nc.tensor, "mov", p.tail("mov"))
                    it = None
                    for c4 in range(4):
                        it = nc.tensor.transpose(tps[:, c4 * 128:(c4 + 1) * 128],
                                                 hh[:, c4 * 128:(c4 + 1) * 128],
                                                 identb[:])
                    htm = p.inc(it, "mot")
                    p.wait(nc.vector, "mot", htm)
                    ihe = nc.vector.tensor_copy(out=hTs[:], in_=tps[:])
                    hte = p.inc(ihe, "mov")
                    p.wait(nc.tensor, "mov", hte)
                    p.wait(nc.tensor, "mov", p.tail("mov"))
                    iy = None
                    for hc in range(4):
                        for half in range(2):
                            iy = nc.tensor.matmul(
                                yps[:, half * 512:(half + 1) * 512],
                                hTs[:, hc * 128:(hc + 1) * 128],
                                w2b[:, hc * D + half * 512: hc * D + (half + 1) * 512],
                                start=(hc == 0), stop=(hc == 3))
                    ym = p.inc(iy, "mot")
                    p.wait(nc.vector, "mot", ym)
                    if e == 0 and tt == 0:
                        p.wait(nc.vector, tag + "yz", B["yz"] + 1)
                    iya = nc.vector.tensor_add(y[:, tt * D:(tt + 1) * D],
                                               y[:, tt * D:(tt + 1) * D], yps[:])
                    p.inc(iya, "mov")
        p.barrier()
        xtile = p.sbuf([128, D], F32, "rx")
        otile = p.sbuf([128, D], F32, "ro")
        Brev = p.tail("mov")
        Brst = p.tail("lnst")
        for tt in range(NT):
            p.wait(nc.sync, "mov", Brev + tt)
            lt = p.load(xtile[:], x_dram[tt * 128:(tt + 1) * 128, :], "lnld0")
            p.wait(nc.vector, "lnld0", lt)
            p.wait(nc.vector, "lnst", Brst + tt * 16)
            iv = nc.vector.scalar_tensor_tensor(out=otile[:],
                                                in0=y[:, tt * D:(tt + 1) * D],
                                                scalar=0.5, in1=xtile[:],
                                                op0=ALU.mult, op1=ALU.add)
            rev = p.inc(iv, "mov")
            p.wait(nc.sync, "mov", rev)
            p.store(out_dram[tt * 128:(tt + 1) * 128, :], otile[:], "lnst")
        p.wait(nc.sync, "lnst")


def mamba_branch(p, xz_d, ocum_d, w, pfx, tag=None, reverse=False,
                 zmax_sb=None):
    tag = "mb"
    p._dma_sems.add(tag + "oc")
    B = {s: p.tail(tag + s) for s in ("cse", "cvv", "xcw", "dld", "zs", "yd",
                                     "cvd")}
    """One mamba branch. xz_d: DRAM [4096, 2048] fp32 (feature-major).
    ocum_d: DRAM [2048, 2048] fp32 accumulator (write if first=not reverse,
    DMA-add otherwise). reverse: process xz reversed along l (bwd branch).
    zmax_sb: [128, 16] sbuf to fill with per-chunk max_l silu(z) (fwd only).
    """
    nc = p.nc
    DI, DS, DR = 2048, 16, 64
    KC = DI // 128

    with p.scope():
        convw = p.sbuf([128, KC * 4], F32, "cvw")
        convb = p.sbuf([128, KC], F32, "cvb")
        Asb = p.sbuf([128, KC * DS], F32, "A")
        Dsb = p.sbuf([128, KC], F32, "D")
        xpT = p.sbuf([128, KC * 96], BF16, "xpT")
        dtT = p.sbuf([64, DI], BF16, "dtT")
        dtb = p.sbuf([128, KC], F32, "dtb")
        for kc in range(KC):
            r0 = kc * 128
            p.load(convw[:, kc * 4:(kc + 1) * 4], w[pfx + "conv_w"][r0:r0 + 128, :], "ld")
            p.load(convb[:, kc:kc + 1], w[pfx + "conv_b"][r0:r0 + 128, :], "ld")
            p.load(Asb[:, kc * DS:(kc + 1) * DS], w[pfx + "A_log"][r0:r0 + 128, :], "ld")
            p.load(Dsb[:, kc:kc + 1], w[pfx + "Dp"][r0:r0 + 128, :], "ld")
            p.loadc(xpT[:, kc * 96:(kc + 1) * 96], w[pfx + "x_projT"][r0:r0 + 128, :], "ldg")
            p.load(dtb[:, kc:kc + 1], w[pfx + "dt_b"][r0:r0 + 128, :], "ld")
        p.loadc(dtT[:, :], w[pfx + "dt_projT"][:, :], "ldg")
        p.barrier()
        p.inc(nc.scalar.activation(out=Asb[:], in_=Asb[:], func=AF.Exp),
              "mbs")
        p.wait(nc.vector, "mbs")
        nc.vector.drain()
        p.inc(nc.vector.tensor_scalar_mul(Asb[:], Asb[:], -1.0), "mbv")
        p.barrier()

        xconv_d = nc.dram_tensor(p.name(tag + "xc"), [DI, T], F32, kind="Internal").ap()
        delta_d = nc.dram_tensor(p.name(tag + "dl"), [DI, T], F32, kind="Internal").ap()
        bc_d = nc.dram_tensor(p.name(tag + "bc"), [32, T], F32, kind="Internal").ap()

        ph12 = p.scope()
        ph12.__enter__()
        xcb = p.sbuf([128, KC * T], BF16, "xcb")
        with p.scope():
            xzc = p.sbuf([128, T], F32, "xzc")
            xrv = p.sbuf([128, T], F32, "xrv")
            cacc = p.sbuf([128, T], F32, "cacc")
            xcv = p.sbuf([128, T], F32, "xcv")
            for kc in range(KC):
                par = tag + "zl" + str(kc % 2)
                p.wait(nc.sync, tag + "cse", B["cse"] + kc - 1)
                lt = p.load(xzc[:], xz_d[kc * 128:(kc + 1) * 128, :], par)
                p.wait(nc.vector, par, lt)
                if reverse:
                    nc.vector.tensor_copy(out=xrv[:], in_=xzc[:, ::-1])
                    nc.vector.drain()
                    src = xrv
                else:
                    src = xzc
                wc = convw[:, kc * 4:kc * 4 + 4]
                p.wait(nc.vector, tag + "cse", B["cse"] + kc)
                nc.vector.tensor_scalar_mul(cacc[:], src[:], wc[:, 3:4])
                for t_ in (2, 1, 0):
                    sh = 3 - t_
                    nc.vector.drain()
                    nc.vector.scalar_tensor_tensor(
                        out=cacc[:, sh:T], in0=src[:, 0:T - sh],
                        scalar=wc[:, t_:t_ + 1], in1=cacc[:, sh:T],
                        op0=ALU.mult, op1=ALU.add)
                iv = nc.vector.drain()
                p.inc(nc.vector.engine_nop(), tag + "cvv")
                p.wait(nc.scalar, tag + "cvv", B["cvv"] + kc + 1)
                p.wait(nc.scalar, tag + "xcw", B["xcw"] + kc)
                isl = nc.scalar.activation(out=xcv[:], in_=cacc[:], func=AF.Silu,
                                           bias=convb[:, kc:kc + 1])
                cse = p.inc(isl, tag + "cse")
                p.wait(nc.vector, tag + "cse", cse)
                nc.vector.drain()
                ic = nc.vector.tensor_copy(out=xcb[:, kc * T:(kc + 1) * T], in_=xcv[:])
                p.inc(ic, tag + "xcw")
                p.wait(nc.sync, tag + "xcw", p.tail(tag + "xcw"))
                p.store(xconv_d[kc * 128:(kc + 1) * 128, :], xcv[:], tag + "cvd")
            p.wait(nc.sync, tag + "cvd")
        p.barrier()

        dtsb = p.sbuf([64, T], BF16, "dts")
        with p.scope():
            dt_ps = p.psum([64, 512], F32, "dtp")
            b_ps = p.psum([16, 512], F32, "bp")
            c_ps = p.psum([16, 512], F32, "cp")
            Bsb0 = p.sbuf([16, T], F32, "bsb")
            Csb0 = p.sbuf([16, T], F32, "csb")
            for n in range(NQ):
                sl = slice(n * 512, (n + 1) * 512)
                p.wait(nc.tensor, tag + "dbe", p.tail(tag + "dbe"))
                for kc in range(KC):
                    i1 = nc.tensor.matmul(dt_ps[:], xpT[:, kc * 96:kc * 96 + 64],
                                          xcb[:, kc * T + n * 512:kc * T + (n + 1) * 512],
                                          start=(kc == 0), stop=(kc == KC - 1))
                for kc in range(KC):
                    i2 = nc.tensor.matmul(b_ps[:], xpT[:, kc * 96 + 64:kc * 96 + 80],
                                          xcb[:, kc * T + n * 512:kc * T + (n + 1) * 512],
                                          start=(kc == 0), stop=(kc == KC - 1))
                for kc in range(KC):
                    i3 = nc.tensor.matmul(c_ps[:], xpT[:, kc * 96 + 80:kc * 96 + 96],
                                          xcb[:, kc * T + n * 512:kc * T + (n + 1) * 512],
                                          start=(kc == 0), stop=(kc == KC - 1))
                mt = p.inc(i3, tag + "dbm")
                p.wait(nc.vector, tag + "dbm", mt)
                nc.vector.tensor_copy(out=dtsb[:, sl], in_=dt_ps[:])
                nc.vector.tensor_copy(out=Bsb0[:, sl], in_=b_ps[:])
                p.inc(nc.vector.tensor_copy(out=Csb0[:, sl], in_=c_ps[:]),
                      tag + "dbe")
            p.barrier()
            p.store(bc_d[0:16, :], Bsb0[:, :], "st")
            p.store(bc_d[16:32, :], Csb0[:, :], "st")
            p.wait(nc.sync, "st")
        with p.scope():
            d_ps = [p.psum([128, 512], F32, "dps"), p.psum([128, 512], F32, "dps")]
            sabs = p.sbuf([128, 512], F32, "sab")
            sexp = p.sbuf([128, 512], F32, "sex")
            srel = p.sbuf([128, 512], F32, "srl")
            dchunk = p.sbuf([128, T], F32, "dch")
            si = 0
            for jd in range(KC):
                p.wait(nc.vector, tag + "dld", B["dld"] + jd * 16)
                for n in range(NQ):
                    sl = slice(n * 512, (n + 1) * 512)
                    p.wait(nc.tensor, tag + "dle", p.tail(tag + "dle") - 1)
                    im = nc.tensor.matmul(d_ps[si][:],
                                          dtT[:, jd * 128:(jd + 1) * 128],
                                          dtsb[:, sl], start=True, stop=True)
                    mt = p.inc(im, tag + "dlm")
                    p.wait(nc.scalar, tag + "dlm", mt)
                    p.wait(nc.scalar, tag + "dlv", p.tail(tag + "dlv"))
                    nc.scalar.activation(out=srel[:], in_=d_ps[si][:], func=AF.Relu,
                                         bias=dtb[:, jd:jd + 1])
                    nc.scalar.drain()
                    nc.scalar.activation(out=sabs[:], in_=d_ps[si][:], func=AF.Abs,
                                         bias=dtb[:, jd:jd + 1])
                    nc.scalar.drain()
                    nc.scalar.activation(out=sexp[:], in_=sabs[:], func=AF.Exp,
                                         scale=-1.0)
                    nc.scalar.drain()
                    ie = nc.scalar.activation(out=sabs[:], in_=sexp[:], func=AF.Ln,
                                              bias=p.constf(1.0))
                    se = p.inc(ie, tag + "dle")
                    p.wait(nc.vector, tag + "dle", se)
                    iv = nc.vector.tensor_add(dchunk[:, sl], srel[:], sabs[:])
                    p.inc(iv, tag + "dlv")
                    si ^= 1
                p.wait(nc.sync, tag + "dlv")
                p.store(delta_d[jd * 128:(jd + 1) * 128, :], dchunk[:], tag + "dld")
                p.wait(nc.sync, tag + "dld")
        ph12.__exit__(None, None, None)
        p.barrier()

        with p.scope():
            Bball = p.sbuf([128, DS * T], BF16, "Bb")
            Cball = p.sbuf([128, DS * T], BF16, "Cb")
            with p.scope():
                rowst = p.sbuf([128, T], F32, "rwst")
                for n in range(DS):
                    for (dst, roff) in ((Bball, 0), (Cball, 16)):
                        bc_in = bass.AP(tensor=bc_d.tensor, offset=(roff + n) * T,
                                        ap=[[0, 128], [1, T]])
                        lt = p.load(rowst[:], bc_in, "ld")
                        p.wait(nc.vector, "ld", lt)
                        p.inc(nc.vector.tensor_copy(out=dst[:, n * T:(n + 1) * T],
                                                    in_=rowst[:]), tag + "bcc")
                        p.wait(nc.sync, tag + "bcc", p.tail(tag + "bcc"))
            dch = p.sbuf([128, T], F32, "dch")
            xcvc = p.sbuf([128, T], F32, "xcv")
            ub = p.sbuf([128, T], BF16, "ub")
            dAb = p.sbuf([128, T], F32, "dA")
            dBu = p.sbuf([128, T], F32, "dBu")
            hsb = p.sbuf([128, T], BF16, "hs")
            prod = p.sbuf([128, T], F32, "prod")
            yacc = p.sbuf([128, T], F32, "y")
            zch = dch
            zsl = dAb
            for kc in range(KC):
                p.wait(nc.sync, tag + "zs", B["zs"] + kc)
                p.wait(nc.sync, tag + "yd", B["yd"] + kc)
                l1 = p.load(dch[:], delta_d[kc * 128:(kc + 1) * 128, :], "mbl" + str(kc % 2))
                l2 = p.load(xcvc[:], xconv_d[kc * 128:(kc + 1) * 128, :], "mbl" + str(kc % 2))
                p.wait(nc.vector, "mbl" + str(kc % 2), l1)
                p.wait(nc.vector, "mbl" + str(kc % 2), l2)
                iu = nc.vector.tensor_mul(ub[:], dch[:], xcvc[:])
                uu = p.inc(iu, "mbv")
                p.wait(nc.scalar, "mbv", uu)
                nc.vector.memset(yacc[:], 0.0)
                for n in range(DS):
                    p.wait(nc.scalar, "mbv", p.tail("mbv"))
                    ia = nc.scalar.activation(out=dAb[:], in_=dch[:], func=AF.Exp,
                                              scale=Asb[:, kc * DS + n:kc * DS + n + 1])
                    da = p.inc(ia, tag + "sdA")
                    p.wait(nc.vector, "mbv", p.tail("mbv"))
                    ib = nc.vector.tensor_mul(dBu[:], ub[:], Bball[:, n * T:(n + 1) * T])
                    p.inc(ib, "mbv")
                    nc.vector.drain()
                    p.wait(nc.vector, tag + "sdA", da)
                    isc = nc.vector.tensor_tensor_scan(
                        out=hsb[:], data0=dAb[:], data1=dBu[:], initial=0.0,
                        op0=ALU.mult, op1=ALU.add)
                    sc = p.inc(isc, "mbv")
                    nc.vector.drain()
                    ip2 = nc.vector.tensor_mul(prod[:], hsb[:],
                                               Cball[:, n * T:(n + 1) * T])
                    p.inc(ip2, "mbv")
                    nc.vector.drain()
                    p.inc(nc.vector.tensor_add(yacc[:], yacc[:], prod[:]), "mbv")
                nc.vector.drain()
                iy = nc.vector.scalar_tensor_tensor(
                    out=yacc[:], in0=xcvc[:], scalar=Dsb[:, kc:kc + 1], in1=yacc[:],
                    op0=ALU.mult, op1=ALU.add)
                p.inc(iy, tag + "yd")
                p.wait(nc.sync, "mbv", p.tail("mbv"))
                lz = p.load(zch[:], xz_d[DI + kc * 128: DI + (kc + 1) * 128, :],
                            "mbl" + str(kc % 2))
                p.wait(nc.scalar, "mbl" + str(kc % 2), lz)
                p.wait(nc.scalar, "mbv", p.tail("mbv"))
                if reverse:
                    p.wait(nc.vector, "mbl" + str(kc % 2), lz)
                    nc.vector.drain()
                    izr = nc.vector.tensor_copy(out=zsl[:], in_=zch[:, ::-1])
                    p.inc(izr, "mbv")
                    p.wait(nc.scalar, "mbv", p.tail("mbv"))
                    iz = nc.scalar.activation(out=zsl[:], in_=zsl[:], func=AF.Silu)
                else:
                    p.wait(nc.scalar, "mbv", p.tail("mbv"))
                    iz = nc.scalar.activation(out=zsl[:], in_=zch[:], func=AF.Silu)
                zs = p.inc(iz, tag + "zs")
                p.wait(nc.vector, tag + "zs", zs)
                nc.vector.drain()
                if (zmax_sb is not None) and not reverse:
                    nc.vector.reduce_max(out=zmax_sb[:, kc:kc + 1], in_=zsl[:],
                                         axis=AX.X)
                    nc.vector.drain()
                ig = nc.vector.tensor_mul(yacc[:], yacc[:], zsl[:])
                p.inc(ig, "mbv")
                nc.vector.drain()
                if reverse:
                    nc.vector.tensor_copy(out=prod[:], in_=yacc[:, ::-1])
                    out_src = prod
                else:
                    out_src = yacc
                nc.vector.drain()
                gv = p.inc(nc.vector.engine_nop(), "mbv")
                p.wait(nc.gpsimd, "mbv", gv)
                if reverse:
                    p.inc(nc.gpsimd.dma_start(
                        out=ocum_d[kc * 128:(kc + 1) * 128, :], in_=out_src[:],
                        accum_op=ALU.add), tag + "oc", 16)
                else:
                    p.inc(nc.gpsimd.dma_start(
                        out=ocum_d[kc * 128:(kc + 1) * 128, :], in_=out_src[:]),
                        tag + "oc", 16)
                p.wait(nc.vector, tag + "oc", p.tail(tag + "oc"))
        p.barrier()


def sub_l1attn(p, x_dram, out_dram, w, tag=None, with_fft=True,
               branches=("f", "b")):
    nc = p.nc
    tag = "la"
    DI = 2048
    B = {s: p.tail(tag + s) for s in ("ipd", "ybc")}
    with p.scope():
        identb = p.sbuf([128, 128], BF16, "idb")
        identf = p.sbuf([128, 128], F32, "idf")
        p.loadc(identb[:], w["ident"][:])
        p.load(identf[:], w["ident"][:])
        zmax = p.sbuf([128, 16], F32, "zmax")
        p.barrier()
        xz_d = nc.dram_tensor(p.name(tag + "xz"), [2 * DI, T], F32,
                              kind="Internal").ap()
        ocum_d = nc.dram_tensor(p.name(tag + "oc"), [DI, T], F32,
                                kind="Internal").ap()

        with p.scope():
            xT = p.sbuf([128, DC * T], BF16, "xT")
            ln_to_xT(p, x_dram, xT, identb, eps=1e-5, gain=None, tag=tag + "ln")
            p.barrier()
            stg = p.sbuf([128, T], F32, "stg")
            with p.scope():
                wq, _ = load_w(p, w["in_projT"], 1024, 4096)
                p.barrier()

                KCW = 4096 // 128
                ps2 = [p.psum([128, 512], F32, "ip"), p.psum([128, 512], F32, "ip")]
                slot_tok = [0, 0]
                si = 0
                for j in range(KCW):
                    for n in range(NQ):
                        if slot_tok[si]:
                            p.wait(nc.tensor, tag + "ipe", slot_tok[si])
                        im = None
                        for kc in range(DC):
                            im = nc.tensor.matmul(
                                ps2[si][:], w_tile(wq, 4096, kc, j),
                                xT[:, kc * T + n * 512:kc * T + (n + 1) * 512],
                                start=(kc == 0), stop=(kc == DC - 1))
                        mtok = p.inc(im, tag + "ipm")
                        p.wait(nc.scalar, tag + "ipm", mtok)
                        p.wait(nc.scalar, tag + "ipd", B["ipd"] + j * 16)
                        ie = nc.scalar.activation(
                            out=stg[:, n * 512:(n + 1) * 512], in_=ps2[si][:],
                            func=AF.Copy)
                        slot_tok[si] = p.inc(ie, tag + "ipe")
                        si ^= 1
                    p.wait(nc.sync, tag + "ipe", p.tail(tag + "ipe"))
                    p.store(xz_d[j * 128:(j + 1) * 128, :], stg[:], tag + "ipd")
        p.barrier()

        if "f" in branches:
            mamba_branch(p, xz_d, ocum_d, w, "f_", tag + "mf", reverse=False,
                         zmax_sb=zmax)
        if "b" in branches:
            mamba_branch(p, xz_d, ocum_d, w, "b_", tag + "mb",
                         reverse=True)
        if "f" not in branches:
            pass

        if with_fft:
            fft_branch(p, xz_d, ocum_d, w, zmax, identb, tag + "ff")

        with p.scope():
            oT = p.sbuf([128, DC * T], F32, "oT")
            wo, _ = load_w(p, w["out_projT"], 2048, 1024)
            ych = p.sbuf([128, T], BF16, "ych")
            yb = p.sbuf([128, 16 * T], BF16, "yb")
            for kc in range(16):
                par = tag + "yl" + str(kc % 2)
                p.wait(nc.gpsimd, tag + "ybc", B["ybc"] + kc)
                lt = p.loadc(ych[:], ocum_d[kc * 128:(kc + 1) * 128, :], par)
                p.wait(nc.vector, par, lt)
                nc.vector.drain()
                p.inc(nc.vector.tensor_copy(out=yb[:, kc * T:(kc + 1) * T],
                                            in_=ych[:]), tag + "ybc")
            p.barrier()

            def evac_o(j, n, ps, mtok):
                p.wait(nc.scalar, "mmm", mtok)
                return nc.scalar.activation(
                    out=oT[:, j * T + n * 512:j * T + (n + 1) * 512],
                    in_=ps[:], func=AF.Copy)
            mm_ff(p, wo, 2048, 1024, yb, evac_o, rhs_T=T)
            p.barrier()
            residual_store(p, oT, x_dram, out_dram, identf, scale_src=1.0,
                           tag=tag + "res")


def chunk_major(dram_ap, col0, ncols, nrows=2048):
    W = dram_ap.ap[-1][1]
    base = dram_ap.offset + col0
    return bass.AP(tensor=dram_ap.tensor, offset=base,
                   ap=[[W, 128], [128 * W, nrows // 128], [1, ncols]])


def dft_pass(p, in_re_d, in_im_d, out_re_d, out_im_d, fre_d, fim_d, fimn_d,
             identb, tag=None, in_fp32=False):
    tag = "df"
    """Complex DFT along the partition axis of in (2048), with output tiles
    transposed on evacuation so out dims are [transform-axis-free, other].
    in_im_d None => real input. F matrices dram bf16 [2048, 2048].
    All in/out DRAM [2048, 2048]; in bf16 (or fp32 with cast), out bf16."""
    nc = p.nc
    cplx = in_im_d is not None
    with p.scope():
        slabs = 2 if cplx else 1
        inslab = p.sbuf([128, 16 * 512 * slabs], BF16, "dfin")

        def in_sl(kc, which):
            off = which * 16 * 512
            return inslab[:, off + kc * 512: off + (kc + 1) * 512]
        fsl = p.sbuf([128, 16 * 128 * 3], BF16, "dfF")

        def f_sl(kc, which):
            off = which * 16 * 128
            return fsl[:, off + kc * 128: off + (kc + 1) * 128]
        pre = p.psum([128, 512], F32, "dfr")
        pim = p.psum([128, 512], F32, "dfi")
        ptr = p.psum([128, 512], BF16, "dft")
        ev1 = p.sbuf([128, 512], BF16, "dfe")
        ev2 = p.sbuf([128, 512], BF16, "dfe")
        def slab_ap(dram, cols):
            return dram.rearrange("(kc p) c -> p (kc c)", p=128)[:, :] \
                if cols is None else dram
        for n in range(NQ):
            p.wait(nc.gpsimd, tag + "mm", p.tail(tag + "mm"))
            p.loadc(inslab[:, 0:16 * 512], chunk_major(in_re_d, n * 512, 512),
                    tag + "il")
            if cplx:
                p.loadc(inslab[:, 16 * 512:32 * 512],
                        chunk_major(in_im_d, n * 512, 512), tag + "il")
            for j in range(16):
                p.wait(nc.gpsimd, tag + "mm", p.tail(tag + "mm"))
                for wi, fd in enumerate((fre_d, fim_d) + ((fimn_d,) if cplx else ())):
                    p.loadc(fsl[:, wi * 16 * 128:(wi + 1) * 16 * 128],
                            chunk_major(fd, j * 128, 128), tag + "fl")
                p.wait(nc.tensor, tag + "il", p.tail(tag + "il"))
                p.wait(nc.tensor, tag + "fl", p.tail(tag + "fl"))
                p.wait(nc.tensor, tag + "e1", p.tail(tag + "e1"))
                im1 = im2 = None
                for kc in range(16):
                    im1 = nc.tensor.matmul(pre[:], f_sl(kc, 0), in_sl(kc, 0),
                                           start=(kc == 0), stop=(kc == 15) and not cplx)
                if cplx:
                    for kc in range(16):
                        im1 = nc.tensor.matmul(pre[:], f_sl(kc, 2), in_sl(kc, 1),
                                               start=False, stop=(kc == 15))
                for kc in range(16):
                    im2 = nc.tensor.matmul(pim[:], f_sl(kc, 1), in_sl(kc, 0),
                                           start=(kc == 0), stop=(kc == 15) and not cplx)
                if cplx:
                    for kc in range(16):
                        im2 = nc.tensor.matmul(pim[:], f_sl(kc, 0), in_sl(kc, 1),
                                               start=False, stop=(kc == 15))
                mt = p.inc(im2, tag + "mm")
                for (ps, od) in ((pre, out_re_d), (pim, out_im_d)):
                    p.wait(nc.vector, tag + "mm", mt)
                    p.wait(nc.vector, tag + "tt", p.tail(tag + "tt"))
                    e1 = nc.vector.tensor_copy(out=ev1[:], in_=ps[:])
                    ee = p.inc(e1, tag + "e1")
                    p.wait(nc.tensor, tag + "e1", ee)
                    p.wait(nc.tensor, tag + "e2", p.tail(tag + "e2"))
                    it = None
                    for s in range(4):
                        it = nc.tensor.transpose(ptr[:, s * 128:(s + 1) * 128],
                                                 ev1[:, s * 128:(s + 1) * 128],
                                                 identb[:])
                    tt2 = p.inc(it, tag + "tt")
                    p.wait(nc.vector, tag + "tt", tt2)
                    p.wait(nc.vector, tag + "st", p.tail(tag + "st"))
                    e2 = nc.vector.tensor_copy(out=ev2[:], in_=ptr[:])
                    e2t = p.inc(e2, tag + "e2")
                    p.wait(nc.sync, tag + "e2", e2t)
                    for s in range(4):
                        p.store(od[n * 512 + s * 128: n * 512 + (s + 1) * 128,
                                   j * 128:(j + 1) * 128],
                                ev2[:, s * 128:(s + 1) * 128], tag + "st")
    p.barrier()


def fft_branch(p, xz_d, ocum_d, w, zmax, identb, tag=None):
    nc = p.nc
    tag = "ff"
    N2 = float(T) * float(T)
    X1r = nc.dram_tensor(p.name(tag + "x1r"), [T, T], BF16, kind="Internal").ap()
    X1i = nc.dram_tensor(p.name(tag + "x1i"), [T, T], BF16, kind="Internal").ap()
    X2r = nc.dram_tensor(p.name(tag + "x2r"), [T, T], BF16, kind="Internal").ap()
    X2i = nc.dram_tensor(p.name(tag + "x2i"), [T, T], BF16, kind="Internal").ap()
    YCr = nc.dram_tensor(p.name(tag + "ycr"), [T, T], BF16, kind="Internal").ap()
    YCi = nc.dram_tensor(p.name(tag + "yci"), [T, T], BF16, kind="Internal").ap()
    fre, fim, fimn = w["fft_fre"], w["fft_fim"], w["fft_fimn"]

    dft_pass(p, xz_d[0:T, :], None, X1r, X1i, fre, fim, fimn, identb,
             tag + "p1")
    dft_pass(p, X1r, X1i, X2r, X2i, fre, fim, fimn, identb, tag + "p2")

    with p.scope():
        KJ = 4096 // 128
        KI = 4096 // 128
        HW = T // 2
        xch = p.sbuf([128, KI * (HW + 4)], BF16, "xch")
        wslab = p.sbuf([128, 5 * KI * 128], BF16, "wcv")

        def xc_sl(kc, c0, cn):
            return xch[:, kc * (HW + 4) + c0: kc * (HW + 4) + c0 + cn]
        cps = [p.psum([128, 512], F32, "cvp"), p.psum([128, 512], F32, "cvp")]
        cevs = p.sbuf([128, 512], BF16, "cve")
        fcb = p.sbuf([128, KJ], F32, "fcb")
        for j in range(KJ):
            p.load(fcb[:, j:j + 1], w["fft_b"][j * 128:(j + 1) * 128, :], tag + "cbl")
        si = 0
        for half in range(2):
            base = half * HW
            p.barrier()
            for kc in range(KI):
                src_d = X2r if kc < 16 else X2i
                row0 = (kc % 16) * 128
                lo = max(0, base - 2)
                hi = min(T, base + HW + 2)
                off = lo - (base - 2)
                p.loadc(xc_sl(kc, off, hi - lo),
                        src_d[row0:row0 + 128, lo:hi], tag + "cxl")
                if base - 2 < 0:
                    p.inc(nc.vector.memset(xc_sl(kc, 0, 2), 0.0), tag + "cxz")
                if base + HW + 2 > T:
                    p.inc(nc.vector.memset(xc_sl(kc, HW + 2, 2), 0.0), tag + "cxz")
            p.barrier()
            for j in range(KJ):
                p.wait(nc.gpsimd, tag + "cmm", p.tail(tag + "cmm"))
                for t_ in range(5):
                    p.loadc(wslab[:, t_ * KI * 128:(t_ + 1) * KI * 128],
                            chunk_major(w["fftw_t%d" % t_], j * 128, 128,
                                        nrows=4096), tag + "cwl")
                p.wait(nc.tensor, tag + "cwl", p.tail(tag + "cwl"))
                for n in range(HW // 512):
                    p.wait(nc.tensor, tag + "cev", max(0, p.tail(tag + "cev") - 1))
                    first = True
                    imm = None
                    for t_ in range(5):
                        for kc in range(KI):
                            imm = nc.tensor.matmul(
                                cps[si][:],
                                wslab[:, (t_ * KI + kc) * 128:(t_ * KI + kc + 1) * 128],
                                xc_sl(kc, n * 512 + t_, 512),
                                start=first, stop=(t_ == 4 and kc == KI - 1))
                            first = False
                    cmt = p.inc(imm, tag + "cmm")
                    p.wait(nc.scalar, tag + "cmm", cmt)
                    p.wait(nc.scalar, tag + "cst", p.tail(tag + "cst"))
                    ie = nc.scalar.activation(out=cevs[:], in_=cps[si][:],
                                              func=AF.Silu, bias=fcb[:, j:j + 1])
                    cev = p.inc(ie, tag + "cev")
                    p.wait(nc.sync, tag + "cev", cev)
                    od = YCr if j < 16 else YCi
                    p.store(od[(j % 16) * 128:(j % 16 + 1) * 128,
                               base + n * 512: base + (n + 1) * 512],
                            cevs[:], tag + "cst")
                    si ^= 1
        p.wait(nc.sync, tag + "cst")
    p.barrier()

    dft_pass(p, YCr, YCi, X1r, X1i, fre, fimn, fim, identb, tag + "q1")
    dft_pass(p, X1r, X1i, X2r, X2i, fre, fimn, fim, identb, tag + "q2")

    with p.scope():
        rech = p.sbuf([128, T], BF16, "rech")
        imch = p.sbuf([128, T], BF16, "imch")
        sq1 = p.sbuf([128, T], F32, "sq1")
        sq2 = p.sbuf([128, T], F32, "sq2")
        vch = p.sbuf([128, T], F32, "vch")
        xcn = p.sbuf([128, T], F32, "xcn")
        zab = p.sbuf([128, 16], F32, "zab")
        s1 = p.sbuf([128, 1], F32, "s1")
        s2 = p.sbuf([128, 1], F32, "s2")
        mu = p.sbuf([128, 1], F32, "mu")
        lnv = p.sbuf([128, 1], F32, "lnv")
        rstd = p.sbuf([128, 1], F32, "rstd")
        p.inc(nc.scalar.activation(out=zab[:], in_=zmax[:], func=AF.Abs),
              tag + "zab")
        p.barrier()
        for kc in range(16):
            p.wait(nc.gpsimd, tag + "asq", p.tail(tag + "asq"))
            l1 = p.loadc(rech[:], X2r[kc * 128:(kc + 1) * 128, :], tag + "ar" + str(kc % 2))
            l2 = p.loadc(imch[:], X2i[kc * 128:(kc + 1) * 128, :], tag + "ai" + str(kc % 2))
            p.wait(nc.scalar, tag + "ar" + str(kc % 2), l1)
            p.wait(nc.scalar, tag + "ai" + str(kc % 2), l2)
            p.wait(nc.scalar, tag + "asq", p.tail(tag + "asq"))
            nc.scalar.activation(out=sq1[:], in_=rech[:], func=AF.Square)
            nc.scalar.drain()
            i2 = nc.scalar.activation(out=sq2[:], in_=imch[:], func=AF.Square)
            sqt = p.inc(i2, tag + "asq")
            p.wait(nc.vector, tag + "asq", sqt)
            nc.vector.drain()
            iv = nc.vector.tensor_add(vch[:], sq1[:], sq2[:])
            p.inc(iv, tag + "avv")
            p.wait(nc.scalar, tag + "avv", p.tail(tag + "avv"))
            isq = nc.scalar.activation(out=vch[:], in_=vch[:], func=AF.Sqrt)
            vt = p.inc(isq, tag + "avs")
            p.wait(nc.vector, tag + "avs", vt)
            nc.vector.drain()
            nc.vector.tensor_scalar(out=vch[:], in0=vch[:],
                                    scalar1=zab[:, kc:kc + 1],
                                    scalar2=1.0 / N2, op0=ALU.mult, op1=ALU.mult)
            nc.vector.drain()
            p.inc(nc.vector.reduce_sum(out=s1[:], in_=vch[:], axis=AX.X), tag + "as1")
            nc.vector.drain()
            nc.vector.tensor_scalar_mul(mu[:], s1[:], 1.0 / T)
            nc.vector.drain()
            ixc = nc.vector.tensor_single_scalar(out=xcn[:], in_=vch[:],
                                                 scalar=mu[:], op=ALU.subtract)
            xct = p.inc(ixc, tag + "axc")
            p.wait(nc.scalar, tag + "axc", xct)
            p.wait(nc.scalar, tag + "alw", p.tail(tag + "alw"))
            p.inc(nc.scalar.activation(out=sq1[:], in_=xcn[:], func=AF.Square,
                                       accum_out=s2[:]), tag + "asq2")
            p.wait(nc.scalar, tag + "asq2")
            nc.scalar.activation(out=lnv[:], in_=s2[:], func=AF.Ln,
                                 scale=1.0 / T, bias=p.constf(1e-5))
            nc.scalar.drain()
            ir = nc.scalar.activation(out=rstd[:], in_=lnv[:], func=AF.Exp,
                                      scale=-0.5)
            rt = p.inc(ir, tag + "arstd")
            p.wait(nc.vector, tag + "arstd", rt)
            nc.vector.drain()
            io = nc.vector.tensor_scalar_mul(vch[:], xcn[:], rstd[:])
            ot = p.inc(io, tag + "alw")
            p.wait(nc.gpsimd, tag + "alw", ot)
            p.inc(nc.gpsimd.dma_start(out=ocum_d[kc * 128:(kc + 1) * 128, :],
                                      in_=vch[:], accum_op=ALU.add), tag + "aoc", 16)
            p.wait(nc.vector, tag + "aoc", p.tail(tag + "aoc"))
    p._dma_sems.add(tag + "aoc")
    p.barrier()


def _host_consts():
    cosF, sinF, pi = rope_tables()
    ang = -2.0 * np.pi * np.outer(np.arange(T), np.arange(T)) / T
    FRE = np.cos(ang).astype(np.float32)
    FIM = np.sin(ang).astype(np.float32)
    return {
        "ident": np.eye(128, dtype=np.float32),
        "ident64s": np.ascontiguousarray(np.tile(np.eye(64, dtype=np.float32), (2, 1))),
        "rope_pi": pi, "rope_cos": cosF, "rope_sin": sinF,
        "fft_fre": FRE, "fft_fim": FIM, "fft_fimn": np.ascontiguousarray(-FIM),
    }


def _launch(build_fn, host, xs):
    p = Prog()
    nc = p.nc
    xd = nc.dram_tensor("x", [T, D], F32, kind="ExternalInput").ap()
    od = nc.dram_tensor("out", [T, D], F32, kind="ExternalOutput").ap()
    w = {k: nc.dram_tensor(k, list(np.asarray(a).shape), F32,
                           kind="ExternalInput").ap() for k, a in host.items()}
    build_fn(p, xd, od, w)
    p.es.close()
    hostc = dict(host)
    hostc["fconsts"] = p.const_array()
    in_maps = [dict(hostc, x=np.ascontiguousarray(xs[i])) for i in range(8)]
    res = run_bass_kernel_spmd(nc, in_maps, core_ids=list(range(8)))
    return [res.results[i]["out"] for i in range(8)]


FUSED = True


def kernel(**inputs):
    A = np.ascontiguousarray
    g = lambda k: np.asarray(inputs[k], dtype=np.float32)
    x = A(g("L0_attn_x"))
    C = _host_consts()

    h_l0f = {
        "ident": C["ident"], "ident64s": C["ident64s"],
        "We": A(g("L0_ffn_We").reshape(20 * 1024, 1024)),
        "be": A(g("L0_ffn_be").reshape(-1, 1)),
        "Wq": A(g("L0_ffn_Wq").reshape(6 * 1024, 1024)),
        "Wk": A(g("L0_ffn_Wk").reshape(6 * 1024, 1024)),
        "Wv": A(g("L0_ffn_Wv").reshape(6 * 1024, 1024)),
        "Wo": A(g("L0_ffn_Wo").reshape(6 * 1024, 1024)),
        "ew_bc": A(np.broadcast_to(g("L0_ffn_ew").reshape(1, 18), (128, 18))),
        "actw_bc": A(np.broadcast_to(g("L0_ffn_act_w").reshape(1, 6), (128, 6))),
        "lng_bc": A(np.broadcast_to(g("L0_ffn_ln_g"), (128, D))),
        "lnb_bc": A(np.broadcast_to(g("L0_ffn_ln_b"), (128, D))),
    }
    h_l0a = {
        "ident": C["ident"], "ident64s": C["ident64s"],
        "rope_pi": C["rope_pi"], "rope_cos": C["rope_cos"],
        "rope_sin": C["rope_sin"],
        "ln_gamma_bc": A(np.broadcast_to(g("L0_attn_ln_gamma"), (128, D))),
        "ln_beta_bc": A(np.broadcast_to(g("L0_attn_ln_beta"), (128, D))),
        "w_qkv": g("L0_attn_w_qkv"), "w_out": g("L0_attn_w_out"),
        "b_out": A(g("L0_attn_b_out").reshape(D, 1)),
    }
    h_l1f = {
        "ident": C["ident"], "gate_w": g("L1_ffn_gate_w"),
        "w1": A(g("L1_ffn_w1").reshape(8 * 1024, 512)),
        "w3": A(g("L1_ffn_w3").reshape(8 * 1024, 512)),
        "w2": A(g("L1_ffn_w2").reshape(8 * 512, 1024)),
    }
    h_l1a = {
        "ident": C["ident"],
        "in_projT": A(g("L1_attn_in_proj_w").T),
        "out_projT": A(g("L1_attn_out_proj_w").T),
        "f_conv_w": A(g("L1_attn_conv1d_w")[:, 0, :]),
        "f_conv_b": A(g("L1_attn_conv1d_bias").reshape(-1, 1)),
        "f_x_projT": A(g("L1_attn_x_proj_w").T),
        "f_dt_projT": A(g("L1_attn_dt_proj_w").T),
        "f_dt_b": A(g("L1_attn_dt_proj_b").reshape(-1, 1)),
        "f_A_log": g("L1_attn_A_log"),
        "f_Dp": A(g("L1_attn_D").reshape(-1, 1)),
        "b_conv_w": A(g("L1_attn_conv1db_w")[:, 0, :]),
        "b_conv_b": A(g("L1_attn_conv1db_bias").reshape(-1, 1)),
        "b_x_projT": A(g("L1_attn_x_projb_w").T),
        "b_dt_projT": A(g("L1_attn_dt_projb_w").T),
        "b_dt_b": A(g("L1_attn_dt_projb_b").reshape(-1, 1)),
        "b_A_log": A(g("L1_attn_A_b_log")),
        "b_Dp": A(g("L1_attn_D_b").reshape(-1, 1)),
        "fft_fre": C["fft_fre"], "fft_fim": C["fft_fim"],
        "fft_fimn": C["fft_fimn"],
        "fft_b": A(g("L1_attn_fft_conv_b").reshape(-1, 1)),
    }
    fcw = g("L1_attn_fft_conv_w")
    for t_ in range(5):
        h_l1a["fftw_t%d" % t_] = A(fcw[:, :, t_].T)

    xs = [x[i] for i in range(8)]
    if not FUSED:
        xs = _launch(lambda p, xd, od, w: sub_l0ffn(p, xd, od, w, tag="lf"),
                     h_l0f, xs)
        xs = _launch(sub_l0attn, h_l0a, xs)
        xs = _launch(lambda p, xd, od, w: sub_l0ffn(p, xd, od, w, tag="lf"),
                     h_l0f, xs)
        xs = _launch(lambda p, xd, od, w: sub_l1ffn(p, xd, od, w), h_l1f, xs)
        xs = _launch(lambda p, xd, od, w: sub_l1attn(p, xd, od, w), h_l1a, xs)
        xs = _launch(lambda p, xd, od, w: sub_l1ffn(p, xd, od, w), h_l1f, xs)
        return np.stack([np.asarray(o, dtype=np.float32) for o in xs])

    host = {}
    wmaps = {}
    for pre, h in (("A", h_l0f), ("B", h_l0a), ("C", h_l1f), ("D", h_l1a)):
        wmaps[pre] = {}
        for k, a in h.items():
            a = np.asarray(a)
            key = pre + "_" + k
            if k in ("ident", "ident64s", "rope_pi", "rope_cos", "rope_sin",
                     "fft_fre", "fft_fim", "fft_fimn"):
                key = "K_" + k
            if key not in host:
                host[key] = a
            wmaps[pre][k] = key

    p = Prog()
    nc = p.nc
    xd = nc.dram_tensor("x", [T, D], F32, kind="ExternalInput").ap()
    od = nc.dram_tensor("out", [T, D], F32, kind="ExternalOutput").ap()
    dr = {k: nc.dram_tensor(k, list(np.asarray(a).shape), F32,
                            kind="ExternalInput").ap() for k, a in host.items()}
    s0 = nc.dram_tensor("scr0", [T, D], F32, kind="Internal").ap()
    s1 = nc.dram_tensor("scr1", [T, D], F32, kind="Internal").ap()

    def wview(pre):
        return {k: dr[key] for k, key in wmaps[pre].items()}
    sub_l0ffn(p, xd, s0, wview("A"), tag="lf")
    sub_l0attn(p, s0, s1, wview("B"))
    sub_l0ffn(p, s1, s0, wview("A"), tag="lf")
    sub_l1ffn(p, s0, s1, wview("C"))
    sub_l1attn(p, s1, s0, wview("D"))
    sub_l1ffn(p, s0, od, wview("C"))
    p.es.close()
    hostc = dict(host)
    hostc["fconsts"] = p.const_array()
    in_maps = [dict(hostc, x=np.ascontiguousarray(x[i])) for i in range(8)]
    res = run_bass_kernel_spmd(nc, in_maps, core_ids=list(range(8)))
    return np.stack([np.asarray(res.results[i]["out"], dtype=np.float32)
                     for i in range(8)])


# version 04
# speedup vs baseline: 1.1785x; 1.1785x over previous
import numpy as np
from contextlib import ExitStack, contextmanager

import concourse.bass as bass
import concourse.mybir as mybir
from concourse.bass_utils import run_bass_kernel_spmd

F32 = mybir.dt.float32
BF16 = mybir.dt.bfloat16
AF = mybir.ActivationFunctionType
ALU = mybir.AluOpType
AX = mybir.AxisListType

T = 2048
D = 1024
DC = D // 128
NT = T // 128
NQ = T // 512


NCONST = 64


class Prog:
    def __init__(self, detect_races=True):
        self.nc = bass.Bass("TRN2", target_bir_lowering=False,
                            detect_race_conditions=detect_races)
        self.es = ExitStack()
        self._stack = [self.es]
        self._sems = {}
        self._dma_sems = set()
        self._uid = 0
        self._cvals = []
        self._cdram = self.nc.dram_tensor("fconsts", [128, NCONST], F32,
                                          kind="ExternalInput").ap()
        self._csb = self.sbuf([128, NCONST], F32, "fconsts")
        self.load(self._csb[:], self._cdram[:], "cst")
        for e in (self.nc.tensor, self.nc.vector, self.nc.scalar,
                  self.nc.gpsimd, self.nc.sync):
            self.wait(e, "cst", 16)

    def constf(self, val):
        val = float(val)
        if val not in self._cvals:
            assert len(self._cvals) < NCONST
            self._cvals.append(val)
            idx = self._cvals.index(val)
            self.nc.const_aps.register(F32, val, self._csb[:, idx:idx + 1])
        return val

    def const_array(self):
        arr = np.zeros((128, NCONST), np.float32)
        arr[:, :len(self._cvals)] = np.asarray(self._cvals, np.float32)[None, :]
        return arr

    def name(self, prefix):
        self._uid += 1
        return f"{prefix}{self._uid}"

    def sbuf(self, shape, dt, name="t"):
        return self._stack[-1].enter_context(
            self.nc.sbuf_tensor(self.name(name), list(shape), dt))

    def psum(self, shape, dt=F32, name="p"):
        return self._stack[-1].enter_context(
            self.nc.psum_tensor(self.name(name), list(shape), dt))

    @contextmanager
    def scope(self):
        es = ExitStack()
        self._stack.append(es)
        try:
            yield
        finally:
            self.barrier()
            es.close()
            self._stack.pop()

    def _sem(self, sname):
        if sname not in self._sems:
            h = self.es.enter_context(self.nc.semaphore(self.name("s" + sname)))
            self._sems[sname] = [h, 0]
        return self._sems[sname]

    def inc(self, instr, sname, n=1):
        s = self._sem(sname)
        s[1] += n
        instr.then_inc(s[0], n)
        return s[1]

    def tail(self, sname):
        return self._sem(sname)[1]

    def wait(self, eng, sname, val=None):
        s = self._sem(sname)
        v = s[1] if val is None else val
        if v > 0:
            eng.wait_ge(s[0], v)

    def barrier(self):
        nc = self.nc
        engs = [nc.tensor, nc.vector, nc.scalar, nc.sync, nc.gpsimd]
        s = self._sem("bar")
        base = s[1]
        dmas = [(self._sems[n][0], self._sems[n][1])
                for n in self._dma_sems
                if n in self._sems and self._sems[n][1] > 0]
        for e in engs:
            for h, v in dmas:
                e.wait_ge(h, v)
            e.drain()
            self.inc(e.nop(), "bar", 1)
        for e in engs:
            e.wait_ge(s[0], base + len(engs))

    def load(self, out_ap, in_ap, sname="ld"):
        self._dma_sems.add(sname)
        return self.inc(self.nc.sync.dma_start(out=out_ap, in_=in_ap), sname, 16)

    def loadc(self, out_ap, in_ap, sname="ldg"):
        self._dma_sems.add(sname)
        return self.inc(self.nc.gpsimd.dma_start(out=out_ap, in_=in_ap), sname, 16)

    def store(self, out_ap, in_ap, sname="st"):
        self._dma_sems.add(sname)
        return self.inc(self.nc.sync.dma_start(out=out_ap, in_=in_ap), sname, 16)


def load_w(p, w_dram, K, M, sname="ldg"):
    wsb = p.sbuf([128, (K // 128) * M], BF16, "w")
    tok = None
    for kc in range(K // 128):
        tok = p.loadc(wsb[:, kc * M:(kc + 1) * M],
                      w_dram[kc * 128:(kc + 1) * 128, :], sname)
    return wsb, tok


def w_tile(wsb, M, kc, j):
    return wsb[:, kc * M + j * 128: kc * M + (j + 1) * 128]


def ln_to_xT(p, x_dram, xT, identb, eps=1e-5, gain=None, bias_=None,
             nchunk=DC, tag="ln"):
    nc = p.nc
    tag = "ln"
    s_ld, s_mu, s_xc, s_t, s_rstd, s_xn, s_mm, s_ev = (
        tag + k for k in ("ld", "mu", "xc", "t", "rstd", "xn", "mm", "ev"))
    B = {s: p.tail(s) for s in (s_xc, s_t, s_rstd, s_xn, s_mm, s_ev,
                                s_xn + "m", s_rstd + "w")}
    p._lnscope = p.scope()
    p._lnscope.__enter__()
    xt = [p.sbuf([128, D], F32, "lnx"), p.sbuf([128, D], F32, "lnx")]
    xc = p.sbuf([128, D], F32, "lnxc")
    tmid = p.sbuf([128, D], F32, "lnmid")
    xnb = p.sbuf([128, D], BF16, "lnxn")
    sq = p.sbuf([128, D], F32, "lnsq")
    s1 = p.sbuf([128, 1], F32, "s1")
    s2 = p.sbuf([128, 1], F32, "s2")
    mu = p.sbuf([128, 1], F32, "mu")
    lnv = p.sbuf([128, 1], F32, "lnv")
    rstd = p.sbuf([128, 1], F32, "rstd")
    tp = p.psum([128, D], BF16, "tp")
    npass = 2 if gain is not None else 1

    def one_pass(src_buf, dst_buf, k):
        nc.vector.drain()
        p.inc(nc.vector.reduce_sum(out=s1[:], in_=src_buf[:], axis=AX.X), s_mu)
        nc.vector.drain()
        nc.vector.tensor_scalar_mul(mu[:], s1[:], 1.0 / D)
        nc.vector.drain()
        p.inc(nc.vector.tensor_single_scalar(out=xc[:], in_=src_buf[:],
                                             scalar=mu[:], op=ALU.subtract), s_xc)
        p.wait(nc.scalar, s_xc, B[s_xc] + k + 1)
        p.wait(nc.scalar, s_rstd + "w", B[s_rstd + "w"] + k)
        p.inc(nc.scalar.activation(out=sq[:], in_=xc[:], func=AF.Square,
                                   accum_out=s2[:]), s_t)
        p.wait(nc.scalar, s_t)
        nc.scalar.activation(out=lnv[:], in_=s2[:], func=AF.Ln,
                             scale=1.0 / D, bias=p.constf(eps))
        nc.scalar.drain()
        p.inc(nc.scalar.activation(out=rstd[:], in_=lnv[:], func=AF.Exp,
                                   scale=-0.5), s_rstd)
        p.wait(nc.vector, s_rstd, B[s_rstd] + k + 1)
        nc.vector.drain()
        i = nc.vector.tensor_scalar_mul(dst_buf[:], xc[:], rstd[:])
        p.inc(i, s_rstd + "w")
        return i

    for tt in range(NT):
        buf = xt[tt % 2]
        par = s_ld + str(tt % 2)
        p.wait(nc.sync, s_xn, B[s_xn] + tt - 1)
        lt = p.load(buf[:], x_dram[tt * 128:(tt + 1) * 128, :], par)
        p.wait(nc.vector, par, lt)
        p.wait(nc.vector, s_mm, B[s_mm] + tt * nchunk)
        if gain is None:
            i5 = one_pass(buf, tmid, tt)
            nc.vector.drain()
            i5 = nc.vector.tensor_copy(out=xnb[:], in_=tmid[:])
        else:
            one_pass(buf, tmid, 2 * tt)
            one_pass(tmid, xc, 2 * tt + 1)
            nc.vector.drain()
            nc.vector.tensor_tensor(out=xnb[:], in0=xc[:], in1=gain[:],
                                    op=ALU.mult)
            nc.vector.drain()
            i5 = nc.vector.tensor_tensor(out=xnb[:], in0=xnb[:], in1=bias_[:],
                                         op=ALU.add)
        p.inc(i5, s_xn)
        p.wait(nc.tensor, s_xn, B[s_xn] + tt + 1)
        p.wait(nc.tensor, s_ev, B[s_ev] + tt)
        it = None
        for j in range(nchunk):
            it = nc.tensor.transpose(tp[:, j * 128:(j + 1) * 128],
                                     xnb[:, j * 128:(j + 1) * 128], identb[:])
        p.inc(it, s_mm, nchunk)
        p.wait(nc.vector, s_mm, B[s_mm] + (tt + 1) * nchunk)
        dst = xT[:, :].rearrange("p (c t) -> p c t", c=nchunk)[
            :, :, tt * 128:(tt + 1) * 128]
        srcv = tp[:, :].rearrange("p (c t) -> p c t", c=nchunk)
        p.inc(nc.vector.tensor_copy(out=dst, in_=srcv), s_ev)
    p._lnscope.__exit__(None, None, None)
    return p.tail(s_ev)


def mm_ff(p, wsb, K, M, rhsT, evac, rhs_T=T, tag="mm"):
    nc = p.nc
    KC = K // 128
    nq = rhs_T // 512
    s_mm, s_ev = tag + "m", tag + "e"
    NSL = 4
    ps = [p.psum([128, 512], F32, "mmps") for _ in range(NSL)]
    slot_tok = [0] * NSL
    si = 0
    for j in range(M // 128):
        for n in range(nq):
            if slot_tok[si]:
                p.wait(nc.tensor, s_ev, slot_tok[si])
            im = None
            for kc in range(KC):
                im = nc.tensor.matmul(
                    ps[si][:], w_tile(wsb, M, kc, j),
                    rhsT[:, kc * rhs_T + n * 512:kc * rhs_T + (n + 1) * 512],
                    start=(kc == 0), stop=(kc == KC - 1))
            mtok = p.inc(im, s_mm)
            ie = evac(j, n, ps[si], mtok)
            slot_tok[si] = p.inc(ie, s_ev)
            si = (si + 1) % NSL
    return p.tail(s_ev)


def attention_c(p, qT, kT, vT, ctxT_out, H, dh, scale, identb, identq=None,
                shift=12.0, tag=None):
    tag = "at"
    """identq: [128, dh] identity stack (eye(dh) tiled) for dh<128 transposes."""
    """Scheme-C attention. qT/kT/vT/ctxT_out: sbuf bf16 [128, (H*dh/128)*T]."""
    nc = p.nc
    KCH = T // 128
    hpp = 128 // dh
    s_vt, s_ve, s_sm, s_pe, s_cm, s_su, s_rs, s_bm, s_be, s_ce = (
        tag + k for k in ("vt", "ve", "sm", "pe", "cm", "su", "rs", "bm", "be",
                          "ce"))
    with p.scope():
        PTb = [p.sbuf([128, KCH * 512], BF16, "PT") for _ in range(hpp)]
        Vp = [p.sbuf([128, KCH * dh], BF16, "Vp") for _ in range(hpp)]
        vtp = p.psum([128, 512], BF16, "vtp")
        sps = [p.psum([128, 512], F32, "sps"), p.psum([128, 512], F32, "sps")]
        cps = p.psum([128, 512], F32, "cps")
        sm = [p.psum([1, 512], F32, "sm") for _ in range(hpp)]
        rs = [p.sbuf([1, 512], F32, "rs") for _ in range(hpp)]
        rsb = [p.sbuf([1, 512], BF16, "rsb") for _ in range(hpp)]
        bcp = p.psum([128, 512], F32, "bcp")
        bcs = p.sbuf([128, 512], F32, "bcs")
        ones1 = p.sbuf([1, 128], BF16, "ones1")
        onesk = p.sbuf([128, 1], BF16, "onesk")
        nc.vector.memset(ones1[:], 1.0)
        p.inc(nc.vector.memset(onesk[:], 1.0), tag + "init")
        p.wait(nc.tensor, tag + "init")

        vslots = 512 // dh
        for hp in range(H // hpp):
            heads = [hp * hpp + i for i in range(hpp)]
            for i, h in enumerate(heads):
                c0, r0 = (h * dh) // 128, (h * dh) % 128
                for g in range(KCH // vslots):
                    p.wait(nc.tensor, s_ve, p.tail(s_ve))
                    it = None
                    for s in range(vslots):
                        kc = g * vslots + s
                        idq = identb if dh == 128 else identq
                        it = nc.tensor.transpose(
                            vtp[:, s * dh:(s + 1) * dh],
                            vT[r0:r0 + dh,
                               c0 * T + kc * 128: c0 * T + (kc + 1) * 128],
                            idq[r0:r0 + dh, 0:dh])
                    vt = p.inc(it, s_vt)
                    p.wait(nc.vector, s_vt, vt)
                    p.inc(nc.vector.tensor_copy(
                        out=Vp[i][:, g * 512:(g + 1) * 512], in_=vtp[:]), s_ve)
            for n in range(NQ):
                for i, h in enumerate(heads):
                    c0, r0 = (h * dh) // 128, (h * dh) % 128
                    p.wait(nc.scalar, s_su, p.tail(s_su))
                    si = 0
                    for kc in range(KCH):
                        if p.tail(s_pe) >= 2:
                            p.wait(nc.tensor, s_pe, p.tail(s_pe) - 1)
                        im = nc.tensor.matmul(
                            sps[si][:],
                            kT[r0:r0 + dh,
                               c0 * T + kc * 128: c0 * T + (kc + 1) * 128],
                            qT[r0:r0 + dh,
                               c0 * T + n * 512: c0 * T + (n + 1) * 512],
                            start=True, stop=True)
                        tm = p.inc(im, s_sm)
                        p.wait(nc.scalar, s_sm, tm)
                        p.inc(nc.scalar.activation(
                            out=PTb[i][:, kc * 512:(kc + 1) * 512],
                            in_=sps[si][:], func=AF.Exp, scale=float(scale),
                            bias=p.constf(-shift)), s_pe)
                        si ^= 1
                    p.wait(nc.tensor, s_pe, p.tail(s_pe))
                    p.wait(nc.tensor, s_ce, p.tail(s_ce))
                    im = None
                    for kc in range(KCH):
                        im = nc.tensor.matmul(
                            cps[i * dh:(i + 1) * dh, :],
                            Vp[i][:, kc * dh:(kc + 1) * dh],
                            PTb[i][:, kc * 512:(kc + 1) * 512],
                            start=(kc == 0), stop=(kc == KCH - 1))
                    p.inc(im, s_cm)
                    p.wait(nc.tensor, s_rs, p.tail(s_rs))
                    im = None
                    for kc in range(KCH):
                        im = nc.tensor.matmul(
                            sm[i][:], onesk[:, 0:1],
                            PTb[i][:, kc * 512:(kc + 1) * 512],
                            start=(kc == 0), stop=(kc == KCH - 1))
                    sut = p.inc(im, s_su)
                    p.wait(nc.vector, s_su, sut)
                    p.wait(nc.vector, s_bm, p.tail(s_bm))
                    rt = p.inc(nc.vector.reciprocal(
                        out=rs[i][:], in_=sm[i][:]), s_rs)
                    p.wait(nc.scalar, s_rs, rt)
                    p.wait(nc.scalar, s_bm, p.tail(s_bm))
                    p.inc(nc.scalar.activation(out=rsb[i][:], in_=rs[i][:],
                                               func=AF.Copy), s_rs + "c")
                p.wait(nc.tensor, s_rs + "c", p.tail(s_rs + "c"))
                p.wait(nc.tensor, s_be, p.tail(s_be))
                ib = None
                for i in range(hpp):
                    ib = nc.tensor.matmul(bcp[i * dh:(i + 1) * dh, :],
                                          ones1[0:1, 0:dh], rsb[i][:],
                                          start=True, stop=True)
                bt = p.inc(ib, s_bm)
                p.wait(nc.vector, s_bm, bt)
                p.inc(nc.vector.tensor_copy(out=bcs[:], in_=bcp[:]), s_be)
                nc.vector.drain()
                p.wait(nc.vector, s_cm, p.tail(s_cm))
                c0 = (heads[0] * dh) // 128
                p.inc(nc.vector.tensor_tensor(
                    out=ctxT_out[:, c0 * T + n * 512: c0 * T + (n + 1) * 512],
                    in0=cps[:], in1=bcs[:], op=ALU.mult), s_ce)


def residual_store(p, srcT, x_dram, out_dram, identf, scale_src=1.0, tag="res"):
    nc = p.nc
    s_ld, s_mm, s_ev, s_st = "lnld0", "lnmm", "lnev", "lnst"
    B = {s: p.tail(s) for s in (s_mm, s_ev, s_st)}
    rp = p.psum([128, 1024], F32, "rp")
    xtile = p.sbuf([128, D], F32, "rx")
    otile = p.sbuf([128, D], F32, "ro")
    for tt in range(NT):
        p.wait(nc.sync, s_ev, B[s_ev] + tt)
        lt = p.load(xtile[:], x_dram[tt * 128:(tt + 1) * 128, :], s_ld)
        p.wait(nc.tensor, s_ev, B[s_ev] + tt)
        it = None
        for j in range(DC):
            it = nc.tensor.transpose(
                rp[:, j * 128:(j + 1) * 128],
                srcT[:, j * T + tt * 128: j * T + (tt + 1) * 128], identf[:])
        mt = p.inc(it, s_mm)
        p.wait(nc.vector, s_mm, mt)
        p.wait(nc.vector, s_ld, lt)
        p.wait(nc.vector, s_st, B[s_st] + tt * 16)
        ia = nc.vector.scalar_tensor_tensor(out=otile[:], in0=rp[:],
                                            scalar=float(scale_src), in1=xtile[:],
                                            op0=ALU.mult, op1=ALU.add)
        ev = p.inc(ia, s_ev)
        p.wait(nc.sync, s_ev, ev)
        p.store(out_dram[tt * 128:(tt + 1) * 128, :], otile[:], s_st)
    p.wait(nc.sync, s_st)


def sub_l0attn(p, x_dram, out_dram, w):
    nc = p.nc
    H, dh = 16, 64
    with p.scope():
        identb = p.sbuf([128, 128], BF16, "idb")
        identf = p.sbuf([128, 128], F32, "idf")
        identq = p.sbuf([128, 64], BF16, "idq")
        p.loadc(identb[:], w["ident"][:])
        p.load(identf[:], w["ident"][:])
        p.loadc(identq[:], w["ident64s"][:])
        p.barrier()

        ctxT = p.sbuf([128, 8 * T], BF16, "ctxT")
        with p.scope():
            qkvT = p.sbuf([128, 24 * T], BF16, "qkvT")
            with p.scope():
                xT = p.sbuf([128, DC * T], BF16, "xT")
                gainb = p.sbuf([128, D], BF16, "gain")
                biasb = p.sbuf([128, D], BF16, "bias")
                p.loadc(gainb[:], w["ln_gamma_bc"][:])
                p.loadc(biasb[:], w["ln_beta_bc"][:])
                p.barrier()
                ln_to_xT(p, x_dram, xT, identb, eps=1e-5, gain=gainb, bias_=biasb)
                p.barrier()

                def evac_qkv(half):
                    def f(j, n, ps, mtok):
                        jj = j + 12 * half
                        p.wait(nc.scalar, "mmm", mtok)
                        return nc.scalar.activation(
                            out=qkvT[:, jj * T + n * 512:jj * T + (n + 1) * 512],
                            in_=ps[:], func=AF.Copy)
                    return f
                for half in range(2):
                    with p.scope():
                        wq, _ = load_w(p, w["w_qkv"][:, half * 1536:(half + 1) * 1536],
                                       1024, 1536)
                        p.barrier()
                        mm_ff(p, wq, 1024, 1536, xT, evac_qkv(half))

            qT = qkvT[:, 0 * T: 8 * T]
            kT = qkvT[:, 8 * T: 16 * T]
            vT = qkvT[:, 16 * T: 24 * T]

            with p.scope():
                pib = p.sbuf([128, 128], BF16, "pib")
                cosF = p.sbuf([128, T], BF16, "cosF")
                sinF = p.sbuf([128, T], BF16, "sinF")
                p.loadc(pib[:], w["rope_pi"][:])
                p.loadc(cosF[:], w["rope_cos"][:])
                p.loadc(sinF[:], w["rope_sin"][:])
                p.barrier()
                swp = p.psum([128, 512], F32, "swp")
                sws = p.sbuf([128, 512], BF16, "sws")
                tmp = p.sbuf([128, 512], BF16, "ropet")
                for base in (0, 8 * T):
                    for c in range(8):
                        for n in range(NQ):
                            sl = slice(base + c * T + n * 512,
                                       base + c * T + (n + 1) * 512)
                            p.wait(nc.tensor, "rpsw", p.tail("rpsw"))
                            im = nc.tensor.matmul(swp[:], pib[:], qkvT[:, sl],
                                                  start=True, stop=True)
                            mt = p.inc(im, "rpmm")
                            p.wait(nc.scalar, "rpmm", mt)
                            p.wait(nc.scalar, "rpev", p.tail("rpev"))
                            isw = nc.scalar.activation(out=sws[:], in_=swp[:],
                                                       func=AF.Copy)
                            swt = p.inc(isw, "rpsw")
                            p.wait(nc.vector, "rpsw", swt)
                            nc.vector.tensor_tensor(
                                out=tmp[:], in0=sws[:],
                                in1=sinF[:, n * 512:(n + 1) * 512], op=ALU.mult)
                            nc.vector.tensor_tensor(
                                out=qkvT[:, sl], in0=qkvT[:, sl],
                                in1=cosF[:, n * 512:(n + 1) * 512], op=ALU.mult)
                            nc.vector.drain()
                            p.inc(nc.vector.tensor_tensor(
                                out=qkvT[:, sl], in0=qkvT[:, sl], in1=tmp[:],
                                op=ALU.add), "rpev")

            attention_c(p, qT, kT, vT, ctxT, H, dh, dh ** -0.5, identb, identq)

        with p.scope():
            oT = p.sbuf([128, DC * T], F32, "oT")
            wo, _ = load_w(p, w["w_out"], 1024, 1024)
            bout = p.sbuf([128, DC], F32, "bout")
            for j in range(DC):
                p.load(bout[:, j:j + 1], w["b_out"][j * 128:(j + 1) * 128, :], "ldb")
            p.barrier()

            def evac_o(j, n, ps, mtok):
                p.wait(nc.vector, "mmm", mtok)
                p.wait(nc.vector, "ldb", p.tail("ldb"))
                return nc.vector.tensor_scalar_add(
                    oT[:, j * T + n * 512:j * T + (n + 1) * 512],
                    ps[:], bout[:, j:j + 1])
            mm_ff(p, wo, 1024, 1024, ctxT, evac_o)
            p.barrier()
            residual_store(p, oT, x_dram, out_dram, identf, scale_src=1.0)


def rope_tables(n=T, head_dim=64):
    inv_freq = 1.0 / 10000.0 ** (np.arange(0, head_dim, 2, dtype=np.float64) / head_dim)
    pos = np.arange(n, dtype=np.float64)
    ang = pos[:, None] * inv_freq[None, :]
    sin, cos = np.sin(ang), np.cos(ang)
    cosF = np.zeros((128, n), np.float32)
    sinF = np.zeros((128, n), np.float32)
    for r in range(128):
        i = r % 64
        f = i % 32
        cosF[r] = cos[:, f]
        sinF[r] = sin[:, f] if i >= 32 else -sin[:, f]
    pi = np.zeros((128, 128), np.float32)
    for r in range(128):
        i = r % 64
        partner = r + 32 if i < 32 else r - 32
        pi[partner, r] = 1.0
    return cosF, sinF, pi


def l0attn_consts():
    cosF, sinF, pi = rope_tables()
    return {
        "ident": np.eye(128, dtype=np.float32),
        "ident64s": np.ascontiguousarray(np.tile(np.eye(64, dtype=np.float32), (2, 1))),
        "rope_pi": pi, "rope_cos": cosF, "rope_sin": sinF,
    }


L0F_EDGES = {0: (0, 0, 0), 1: (2, 1, 2), 2: (5, 3, 4),
             3: (9, 6, 8), 4: (14, 10, 12), 5: (19, 15, 17)}
L0F_INS = {0: (-1, -1, -1), 1: (0, -1, 0), 2: (1, -1, 0),
           3: (2, -1, 1), 4: (3, -1, 1), 5: (4, 0, 2)}


def ln_residual_store(p, srcT, x_dram, out_dram, identf, gain, bias_,
                      eps, scale_src, tag="lrs"):
    nc = p.nc
    s_ld, s_mm, s_xc, s_rstd, s_ev, s_st = (
        "lnld0", "lnmm", "lnxc", "lnrstd", "lnev", "lnst")
    B = {s: p.tail(s) for s in (s_mm, s_xc, s_rstd, s_ev, s_st,
                                s_rstd + "w")}
    rp = p.psum([128, 1024], F32, "rp")
    xtile = p.sbuf([128, D], F32, "rx")
    xc = p.sbuf([128, D], F32, "rxc")
    sq = p.sbuf([128, D], F32, "rsq")
    otile = p.sbuf([128, D], F32, "ro")
    s1 = p.sbuf([128, 1], F32, "s1")
    s2 = p.sbuf([128, 1], F32, "s2")
    mu = p.sbuf([128, 1], F32, "mu")
    lnv = p.sbuf([128, 1], F32, "lnv")
    rstd = p.sbuf([128, 1], F32, "rstd")
    for tt in range(NT):
        p.wait(nc.sync, s_ev, B[s_ev] + tt)
        lt = p.load(xtile[:], x_dram[tt * 128:(tt + 1) * 128, :], s_ld)
        p.wait(nc.tensor, s_xc, B[s_xc] + tt)
        it = None
        for j in range(DC):
            it = nc.tensor.transpose(
                rp[:, j * 128:(j + 1) * 128],
                srcT[:, j * T + tt * 128: j * T + (tt + 1) * 128], identf[:])
        mt = p.inc(it, s_mm)
        p.wait(nc.vector, s_mm, mt)
        nc.vector.reduce_sum(out=s1[:], in_=rp[:], axis=AX.X)
        nc.vector.drain()
        nc.vector.tensor_scalar_mul(mu[:], s1[:], 1.0 / D)
        nc.vector.drain()
        p.inc(nc.vector.tensor_single_scalar(out=xc[:], in_=rp[:],
                                             scalar=mu[:], op=ALU.subtract), s_xc)
        p.wait(nc.scalar, s_xc, B[s_xc] + tt + 1)
        p.wait(nc.scalar, s_rstd + "w", B[s_rstd + "w"] + tt)
        p.inc(nc.scalar.activation(out=sq[:], in_=xc[:], func=AF.Square,
                                   accum_out=s2[:]), "lnt")
        p.wait(nc.scalar, "lnt")
        nc.scalar.activation(out=lnv[:], in_=s2[:], func=AF.Ln,
                             scale=1.0 / D, bias=p.constf(eps))
        nc.scalar.drain()
        p.inc(nc.scalar.activation(out=rstd[:], in_=lnv[:], func=AF.Exp,
                                   scale=-0.5), s_rstd)
        p.wait(nc.vector, s_rstd, B[s_rstd] + tt + 1)
        p.wait(nc.vector, s_ld, lt)
        p.wait(nc.vector, s_st, B[s_st] + tt * 16)
        nc.vector.drain()
        nc.vector.tensor_scalar_mul(otile[:], xc[:], rstd[:])
        nc.vector.drain()
        nc.vector.tensor_tensor(out=otile[:], in0=otile[:], in1=gain[:], op=ALU.mult)
        nc.vector.drain()
        p.inc(nc.vector.tensor_tensor(out=otile[:], in0=otile[:], in1=bias_[:],
                                      op=ALU.add), s_rstd + "w")
        nc.vector.drain()
        iv = nc.vector.scalar_tensor_tensor(out=otile[:], in0=otile[:],
                                            scalar=float(scale_src), in1=xtile[:],
                                            op0=ALU.mult, op1=ALU.add)
        ev = p.inc(iv, s_ev)
        p.wait(nc.sync, s_ev, ev)
        p.store(out_dram[tt * 128:(tt + 1) * 128, :], otile[:], s_st)
    p.wait(nc.sync, s_st)


def sub_l0ffn(p, x_dram, out_dram, w, tag):
    nc = p.nc
    H, dh = 8, 128

    def Wsl(name, e):
        return w[name][e * 1024:(e + 1) * 1024, :]

    with p.scope():
        identb = p.sbuf([128, 128], BF16, "idb")
        identf = p.sbuf([128, 128], F32, "idf")
        ewb = p.sbuf([128, 18], F32, "ewb")
        awb = p.sbuf([128, 6], F32, "awb")
        p.loadc(identb[:], w["ident"][:])
        p.load(identf[:], w["ident"][:])
        p.load(ewb[:], w["ew_bc"][:])
        p.load(awb[:], w["actw_bc"][:])
        nodes_d = {c: nc.dram_tensor(p.name(f"{tag}nd{c}"), [128, DC * T], BF16,
                                     kind="Internal").ap()
                   for c in range(-1, 6)}
        p.barrier()

        with p.scope():
            xT = p.sbuf([128, DC * T], BF16, "xT")
            ln_to_xT(p, x_dram, xT, identb, eps=1e-5, gain=None, tag=tag + "ln")
            p.barrier()
            p.store(nodes_d[-1][:], xT[:], tag + "nst")
            p.wait(nc.sync, tag + "nst")
        p.barrier()

        TMP = p.sbuf([128, DC * T], BF16, "sTMP")
        Q = p.sbuf([128, DC * T], BF16, "sQ")
        K = p.sbuf([128, DC * T], BF16, "sK")
        MV = TMP
        CTX = K

        def edge_mm(e, src_c, dest, ew_idx=None):
            with p.scope():
                ldt = p.load(TMP[:], nodes_d[src_c][:], "ldg")
                wsb, _ = load_w(p, Wsl("We", e), 1024, 1024)
                beb = p.sbuf([128, DC], F32, "beb")
                for j in range(DC):
                    p.load(beb[:, j:j + 1],
                           w["be"][e * 1024 + j * 128: e * 1024 + (j + 1) * 128, :],
                           "ld")
                p.barrier()

                def evac(j, n, ps, mtok):
                    p.wait(nc.vector, "mmm", mtok)
                    sl = slice(j * T + n * 512, j * T + (n + 1) * 512)
                    if ew_idx is None:
                        return nc.vector.tensor_scalar_add(dest[:, sl], ps[:],
                                                           beb[:, j:j + 1])
                    return nc.vector.tensor_scalar(
                        out=dest[:, sl], in0=ps[:], scalar1=beb[:, j:j + 1],
                        scalar2=ewb[:, ew_idx:ew_idx + 1], op0=ALU.add,
                        op1=ALU.mult)
                mm_ff(p, wsb, 1024, 1024, TMP, evac)

        def proj_mm(wname, c, src, dest, ew_idx):
            with p.scope():
                wsb, _ = load_w(p, Wsl(wname, c), 1024, 1024)
                p.barrier()

                def evac(j, n, ps, mtok):
                    p.wait(nc.vector, "mmm", mtok)
                    sl = slice(j * T + n * 512, j * T + (n + 1) * 512)
                    return nc.vector.tensor_scalar_mul(
                        dest[:, sl], ps[:], ewb[:, ew_idx:ew_idx + 1])
                mm_ff(p, wsb, 1024, 1024, src, evac)

        for c in range(6):
            eq, ek, ev = L0F_EDGES[c]
            iq, ik, iv_ = L0F_INS[c]
            ew_q, ew_k, ew_v = 3 * c + 0, 3 * c + 1, 3 * c + 2
            last = (c == 5)
            with p.scope():
                MQ = p.sbuf([128, DC * T], BF16, "sMQ")
                MK = p.sbuf([128, DC * T], BF16, "sMK")
                edge_mm(eq, iq, Q)
                p.barrier()
                proj_mm("Wq", c, Q, MQ, ew_q)
                p.barrier()
                if (ek, ik) == (eq, iq):
                    ksrc = Q
                else:
                    edge_mm(ek, ik, K)
                    p.barrier()
                    ksrc = K
                proj_mm("Wk", c, ksrc, MK, ew_k)
                p.barrier()
                if (ev, iv_) == (eq, iq):
                    vsrc = Q
                elif (ev, iv_) == (ek, ik):
                    vsrc = ksrc
                else:
                    edge_mm(ev, iv_, K)
                    p.barrier()
                    vsrc = K
                proj_mm("Wv", c, vsrc, MV, ew_v)
                p.barrier()
                attention_c(p, MQ, MK, MV, CTX, H, dh, dh ** -0.5, identb,
                            tag=tag + f"a{c}")
                if not last:
                    with p.scope():
                        wsb, _ = load_w(p, Wsl("Wo", c), 1024, 1024)
                        tsc = p.sbuf([128, 512], BF16, "tsc")
                        p.barrier()

                        def evac(j, n, ps, mtok):
                            sl = slice(j * T + n * 512, j * T + (n + 1) * 512)
                            p.wait(nc.vector, "mmm", mtok)
                            nc.vector.tensor_scalar_mul(tsc[:], ps[:],
                                                        awb[:, c:c + 1])
                            nc.vector.drain()
                            return nc.vector.scalar_tensor_tensor(
                                out=MK[:, sl], in0=Q[:, sl],
                                scalar=ewb[:, ew_q:ew_q + 1],
                                in1=tsc[:], op0=ALU.mult, op1=ALU.add)
                        mm_ff(p, wsb, 1024, 1024, CTX, evac)
                        p.barrier()
                        p.store(nodes_d[c][:], MK[:], tag + "nst")
                        p.wait(nc.sync, tag + "nst")
            if last:
                with p.scope():
                    wsb, _ = load_w(p, Wsl("Wo", c), 1024, 1024)
                    n5f = p.sbuf([128, DC * T], F32, "n5f")
                    tsc = p.sbuf([128, 512], F32, "tsc")
                    p.barrier()

                    def evac5(j, n, ps, mtok):
                        sl = slice(j * T + n * 512, j * T + (n + 1) * 512)
                        p.wait(nc.vector, "mmm", mtok)
                        nc.vector.tensor_scalar_mul(tsc[:], ps[:], awb[:, c:c + 1])
                        nc.vector.drain()
                        return nc.vector.scalar_tensor_tensor(
                            out=n5f[:, sl], in0=Q[:, sl],
                            scalar=ewb[:, ew_q:ew_q + 1],
                            in1=tsc[:], op0=ALU.mult, op1=ALU.add)
                    mm_ff(p, wsb, 1024, 1024, CTX, evac5)
                    p.barrier()
                    gf = p.sbuf([128, D], F32, "gf")
                    bf = p.sbuf([128, D], F32, "bf")
                    p.load(gf[:], w["lng_bc"][:])
                    p.load(bf[:], w["lnb_bc"][:])
                    p.barrier()
                    ln_residual_store(p, n5f, x_dram, out_dram, identf,
                                      gf, bf, 1e-6, 0.5, tag=tag + "lrs")
            p.barrier()


def sub_l1ffn(p, x_dram, out_dram, w, tag=None):
    nc = p.nc
    tag = "mo"
    E, HD = 8, 512
    B = {s: p.tail(tag + s) for s in ("gsc", "gnm", "gex", "yz")}
    with p.scope():
        identb = p.sbuf([128, 128], BF16, "idb")
        p.loadc(identb[:], w["ident"][:])
        p.barrier()
        xT = p.sbuf([128, DC * T], BF16, "xT")
        ln_to_xT(p, x_dram, xT, identb, eps=1e-5, gain=None, tag=tag + "ln")
        p.barrier()
        if "dbg_xT" in w:
            xc32 = p.sbuf([128, T], F32, "dbgx")
            for cc in range(DC):
                p.wait(nc.vector, tag + "dbg", cc * 16)
                p.inc(nc.vector.tensor_copy(out=xc32[:], in_=xT[:, cc * T:(cc + 1) * T]), tag + "dx")
                p.wait(nc.sync, tag + "dx", cc + 1)
                p.store(w["dbg_xT"][cc * 128:(cc + 1) * 128, :], xc32[:], tag + "dbg")
            p.barrier()

        scale_sb = p.sbuf([128, NT * E], F32, "gsc")
        with p.scope():
            gwb = p.sbuf([128, E * E], BF16, "gw")
            for kc in range(DC):
                p.loadc(gwb[:, kc * E:(kc + 1) * E],
                        w["gate_w"][kc * 128:(kc + 1) * 128, :], "ldg")
            gps = p.psum([128, E], F32, "gps")
            lg = p.sbuf([128, E], F32, "lg")
            ex = p.sbuf([128, E], F32, "ex")
            sm = p.sbuf([128, E], F32, "sm")
            nmx = p.sbuf([128, 1], F32, "nmx")
            ssum = p.sbuf([128, 1], F32, "ssum")
            rsum = p.sbuf([128, 1], F32, "rs")
            m1 = p.sbuf([128, 1], F32, "m1")
            m2 = p.sbuf([128, 1], F32, "m2")
            msk = p.sbuf([128, E], F32, "msk")
            msk2 = p.sbuf([128, E], F32, "msk2")
            dbgg = p.sbuf([128, 32], F32, "dbgg") if "dbg_gate" in w else None
            dbglg = p.sbuf([128, NT * 8], F32, "dbglg") if "dbg_lg" in w else None
            p.barrier()
            if "dbg_gw" in w:
                gwf = p.sbuf([128, 64], F32, "gwf")
                p.inc(nc.vector.tensor_copy(out=gwf[:], in_=gwb[:]), tag + "dgw")
                p.wait(nc.sync, tag + "dgw")
                p.store(w["dbg_gw"][:], gwf[:], tag + "dbg")
                p.barrier()
            for tt in range(NT):
                p.wait(nc.tensor, tag + "gsc", B["gsc"] + tt)
                im = None
                for kc in range(DC):
                    im = nc.tensor.matmul(gps[:], xT[:, kc * T + tt * 128:
                                                     kc * T + (tt + 1) * 128],
                                          gwb[:, kc * E:(kc + 1) * E],
                                          start=(kc == 0), stop=(kc == DC - 1))
                gm = p.inc(im, tag + "gm")
                p.wait(nc.vector, tag + "gm", gm)
                nc.vector.tensor_copy(out=lg[:], in_=gps[:])
                if "dbg_lg" in w:
                    nc.vector.tensor_copy(out=dbglg[:, tt * 8:(tt + 1) * 8], in_=gps[:])
                nc.vector.drain()
                nc.vector.reduce_max(out=nmx[:], in_=lg[:], axis=AX.X, negate=True)
                nc.vector.drain()
                p.inc(nc.vector.engine_nop(), tag + "gnm")
                p.wait(nc.scalar, tag + "gnm", B["gnm"] + tt + 1)
                p.inc(nc.scalar.activation(out=ex[:], in_=lg[:], func=AF.Exp,
                                           bias=nmx[:]), tag + "gex")
                p.wait(nc.vector, tag + "gex", B["gex"] + tt + 1)
                nc.vector.reduce_sum(out=ssum[:], in_=ex[:], axis=AX.X)
                nc.vector.drain()
                nc.vector.reciprocal(out=rsum[:], in_=ssum[:])
                nc.vector.drain()
                nc.vector.tensor_scalar_mul(sm[:], ex[:], rsum[:])
                nc.vector.drain()
                if "dbg_gate" in w and tt == 0:
                    nc.vector.tensor_copy(out=dbgg[:, 0:8], in_=lg[:])
                    nc.vector.tensor_copy(out=dbgg[:, 8:16], in_=ex[:])
                    nc.vector.tensor_copy(out=dbgg[:, 16:17], in_=ssum[:])
                    nc.vector.tensor_copy(out=dbgg[:, 17:18], in_=rsum[:])
                    nc.vector.tensor_copy(out=dbgg[:, 24:32], in_=sm[:])
                    nc.vector.tensor_copy(out=dbgg[:, 18:19], in_=nmx[:])
                nc.vector.reduce_max(out=m1[:], in_=sm[:], axis=AX.X)
                nc.vector.drain()
                nc.vector.tensor_single_scalar(out=msk[:], in_=sm[:], scalar=m1[:],
                                               op=ALU.is_ge)
                nc.vector.drain()
                nc.vector.scalar_tensor_tensor(out=msk2[:], in0=msk[:], scalar=-1.0,
                                               in1=sm[:], op0=ALU.mult,
                                               op1=ALU.mult)
                nc.vector.drain()
                nc.vector.tensor_add(msk2[:], msk2[:], sm[:])
                nc.vector.drain()
                nc.vector.reduce_max(out=m2[:], in_=msk2[:], axis=AX.X)
                nc.vector.drain()
                nc.vector.tensor_single_scalar(out=msk2[:], in_=msk2[:],
                                               scalar=m2[:], op=ALU.is_ge)
                nc.vector.drain()
                nc.vector.tensor_add(msk[:], msk[:], msk2[:])
                nc.vector.drain()
                p.inc(nc.vector.tensor_mul(scale_sb[:, tt * E:(tt + 1) * E],
                                           sm[:], msk[:]), tag + "gsc")
            if "dbg_gate" in w:
                p.store(w["dbg_gate"][:], dbgg[:], tag + "dbg")
            if "dbg_lg" in w:
                p.store(w["dbg_lg"][:], dbglg[:], tag + "dbg")
        p.barrier()
        if "dbg_scale" in w:
            p.store(w["dbg_scale"][:], scale_sb[:], tag + "dbg")
            p.wait(nc.sync, tag + "dbg")

        y = p.sbuf([128, NT * D], F32, "yacc")
        p.inc(nc.vector.memset(y[:], 0.0), tag + "yz")
        sa = p.sbuf([128, HD], F32, "sa")
        hb = p.sbuf([128, HD], F32, "hb")
        hh = p.sbuf([128, HD], BF16, "hh")
        hTs = p.sbuf([128, HD], BF16, "hTs")
        aps = p.psum([128, HD], F32, "aps")
        bps = p.psum([128, HD], F32, "bps")
        tps = p.psum([128, HD], BF16, "tps")
        yps = p.psum([128, D], F32, "yps")
        w1s = [p.sbuf([128, DC * HD], BF16, "w1b") for _ in range(2)]
        w3s = [p.sbuf([128, DC * HD], BF16, "w3b") for _ in range(2)]
        w2s = [p.sbuf([128, 4 * D], BF16, "w2b") for _ in range(2)]

        def exp_load(e, par):
            sn = tag + "wl" + str(par)
            tok = None
            for kc in range(DC):
                r = slice(e * 1024 + kc * 128, e * 1024 + (kc + 1) * 128)
                tok = p.loadc(w1s[par][:, kc * HD:(kc + 1) * HD], w["w1"][r, :], sn)
                tok = p.loadc(w3s[par][:, kc * HD:(kc + 1) * HD], w["w3"][r, :], sn)
            for kc in range(4):
                r = slice(e * HD + kc * 128, e * HD + (kc + 1) * 128)
                tok = p.loadc(w2s[par][:, kc * D:(kc + 1) * D], w["w2"][r, :], sn)
            return tok

        wtok = [exp_load(0, 0), 0]
        lastmm = [0, 0]
        for e in range(E):
            par = e % 2
            if e + 1 < E:
                if lastmm[(e + 1) % 2]:
                    p.wait(nc.gpsimd, "mot", lastmm[(e + 1) % 2])
                wtok[(e + 1) % 2] = exp_load(e + 1, (e + 1) % 2)
            p.wait(nc.tensor, tag + "wl" + str(par), wtok[par])
            w1b, w3b, w2b = w1s[par], w3s[par], w2s[par]
            if True:
                for tt in range(NT):
                    sc = scale_sb[:, tt * E + e: tt * E + e + 1]
                    p.wait(nc.tensor, "mov", p.tail("mov"))
                    ia = ib = None
                    for kc in range(DC):
                        lhs = xT[:, kc * T + tt * 128: kc * T + (tt + 1) * 128]
                        ia = nc.tensor.matmul(aps[:], lhs, w1b[:, kc * HD:(kc + 1) * HD],
                                              start=(kc == 0), stop=(kc == DC - 1))
                    for kc in range(DC):
                        lhs = xT[:, kc * T + tt * 128: kc * T + (tt + 1) * 128]
                        ib = nc.tensor.matmul(bps[:], lhs, w3b[:, kc * HD:(kc + 1) * HD],
                                              start=(kc == 0), stop=(kc == DC - 1))
                    abm = p.inc(ib, "mot")
                    p.wait(nc.scalar, "mot", abm)
                    p.inc(nc.scalar.activation(out=sa[:], in_=aps[:], func=AF.Silu,
                                               scale=sc), tag + "hsa")
                    p.wait(nc.vector, "mot", abm)
                    nc.vector.tensor_scalar_mul(hb[:], bps[:], sc)
                    nc.vector.drain()
                    p.wait(nc.vector, tag + "hsa", p.tail(tag + "hsa"))
                    ih = nc.vector.tensor_mul(hh[:], sa[:], hb[:])
                    hse = p.inc(ih, "mov")
                    p.wait(nc.tensor, "mov", hse)
                    p.wait(nc.tensor, "mov", p.tail("mov"))
                    it = None
                    for c4 in range(4):
                        it = nc.tensor.transpose(tps[:, c4 * 128:(c4 + 1) * 128],
                                                 hh[:, c4 * 128:(c4 + 1) * 128],
                                                 identb[:])
                    htm = p.inc(it, "mot")
                    p.wait(nc.vector, "mot", htm)
                    ihe = nc.vector.tensor_copy(out=hTs[:], in_=tps[:])
                    hte = p.inc(ihe, "mov")
                    p.wait(nc.tensor, "mov", hte)
                    p.wait(nc.tensor, "mov", p.tail("mov"))
                    iy = None
                    for hc in range(4):
                        for half in range(2):
                            iy = nc.tensor.matmul(
                                yps[:, half * 512:(half + 1) * 512],
                                hTs[:, hc * 128:(hc + 1) * 128],
                                w2b[:, hc * D + half * 512: hc * D + (half + 1) * 512],
                                start=(hc == 0), stop=(hc == 3))
                    ym = p.inc(iy, "mot")
                    p.wait(nc.vector, "mot", ym)
                    if e == 0 and tt == 0:
                        p.wait(nc.vector, tag + "yz", B["yz"] + 1)
                    iya = nc.vector.tensor_add(y[:, tt * D:(tt + 1) * D],
                                               y[:, tt * D:(tt + 1) * D], yps[:])
                    p.inc(iya, "mov")
            lastmm[par] = p.tail("mot")
        p.barrier()
        xtile = p.sbuf([128, D], F32, "rx")
        otile = p.sbuf([128, D], F32, "ro")
        Brev = p.tail("mov")
        Brst = p.tail("lnst")
        for tt in range(NT):
            p.wait(nc.sync, "mov", Brev + tt)
            lt = p.load(xtile[:], x_dram[tt * 128:(tt + 1) * 128, :], "lnld0")
            p.wait(nc.vector, "lnld0", lt)
            p.wait(nc.vector, "lnst", Brst + tt * 16)
            iv = nc.vector.scalar_tensor_tensor(out=otile[:],
                                                in0=y[:, tt * D:(tt + 1) * D],
                                                scalar=0.5, in1=xtile[:],
                                                op0=ALU.mult, op1=ALU.add)
            rev = p.inc(iv, "mov")
            p.wait(nc.sync, "mov", rev)
            p.store(out_dram[tt * 128:(tt + 1) * 128, :], otile[:], "lnst")
        p.wait(nc.sync, "lnst")


def mamba_branch(p, xz_d, ocum_d, w, pfx, tag=None, reverse=False,
                 zmax_sb=None):
    tag = "mb"
    p._dma_sems.add(tag + "oc")
    B = {s: p.tail(tag + s) for s in ("cse", "cvv", "xcw", "dld", "zs", "yd",
                                     "cvd")}
    """One mamba branch. xz_d: DRAM [4096, 2048] fp32 (feature-major).
    ocum_d: DRAM [2048, 2048] fp32 accumulator (write if first=not reverse,
    DMA-add otherwise). reverse: process xz reversed along l (bwd branch).
    zmax_sb: [128, 16] sbuf to fill with per-chunk max_l silu(z) (fwd only).
    """
    nc = p.nc
    DI, DS, DR = 2048, 16, 64
    KC = DI // 128

    with p.scope():
        convw = p.sbuf([128, KC * 4], F32, "cvw")
        convb = p.sbuf([128, KC], F32, "cvb")
        Asb = p.sbuf([128, KC * DS], F32, "A")
        Dsb = p.sbuf([128, KC], F32, "D")
        dtb = p.sbuf([128, KC], F32, "dtb")
        for kc in range(KC):
            r0 = kc * 128
            p.load(convw[:, kc * 4:(kc + 1) * 4], w[pfx + "conv_w"][r0:r0 + 128, :], "ld")
            p.load(convb[:, kc:kc + 1], w[pfx + "conv_b"][r0:r0 + 128, :], "ld")
            p.load(Asb[:, kc * DS:(kc + 1) * DS], w[pfx + "A_log"][r0:r0 + 128, :], "ld")
            p.load(Dsb[:, kc:kc + 1], w[pfx + "Dp"][r0:r0 + 128, :], "ld")
            p.load(dtb[:, kc:kc + 1], w[pfx + "dt_b"][r0:r0 + 128, :], "ld")
        p.barrier()
        p.inc(nc.scalar.activation(out=Asb[:], in_=Asb[:], func=AF.Exp),
              "mbs")
        p.wait(nc.vector, "mbs")
        nc.vector.drain()
        p.inc(nc.vector.tensor_scalar_mul(Asb[:], Asb[:], -1.0), "mbv")
        p.barrier()

        xconv_d = nc.dram_tensor(p.name(tag + "xc"), [DI, T], F32, kind="Internal").ap()
        delta_d = nc.dram_tensor(p.name(tag + "dl"), [DI, T], F32, kind="Internal").ap()
        bc_d = nc.dram_tensor(p.name(tag + "bc"), [32, T], F32, kind="Internal").ap()

        ph12 = p.scope()
        ph12.__enter__()
        xcb = p.sbuf([128, KC * T], BF16, "xcb")
        xpT = p.sbuf([128, KC * 96], BF16, "xpT")
        dtT = p.sbuf([64, DI], BF16, "dtT")
        for kc in range(KC):
            r0 = kc * 128
            p.loadc(xpT[:, kc * 96:(kc + 1) * 96], w[pfx + "x_projT"][r0:r0 + 128, :], "ldg")
        p.loadc(dtT[:, :], w[pfx + "dt_projT"][:, :], "ldg")
        ldg_tok = p.tail("ldg")
        for e in (nc.tensor, nc.vector, nc.scalar):
            p.wait(e, "ldg", ldg_tok)
        with p.scope():
            xzc = p.sbuf([128, T], F32, "xzc")
            xrv = p.sbuf([128, T], F32, "xrv")
            cacc = p.sbuf([128, T], F32, "cacc")
            xcv = p.sbuf([128, T], F32, "xcv")
            for kc in range(KC):
                par = tag + "zl" + str(kc % 2)
                p.wait(nc.sync, tag + "cse", B["cse"] + kc - 1)
                lt = p.load(xzc[:], xz_d[kc * 128:(kc + 1) * 128, :], par)
                p.wait(nc.vector, par, lt)
                if reverse:
                    nc.vector.tensor_copy(out=xrv[:], in_=xzc[:, ::-1])
                    nc.vector.drain()
                    src = xrv
                else:
                    src = xzc
                wc = convw[:, kc * 4:kc * 4 + 4]
                p.wait(nc.vector, tag + "cse", B["cse"] + kc)
                nc.vector.tensor_scalar_mul(cacc[:], src[:], wc[:, 3:4])
                for t_ in (2, 1, 0):
                    sh = 3 - t_
                    nc.vector.drain()
                    nc.vector.scalar_tensor_tensor(
                        out=cacc[:, sh:T], in0=src[:, 0:T - sh],
                        scalar=wc[:, t_:t_ + 1], in1=cacc[:, sh:T],
                        op0=ALU.mult, op1=ALU.add)
                iv = nc.vector.drain()
                p.inc(nc.vector.engine_nop(), tag + "cvv")
                p.wait(nc.scalar, tag + "cvv", B["cvv"] + kc + 1)
                p.wait(nc.scalar, tag + "xcw", B["xcw"] + kc)
                isl = nc.scalar.activation(out=xcv[:], in_=cacc[:], func=AF.Silu,
                                           bias=convb[:, kc:kc + 1])
                cse = p.inc(isl, tag + "cse")
                p.wait(nc.vector, tag + "cse", cse)
                nc.vector.drain()
                ic = nc.vector.tensor_copy(out=xcb[:, kc * T:(kc + 1) * T], in_=xcv[:])
                p.inc(ic, tag + "xcw")
                p.wait(nc.sync, tag + "xcw", p.tail(tag + "xcw"))
                p.store(xconv_d[kc * 128:(kc + 1) * 128, :], xcv[:], tag + "cvd")
            p.wait(nc.sync, tag + "cvd")
        p.barrier()

        dtsb = p.sbuf([64, T], BF16, "dts")
        with p.scope():
            dt_ps = p.psum([64, 512], F32, "dtp")
            b_ps = p.psum([16, 512], F32, "bp")
            c_ps = p.psum([16, 512], F32, "cp")
            Bsb0 = p.sbuf([16, T], F32, "bsb")
            Csb0 = p.sbuf([16, T], F32, "csb")
            for n in range(NQ):
                sl = slice(n * 512, (n + 1) * 512)
                p.wait(nc.tensor, tag + "dbe", p.tail(tag + "dbe"))
                for kc in range(KC):
                    i1 = nc.tensor.matmul(dt_ps[:], xpT[:, kc * 96:kc * 96 + 64],
                                          xcb[:, kc * T + n * 512:kc * T + (n + 1) * 512],
                                          start=(kc == 0), stop=(kc == KC - 1))
                for kc in range(KC):
                    i2 = nc.tensor.matmul(b_ps[:], xpT[:, kc * 96 + 64:kc * 96 + 80],
                                          xcb[:, kc * T + n * 512:kc * T + (n + 1) * 512],
                                          start=(kc == 0), stop=(kc == KC - 1))
                for kc in range(KC):
                    i3 = nc.tensor.matmul(c_ps[:], xpT[:, kc * 96 + 80:kc * 96 + 96],
                                          xcb[:, kc * T + n * 512:kc * T + (n + 1) * 512],
                                          start=(kc == 0), stop=(kc == KC - 1))
                mt = p.inc(i3, tag + "dbm")
                p.wait(nc.vector, tag + "dbm", mt)
                nc.vector.tensor_copy(out=dtsb[:, sl], in_=dt_ps[:])
                nc.vector.tensor_copy(out=Bsb0[:, sl], in_=b_ps[:])
                p.inc(nc.vector.tensor_copy(out=Csb0[:, sl], in_=c_ps[:]),
                      tag + "dbe")
            p.barrier()
            p.store(bc_d[0:16, :], Bsb0[:, :], "st")
            p.store(bc_d[16:32, :], Csb0[:, :], "st")
            p.wait(nc.sync, "st")
        with p.scope():
            d_ps = [p.psum([128, 512], F32, "dps"), p.psum([128, 512], F32, "dps")]
            sabs = p.sbuf([128, 512], F32, "sab")
            sexp = p.sbuf([128, 512], F32, "sex")
            srel = p.sbuf([128, 512], F32, "srl")
            dchunk = p.sbuf([128, T], F32, "dch")
            si = 0
            for jd in range(KC):
                p.wait(nc.vector, tag + "dld", B["dld"] + jd * 16)
                for n in range(NQ):
                    sl = slice(n * 512, (n + 1) * 512)
                    p.wait(nc.tensor, tag + "dle", p.tail(tag + "dle") - 1)
                    im = nc.tensor.matmul(d_ps[si][:],
                                          dtT[:, jd * 128:(jd + 1) * 128],
                                          dtsb[:, sl], start=True, stop=True)
                    mt = p.inc(im, tag + "dlm")
                    p.wait(nc.scalar, tag + "dlm", mt)
                    p.wait(nc.scalar, tag + "dlv", p.tail(tag + "dlv"))
                    nc.scalar.activation(out=srel[:], in_=d_ps[si][:], func=AF.Relu,
                                         bias=dtb[:, jd:jd + 1])
                    nc.scalar.drain()
                    nc.scalar.activation(out=sabs[:], in_=d_ps[si][:], func=AF.Abs,
                                         bias=dtb[:, jd:jd + 1])
                    nc.scalar.drain()
                    nc.scalar.activation(out=sexp[:], in_=sabs[:], func=AF.Exp,
                                         scale=-1.0)
                    nc.scalar.drain()
                    ie = nc.scalar.activation(out=sabs[:], in_=sexp[:], func=AF.Ln,
                                              bias=p.constf(1.0))
                    se = p.inc(ie, tag + "dle")
                    p.wait(nc.vector, tag + "dle", se)
                    iv = nc.vector.tensor_add(dchunk[:, sl], srel[:], sabs[:])
                    p.inc(iv, tag + "dlv")
                    si ^= 1
                p.wait(nc.sync, tag + "dlv")
                p.store(delta_d[jd * 128:(jd + 1) * 128, :], dchunk[:], tag + "dld")
                p.wait(nc.sync, tag + "dld")
        ph12.__exit__(None, None, None)
        p.barrier()

        with p.scope():
            Bball = p.sbuf([128, DS * T], BF16, "Bb")
            Cball = p.sbuf([128, DS * T], BF16, "Cb")
            with p.scope():
                rowst = p.sbuf([128, T], F32, "rwst")
                for n in range(DS):
                    for (dst, roff) in ((Bball, 0), (Cball, 16)):
                        bc_in = bass.AP(tensor=bc_d.tensor, offset=(roff + n) * T,
                                        ap=[[0, 128], [1, T]])
                        lt = p.load(rowst[:], bc_in, "ld")
                        p.wait(nc.vector, "ld", lt)
                        p.inc(nc.vector.tensor_copy(out=dst[:, n * T:(n + 1) * T],
                                                    in_=rowst[:]), tag + "bcc")
                        p.wait(nc.sync, tag + "bcc", p.tail(tag + "bcc"))
            dch = p.sbuf([128, T], F32, "dch")
            xcvc = p.sbuf([128, T], F32, "xcv")
            ub = p.sbuf([128, T], BF16, "ub")
            dAb = p.sbuf([128, T], F32, "dA")
            dBu = [p.sbuf([128, T], F32, "dBu"), p.sbuf([128, T], F32, "dBu")]
            hsb = [p.sbuf([128, T], BF16, "hs"), p.sbuf([128, T], BF16, "hs")]
            prod = [p.sbuf([128, T], F32, "prod"), p.sbuf([128, T], F32, "prod")]
            yacc = p.sbuf([128, T], F32, "y")
            zch = dch
            zsl = dBu[0]
            for kc in range(KC):
                p.wait(nc.sync, tag + "zs", B["zs"] + kc)
                p.wait(nc.sync, tag + "yd", B["yd"] + kc)
                l1 = p.load(dch[:], delta_d[kc * 128:(kc + 1) * 128, :], "mbl" + str(kc % 2))
                l2 = p.load(xcvc[:], xconv_d[kc * 128:(kc + 1) * 128, :], "mbl" + str(kc % 2))
                p.wait(nc.vector, "mbl" + str(kc % 2), l1)
                p.wait(nc.vector, "mbl" + str(kc % 2), l2)
                iu = nc.vector.tensor_mul(ub[:], dch[:], xcvc[:])
                uu = p.inc(iu, "mbv")
                p.wait(nc.scalar, "mbv", uu)
                nc.vector.memset(yacc[:], 0.0)
                p.inc(nc.vector.tensor_mul(dBu[0][:], ub[:],
                                           Bball[:, 0 * T:1 * T]), "mbv")
                p.inc(nc.vector.tensor_mul(dBu[1][:], ub[:],
                                           Bball[:, 1 * T:2 * T]), "mbv")
                sc_tok = 0
                for n in range(DS):
                    if n > 0:
                        p.wait(nc.scalar, "mbv", sc_tok)
                    ia = nc.scalar.activation(out=dAb[:], in_=dch[:], func=AF.Exp,
                                              scale=Asb[:, kc * DS + n:kc * DS + n + 1])
                    da = p.inc(ia, tag + "sdA")
                    p.wait(nc.vector, tag + "sdA", da)
                    isc = nc.vector.tensor_tensor_scan(
                        out=hsb[n % 2][:], data0=dAb[:], data1=dBu[n % 2][:],
                        initial=0.0, op0=ALU.mult, op1=ALU.add)
                    sc_tok = p.inc(isc, "mbv")
                    if n + 2 < DS:
                        p.inc(nc.vector.tensor_mul(
                            dBu[n % 2][:], ub[:],
                            Bball[:, (n + 2) * T:(n + 3) * T]), "mbv")
                    if n >= 1:
                        p.inc(nc.vector.tensor_mul(
                            prod[(n - 1) % 2][:], hsb[(n - 1) % 2][:],
                            Cball[:, (n - 1) * T:n * T]), "mbv")
                    if n >= 2:
                        p.inc(nc.vector.tensor_add(yacc[:], yacc[:],
                                                   prod[(n - 2) % 2][:]), "mbv")
                p.inc(nc.vector.tensor_mul(prod[1][:], hsb[1][:],
                                           Cball[:, (DS - 1) * T:DS * T]), "mbv")
                p.inc(nc.vector.tensor_add(yacc[:], yacc[:], prod[0][:]), "mbv")
                nc.vector.drain()
                p.inc(nc.vector.tensor_add(yacc[:], yacc[:], prod[1][:]), "mbv")
                nc.vector.drain()
                iy = nc.vector.scalar_tensor_tensor(
                    out=yacc[:], in0=xcvc[:], scalar=Dsb[:, kc:kc + 1], in1=yacc[:],
                    op0=ALU.mult, op1=ALU.add)
                p.inc(iy, tag + "yd")
                p.wait(nc.sync, "mbv", p.tail("mbv"))
                lz = p.load(zch[:], xz_d[DI + kc * 128: DI + (kc + 1) * 128, :],
                            "mbl" + str(kc % 2))
                p.wait(nc.scalar, "mbl" + str(kc % 2), lz)
                p.wait(nc.scalar, "mbv", p.tail("mbv"))
                if reverse:
                    p.wait(nc.vector, "mbl" + str(kc % 2), lz)
                    nc.vector.drain()
                    izr = nc.vector.tensor_copy(out=zsl[:], in_=zch[:, ::-1])
                    p.inc(izr, "mbv")
                    p.wait(nc.scalar, "mbv", p.tail("mbv"))
                    iz = nc.scalar.activation(out=zsl[:], in_=zsl[:], func=AF.Silu)
                else:
                    p.wait(nc.scalar, "mbv", p.tail("mbv"))
                    iz = nc.scalar.activation(out=zsl[:], in_=zch[:], func=AF.Silu)
                zs = p.inc(iz, tag + "zs")
                p.wait(nc.vector, tag + "zs", zs)
                nc.vector.drain()
                if (zmax_sb is not None) and not reverse:
                    nc.vector.reduce_max(out=zmax_sb[:, kc:kc + 1], in_=zsl[:],
                                         axis=AX.X)
                    nc.vector.drain()
                ig = nc.vector.tensor_mul(yacc[:], yacc[:], zsl[:])
                p.inc(ig, "mbv")
                nc.vector.drain()
                if reverse:
                    nc.vector.tensor_copy(out=prod[0][:], in_=yacc[:, ::-1])
                    out_src = prod[0]
                else:
                    out_src = yacc
                nc.vector.drain()
                gv = p.inc(nc.vector.engine_nop(), "mbv")
                p.wait(nc.gpsimd, "mbv", gv)
                if reverse:
                    p.inc(nc.gpsimd.dma_start(
                        out=ocum_d[kc * 128:(kc + 1) * 128, :], in_=out_src[:],
                        accum_op=ALU.add), tag + "oc", 16)
                else:
                    p.inc(nc.gpsimd.dma_start(
                        out=ocum_d[kc * 128:(kc + 1) * 128, :], in_=out_src[:]),
                        tag + "oc", 16)
                p.wait(nc.vector, tag + "oc", p.tail(tag + "oc"))
        p.barrier()


def sub_l1attn(p, x_dram, out_dram, w, tag=None, with_fft=True,
               branches=("f", "b")):
    nc = p.nc
    tag = "la"
    DI = 2048
    B = {s: p.tail(tag + s) for s in ("ipd", "ybc")}
    with p.scope():
        identb = p.sbuf([128, 128], BF16, "idb")
        identf = p.sbuf([128, 128], F32, "idf")
        p.loadc(identb[:], w["ident"][:])
        p.load(identf[:], w["ident"][:])
        zmax = p.sbuf([128, 16], F32, "zmax")
        p.barrier()
        xz_d = nc.dram_tensor(p.name(tag + "xz"), [2 * DI, T], F32,
                              kind="Internal").ap()
        ocum_d = nc.dram_tensor(p.name(tag + "oc"), [DI, T], F32,
                                kind="Internal").ap()

        with p.scope():
            xT = p.sbuf([128, DC * T], BF16, "xT")
            ln_to_xT(p, x_dram, xT, identb, eps=1e-5, gain=None, tag=tag + "ln")
            p.barrier()
            stg = p.sbuf([128, T], F32, "stg")
            with p.scope():
                wq, _ = load_w(p, w["in_projT"], 1024, 4096)
                p.barrier()

                KCW = 4096 // 128
                ps2 = [p.psum([128, 512], F32, "ip"), p.psum([128, 512], F32, "ip")]
                slot_tok = [0, 0]
                si = 0
                for j in range(KCW):
                    for n in range(NQ):
                        if slot_tok[si]:
                            p.wait(nc.tensor, tag + "ipe", slot_tok[si])
                        im = None
                        for kc in range(DC):
                            im = nc.tensor.matmul(
                                ps2[si][:], w_tile(wq, 4096, kc, j),
                                xT[:, kc * T + n * 512:kc * T + (n + 1) * 512],
                                start=(kc == 0), stop=(kc == DC - 1))
                        mtok = p.inc(im, tag + "ipm")
                        p.wait(nc.scalar, tag + "ipm", mtok)
                        p.wait(nc.scalar, tag + "ipd", B["ipd"] + j * 16)
                        ie = nc.scalar.activation(
                            out=stg[:, n * 512:(n + 1) * 512], in_=ps2[si][:],
                            func=AF.Copy)
                        slot_tok[si] = p.inc(ie, tag + "ipe")
                        si ^= 1
                    p.wait(nc.sync, tag + "ipe", p.tail(tag + "ipe"))
                    p.store(xz_d[j * 128:(j + 1) * 128, :], stg[:], tag + "ipd")
        p.barrier()

        if "f" in branches:
            mamba_branch(p, xz_d, ocum_d, w, "f_", tag + "mf", reverse=False,
                         zmax_sb=zmax)
        if "b" in branches:
            mamba_branch(p, xz_d, ocum_d, w, "b_", tag + "mb",
                         reverse=True)
        if "f" not in branches:
            pass

        if with_fft:
            fft_branch(p, xz_d, ocum_d, w, zmax, identb, tag + "ff")

        with p.scope():
            oT = p.sbuf([128, DC * T], F32, "oT")
            wo, _ = load_w(p, w["out_projT"], 2048, 1024)
            ych = p.sbuf([128, T], BF16, "ych")
            yb = p.sbuf([128, 16 * T], BF16, "yb")
            for kc in range(16):
                par = tag + "yl" + str(kc % 2)
                p.wait(nc.gpsimd, tag + "ybc", B["ybc"] + kc)
                lt = p.loadc(ych[:], ocum_d[kc * 128:(kc + 1) * 128, :], par)
                p.wait(nc.vector, par, lt)
                nc.vector.drain()
                p.inc(nc.vector.tensor_copy(out=yb[:, kc * T:(kc + 1) * T],
                                            in_=ych[:]), tag + "ybc")
            p.barrier()

            def evac_o(j, n, ps, mtok):
                p.wait(nc.scalar, "mmm", mtok)
                return nc.scalar.activation(
                    out=oT[:, j * T + n * 512:j * T + (n + 1) * 512],
                    in_=ps[:], func=AF.Copy)
            mm_ff(p, wo, 2048, 1024, yb, evac_o, rhs_T=T)
            p.barrier()
            residual_store(p, oT, x_dram, out_dram, identf, scale_src=1.0,
                           tag=tag + "res")


def chunk_major(dram_ap, col0, ncols, nrows=2048):
    W = dram_ap.ap[-1][1]
    base = dram_ap.offset + col0
    return bass.AP(tensor=dram_ap.tensor, offset=base,
                   ap=[[W, 128], [128 * W, nrows // 128], [1, ncols]])


def dft_pass(p, in_re_d, in_im_d, out_re_d, out_im_d, fre_d, fim_d, fimn_d,
             identb, tag=None, in_fp32=False):
    tag = "df"
    """Complex DFT along the partition axis of in (2048), with output tiles
    transposed on evacuation so out dims are [transform-axis-free, other].
    in_im_d None => real input. F matrices dram bf16 [2048, 2048].
    All in/out DRAM [2048, 2048]; in bf16 (or fp32 with cast), out bf16."""
    nc = p.nc
    cplx = in_im_d is not None
    with p.scope():
        slabs = 2 if cplx else 1
        inslab = p.sbuf([128, 16 * 512 * slabs], BF16, "dfin")

        def in_sl(kc, which):
            off = which * 16 * 512
            return inslab[:, off + kc * 512: off + (kc + 1) * 512]
        fsl = p.sbuf([128, 16 * 128 * 3], BF16, "dfF")

        def f_sl(kc, which):
            off = which * 16 * 128
            return fsl[:, off + kc * 128: off + (kc + 1) * 128]
        pre = p.psum([128, 512], F32, "dfr")
        pim = p.psum([128, 512], F32, "dfi")
        ptr = p.psum([128, 512], BF16, "dft")
        ev1 = p.sbuf([128, 512], BF16, "dfe")
        ev2 = p.sbuf([128, 512], BF16, "dfe")
        def slab_ap(dram, cols):
            return dram.rearrange("(kc p) c -> p (kc c)", p=128)[:, :] \
                if cols is None else dram
        for n in range(NQ):
            p.wait(nc.gpsimd, tag + "mm", p.tail(tag + "mm"))
            p.loadc(inslab[:, 0:16 * 512], chunk_major(in_re_d, n * 512, 512),
                    tag + "il")
            if cplx:
                p.loadc(inslab[:, 16 * 512:32 * 512],
                        chunk_major(in_im_d, n * 512, 512), tag + "il")
            for j in range(16):
                p.wait(nc.gpsimd, tag + "mm", p.tail(tag + "mm"))
                for wi, fd in enumerate((fre_d, fim_d) + ((fimn_d,) if cplx else ())):
                    p.loadc(fsl[:, wi * 16 * 128:(wi + 1) * 16 * 128],
                            chunk_major(fd, j * 128, 128), tag + "fl")
                p.wait(nc.tensor, tag + "il", p.tail(tag + "il"))
                p.wait(nc.tensor, tag + "fl", p.tail(tag + "fl"))
                p.wait(nc.tensor, tag + "e1", p.tail(tag + "e1"))
                im1 = im2 = None
                for kc in range(16):
                    im1 = nc.tensor.matmul(pre[:], f_sl(kc, 0), in_sl(kc, 0),
                                           start=(kc == 0), stop=(kc == 15) and not cplx)
                if cplx:
                    for kc in range(16):
                        im1 = nc.tensor.matmul(pre[:], f_sl(kc, 2), in_sl(kc, 1),
                                               start=False, stop=(kc == 15))
                for kc in range(16):
                    im2 = nc.tensor.matmul(pim[:], f_sl(kc, 1), in_sl(kc, 0),
                                           start=(kc == 0), stop=(kc == 15) and not cplx)
                if cplx:
                    for kc in range(16):
                        im2 = nc.tensor.matmul(pim[:], f_sl(kc, 0), in_sl(kc, 1),
                                               start=False, stop=(kc == 15))
                mt = p.inc(im2, tag + "mm")
                for (ps, od) in ((pre, out_re_d), (pim, out_im_d)):
                    p.wait(nc.vector, tag + "mm", mt)
                    p.wait(nc.vector, tag + "tt", p.tail(tag + "tt"))
                    e1 = nc.vector.tensor_copy(out=ev1[:], in_=ps[:])
                    ee = p.inc(e1, tag + "e1")
                    p.wait(nc.tensor, tag + "e1", ee)
                    p.wait(nc.tensor, tag + "e2", p.tail(tag + "e2"))
                    it = None
                    for s in range(4):
                        it = nc.tensor.transpose(ptr[:, s * 128:(s + 1) * 128],
                                                 ev1[:, s * 128:(s + 1) * 128],
                                                 identb[:])
                    tt2 = p.inc(it, tag + "tt")
                    p.wait(nc.vector, tag + "tt", tt2)
                    p.wait(nc.vector, tag + "st", p.tail(tag + "st"))
                    e2 = nc.vector.tensor_copy(out=ev2[:], in_=ptr[:])
                    e2t = p.inc(e2, tag + "e2")
                    p.wait(nc.sync, tag + "e2", e2t)
                    for s in range(4):
                        p.store(od[n * 512 + s * 128: n * 512 + (s + 1) * 128,
                                   j * 128:(j + 1) * 128],
                                ev2[:, s * 128:(s + 1) * 128], tag + "st")
    p.barrier()


def fft_branch(p, xz_d, ocum_d, w, zmax, identb, tag=None):
    nc = p.nc
    tag = "ff"
    N2 = float(T) * float(T)
    X1r = nc.dram_tensor(p.name(tag + "x1r"), [T, T], BF16, kind="Internal").ap()
    X1i = nc.dram_tensor(p.name(tag + "x1i"), [T, T], BF16, kind="Internal").ap()
    X2r = nc.dram_tensor(p.name(tag + "x2r"), [T, T], BF16, kind="Internal").ap()
    X2i = nc.dram_tensor(p.name(tag + "x2i"), [T, T], BF16, kind="Internal").ap()
    YCr = nc.dram_tensor(p.name(tag + "ycr"), [T, T], BF16, kind="Internal").ap()
    YCi = nc.dram_tensor(p.name(tag + "yci"), [T, T], BF16, kind="Internal").ap()
    fre, fim, fimn = w["fft_fre"], w["fft_fim"], w["fft_fimn"]

    dft_pass(p, xz_d[0:T, :], None, X1r, X1i, fre, fim, fimn, identb,
             tag + "p1")
    dft_pass(p, X1r, X1i, X2r, X2i, fre, fim, fimn, identb, tag + "p2")

    with p.scope():
        KJ = 4096 // 128
        KI = 4096 // 128
        HW = T // 2
        xch = p.sbuf([128, KI * (HW + 4)], BF16, "xch")
        wslab = p.sbuf([128, 5 * KI * 128], BF16, "wcv")

        def xc_sl(kc, c0, cn):
            return xch[:, kc * (HW + 4) + c0: kc * (HW + 4) + c0 + cn]
        cps = [p.psum([128, 512], F32, "cvp"), p.psum([128, 512], F32, "cvp")]
        cevs = p.sbuf([128, 512], BF16, "cve")
        fcb = p.sbuf([128, KJ], F32, "fcb")
        for j in range(KJ):
            p.load(fcb[:, j:j + 1], w["fft_b"][j * 128:(j + 1) * 128, :], tag + "cbl")
        si = 0
        for half in range(2):
            base = half * HW
            p.barrier()
            for kc in range(KI):
                src_d = X2r if kc < 16 else X2i
                row0 = (kc % 16) * 128
                lo = max(0, base - 2)
                hi = min(T, base + HW + 2)
                off = lo - (base - 2)
                p.loadc(xc_sl(kc, off, hi - lo),
                        src_d[row0:row0 + 128, lo:hi], tag + "cxl")
                if base - 2 < 0:
                    p.inc(nc.vector.memset(xc_sl(kc, 0, 2), 0.0), tag + "cxz")
                if base + HW + 2 > T:
                    p.inc(nc.vector.memset(xc_sl(kc, HW + 2, 2), 0.0), tag + "cxz")
            p.barrier()
            for j in range(KJ):
                p.wait(nc.gpsimd, tag + "cmm", p.tail(tag + "cmm"))
                for t_ in range(5):
                    p.loadc(wslab[:, t_ * KI * 128:(t_ + 1) * KI * 128],
                            chunk_major(w["fftw_t%d" % t_], j * 128, 128,
                                        nrows=4096), tag + "cwl")
                p.wait(nc.tensor, tag + "cwl", p.tail(tag + "cwl"))
                for n in range(HW // 512):
                    p.wait(nc.tensor, tag + "cev", max(0, p.tail(tag + "cev") - 1))
                    first = True
                    imm = None
                    for t_ in range(5):
                        for kc in range(KI):
                            imm = nc.tensor.matmul(
                                cps[si][:],
                                wslab[:, (t_ * KI + kc) * 128:(t_ * KI + kc + 1) * 128],
                                xc_sl(kc, n * 512 + t_, 512),
                                start=first, stop=(t_ == 4 and kc == KI - 1))
                            first = False
                    cmt = p.inc(imm, tag + "cmm")
                    p.wait(nc.scalar, tag + "cmm", cmt)
                    p.wait(nc.scalar, tag + "cst", p.tail(tag + "cst"))
                    ie = nc.scalar.activation(out=cevs[:], in_=cps[si][:],
                                              func=AF.Silu, bias=fcb[:, j:j + 1])
                    cev = p.inc(ie, tag + "cev")
                    p.wait(nc.sync, tag + "cev", cev)
                    od = YCr if j < 16 else YCi
                    p.store(od[(j % 16) * 128:(j % 16 + 1) * 128,
                               base + n * 512: base + (n + 1) * 512],
                            cevs[:], tag + "cst")
                    si ^= 1
        p.wait(nc.sync, tag + "cst")
    p.barrier()

    dft_pass(p, YCr, YCi, X1r, X1i, fre, fimn, fim, identb, tag + "q1")
    dft_pass(p, X1r, X1i, X2r, X2i, fre, fimn, fim, identb, tag + "q2")

    with p.scope():
        rech = p.sbuf([128, T], BF16, "rech")
        imch = p.sbuf([128, T], BF16, "imch")
        sq1 = p.sbuf([128, T], F32, "sq1")
        sq2 = p.sbuf([128, T], F32, "sq2")
        vch = p.sbuf([128, T], F32, "vch")
        xcn = p.sbuf([128, T], F32, "xcn")
        zab = p.sbuf([128, 16], F32, "zab")
        s1 = p.sbuf([128, 1], F32, "s1")
        s2 = p.sbuf([128, 1], F32, "s2")
        mu = p.sbuf([128, 1], F32, "mu")
        lnv = p.sbuf([128, 1], F32, "lnv")
        rstd = p.sbuf([128, 1], F32, "rstd")
        p.inc(nc.scalar.activation(out=zab[:], in_=zmax[:], func=AF.Abs),
              tag + "zab")
        p.barrier()
        for kc in range(16):
            p.wait(nc.gpsimd, tag + "asq", p.tail(tag + "asq"))
            l1 = p.loadc(rech[:], X2r[kc * 128:(kc + 1) * 128, :], tag + "ar" + str(kc % 2))
            l2 = p.loadc(imch[:], X2i[kc * 128:(kc + 1) * 128, :], tag + "ai" + str(kc % 2))
            p.wait(nc.scalar, tag + "ar" + str(kc % 2), l1)
            p.wait(nc.scalar, tag + "ai" + str(kc % 2), l2)
            p.wait(nc.scalar, tag + "asq", p.tail(tag + "asq"))
            nc.scalar.activation(out=sq1[:], in_=rech[:], func=AF.Square)
            nc.scalar.drain()
            i2 = nc.scalar.activation(out=sq2[:], in_=imch[:], func=AF.Square)
            sqt = p.inc(i2, tag + "asq")
            p.wait(nc.vector, tag + "asq", sqt)
            nc.vector.drain()
            iv = nc.vector.tensor_add(vch[:], sq1[:], sq2[:])
            p.inc(iv, tag + "avv")
            p.wait(nc.scalar, tag + "avv", p.tail(tag + "avv"))
            isq = nc.scalar.activation(out=vch[:], in_=vch[:], func=AF.Sqrt)
            vt = p.inc(isq, tag + "avs")
            p.wait(nc.vector, tag + "avs", vt)
            nc.vector.drain()
            nc.vector.tensor_scalar(out=vch[:], in0=vch[:],
                                    scalar1=zab[:, kc:kc + 1],
                                    scalar2=1.0 / N2, op0=ALU.mult, op1=ALU.mult)
            nc.vector.drain()
            p.inc(nc.vector.reduce_sum(out=s1[:], in_=vch[:], axis=AX.X), tag + "as1")
            nc.vector.drain()
            nc.vector.tensor_scalar_mul(mu[:], s1[:], 1.0 / T)
            nc.vector.drain()
            ixc = nc.vector.tensor_single_scalar(out=xcn[:], in_=vch[:],
                                                 scalar=mu[:], op=ALU.subtract)
            xct = p.inc(ixc, tag + "axc")
            p.wait(nc.scalar, tag + "axc", xct)
            p.wait(nc.scalar, tag + "alw", p.tail(tag + "alw"))
            p.inc(nc.scalar.activation(out=sq1[:], in_=xcn[:], func=AF.Square,
                                       accum_out=s2[:]), tag + "asq2")
            p.wait(nc.scalar, tag + "asq2")
            nc.scalar.activation(out=lnv[:], in_=s2[:], func=AF.Ln,
                                 scale=1.0 / T, bias=p.constf(1e-5))
            nc.scalar.drain()
            ir = nc.scalar.activation(out=rstd[:], in_=lnv[:], func=AF.Exp,
                                      scale=-0.5)
            rt = p.inc(ir, tag + "arstd")
            p.wait(nc.vector, tag + "arstd", rt)
            nc.vector.drain()
            io = nc.vector.tensor_scalar_mul(vch[:], xcn[:], rstd[:])
            ot = p.inc(io, tag + "alw")
            p.wait(nc.gpsimd, tag + "alw", ot)
            p.inc(nc.gpsimd.dma_start(out=ocum_d[kc * 128:(kc + 1) * 128, :],
                                      in_=vch[:], accum_op=ALU.add), tag + "aoc", 16)
            p.wait(nc.vector, tag + "aoc", p.tail(tag + "aoc"))
    p._dma_sems.add(tag + "aoc")
    p.barrier()


def _host_consts():
    cosF, sinF, pi = rope_tables()
    ang = -2.0 * np.pi * np.outer(np.arange(T), np.arange(T)) / T
    FRE = np.cos(ang).astype(np.float32)
    FIM = np.sin(ang).astype(np.float32)
    return {
        "ident": np.eye(128, dtype=np.float32),
        "ident64s": np.ascontiguousarray(np.tile(np.eye(64, dtype=np.float32), (2, 1))),
        "rope_pi": pi, "rope_cos": cosF, "rope_sin": sinF,
        "fft_fre": FRE, "fft_fim": FIM, "fft_fimn": np.ascontiguousarray(-FIM),
    }


def _launch(build_fn, host, xs):
    p = Prog()
    nc = p.nc
    xd = nc.dram_tensor("x", [T, D], F32, kind="ExternalInput").ap()
    od = nc.dram_tensor("out", [T, D], F32, kind="ExternalOutput").ap()
    w = {k: nc.dram_tensor(k, list(np.asarray(a).shape), F32,
                           kind="ExternalInput").ap() for k, a in host.items()}
    build_fn(p, xd, od, w)
    p.es.close()
    hostc = dict(host)
    hostc["fconsts"] = p.const_array()
    in_maps = [dict(hostc, x=np.ascontiguousarray(xs[i])) for i in range(8)]
    res = run_bass_kernel_spmd(nc, in_maps, core_ids=list(range(8)))
    return [res.results[i]["out"] for i in range(8)]


FUSED = True


def kernel(**inputs):
    A = np.ascontiguousarray
    g = lambda k: np.asarray(inputs[k], dtype=np.float32)
    x = A(g("L0_attn_x"))
    C = _host_consts()

    h_l0f = {
        "ident": C["ident"], "ident64s": C["ident64s"],
        "We": A(g("L0_ffn_We").reshape(20 * 1024, 1024)),
        "be": A(g("L0_ffn_be").reshape(-1, 1)),
        "Wq": A(g("L0_ffn_Wq").reshape(6 * 1024, 1024)),
        "Wk": A(g("L0_ffn_Wk").reshape(6 * 1024, 1024)),
        "Wv": A(g("L0_ffn_Wv").reshape(6 * 1024, 1024)),
        "Wo": A(g("L0_ffn_Wo").reshape(6 * 1024, 1024)),
        "ew_bc": A(np.broadcast_to(g("L0_ffn_ew").reshape(1, 18), (128, 18))),
        "actw_bc": A(np.broadcast_to(g("L0_ffn_act_w").reshape(1, 6), (128, 6))),
        "lng_bc": A(np.broadcast_to(g("L0_ffn_ln_g"), (128, D))),
        "lnb_bc": A(np.broadcast_to(g("L0_ffn_ln_b"), (128, D))),
    }
    h_l0a = {
        "ident": C["ident"], "ident64s": C["ident64s"],
        "rope_pi": C["rope_pi"], "rope_cos": C["rope_cos"],
        "rope_sin": C["rope_sin"],
        "ln_gamma_bc": A(np.broadcast_to(g("L0_attn_ln_gamma"), (128, D))),
        "ln_beta_bc": A(np.broadcast_to(g("L0_attn_ln_beta"), (128, D))),
        "w_qkv": g("L0_attn_w_qkv"), "w_out": g("L0_attn_w_out"),
        "b_out": A(g("L0_attn_b_out").reshape(D, 1)),
    }
    h_l1f = {
        "ident": C["ident"], "gate_w": g("L1_ffn_gate_w"),
        "w1": A(g("L1_ffn_w1").reshape(8 * 1024, 512)),
        "w3": A(g("L1_ffn_w3").reshape(8 * 1024, 512)),
        "w2": A(g("L1_ffn_w2").reshape(8 * 512, 1024)),
    }
    h_l1a = {
        "ident": C["ident"],
        "in_projT": A(g("L1_attn_in_proj_w").T),
        "out_projT": A(g("L1_attn_out_proj_w").T),
        "f_conv_w": A(g("L1_attn_conv1d_w")[:, 0, :]),
        "f_conv_b": A(g("L1_attn_conv1d_bias").reshape(-1, 1)),
        "f_x_projT": A(g("L1_attn_x_proj_w").T),
        "f_dt_projT": A(g("L1_attn_dt_proj_w").T),
        "f_dt_b": A(g("L1_attn_dt_proj_b").reshape(-1, 1)),
        "f_A_log": g("L1_attn_A_log"),
        "f_Dp": A(g("L1_attn_D").reshape(-1, 1)),
        "b_conv_w": A(g("L1_attn_conv1db_w")[:, 0, :]),
        "b_conv_b": A(g("L1_attn_conv1db_bias").reshape(-1, 1)),
        "b_x_projT": A(g("L1_attn_x_projb_w").T),
        "b_dt_projT": A(g("L1_attn_dt_projb_w").T),
        "b_dt_b": A(g("L1_attn_dt_projb_b").reshape(-1, 1)),
        "b_A_log": A(g("L1_attn_A_b_log")),
        "b_Dp": A(g("L1_attn_D_b").reshape(-1, 1)),
        "fft_fre": C["fft_fre"], "fft_fim": C["fft_fim"],
        "fft_fimn": C["fft_fimn"],
        "fft_b": A(g("L1_attn_fft_conv_b").reshape(-1, 1)),
    }
    fcw = g("L1_attn_fft_conv_w")
    for t_ in range(5):
        h_l1a["fftw_t%d" % t_] = A(fcw[:, :, t_].T)

    xs = [x[i] for i in range(8)]
    if not FUSED:
        xs = _launch(lambda p, xd, od, w: sub_l0ffn(p, xd, od, w, tag="lf"),
                     h_l0f, xs)
        xs = _launch(sub_l0attn, h_l0a, xs)
        xs = _launch(lambda p, xd, od, w: sub_l0ffn(p, xd, od, w, tag="lf"),
                     h_l0f, xs)
        xs = _launch(lambda p, xd, od, w: sub_l1ffn(p, xd, od, w), h_l1f, xs)
        xs = _launch(lambda p, xd, od, w: sub_l1attn(p, xd, od, w), h_l1a, xs)
        xs = _launch(lambda p, xd, od, w: sub_l1ffn(p, xd, od, w), h_l1f, xs)
        return np.stack([np.asarray(o, dtype=np.float32) for o in xs])

    host = {}
    wmaps = {}
    for pre, h in (("A", h_l0f), ("B", h_l0a), ("C", h_l1f), ("D", h_l1a)):
        wmaps[pre] = {}
        for k, a in h.items():
            a = np.asarray(a)
            key = pre + "_" + k
            if k in ("ident", "ident64s", "rope_pi", "rope_cos", "rope_sin",
                     "fft_fre", "fft_fim", "fft_fimn"):
                key = "K_" + k
            if key not in host:
                host[key] = a
            wmaps[pre][k] = key

    p = Prog()
    nc = p.nc
    xd = nc.dram_tensor("x", [T, D], F32, kind="ExternalInput").ap()
    od = nc.dram_tensor("out", [T, D], F32, kind="ExternalOutput").ap()
    dr = {k: nc.dram_tensor(k, list(np.asarray(a).shape), F32,
                            kind="ExternalInput").ap() for k, a in host.items()}
    s0 = nc.dram_tensor("scr0", [T, D], F32, kind="Internal").ap()
    s1 = nc.dram_tensor("scr1", [T, D], F32, kind="Internal").ap()

    def wview(pre):
        return {k: dr[key] for k, key in wmaps[pre].items()}
    sub_l0ffn(p, xd, s0, wview("A"), tag="lf")
    sub_l0attn(p, s0, s1, wview("B"))
    sub_l0ffn(p, s1, s0, wview("A"), tag="lf")
    sub_l1ffn(p, s0, s1, wview("C"))
    sub_l1attn(p, s1, s0, wview("D"))
    sub_l1ffn(p, s0, od, wview("C"))
    p.es.close()
    hostc = dict(host)
    hostc["fconsts"] = p.const_array()
    in_maps = [dict(hostc, x=np.ascontiguousarray(x[i])) for i in range(8)]
    res = run_bass_kernel_spmd(nc, in_maps, core_ids=list(range(8)))
    return np.stack([np.asarray(res.results[i]["out"], dtype=np.float32)
                     for i in range(8)])


# version 05
# speedup vs baseline: 1.2060x; 1.0233x over previous
import numpy as np
from contextlib import ExitStack, contextmanager

import concourse.bass as bass
import concourse.mybir as mybir
from concourse.bass_utils import run_bass_kernel_spmd

F32 = mybir.dt.float32
BF16 = mybir.dt.bfloat16
AF = mybir.ActivationFunctionType
ALU = mybir.AluOpType
AX = mybir.AxisListType

T = 2048
D = 1024
DC = D // 128
NT = T // 128
NQ = T // 512


NCONST = 64


class Prog:
    def __init__(self, detect_races=True):
        self.nc = bass.Bass("TRN2", target_bir_lowering=False,
                            detect_race_conditions=detect_races)
        self.es = ExitStack()
        self._stack = [self.es]
        self._sems = {}
        self._dma_sems = set()
        self._uid = 0
        self._cvals = []
        self._cdram = self.nc.dram_tensor("fconsts", [128, NCONST], F32,
                                          kind="ExternalInput").ap()
        self._csb = self.sbuf([128, NCONST], F32, "fconsts")
        self.load(self._csb[:], self._cdram[:], "cst")
        for e in (self.nc.tensor, self.nc.vector, self.nc.scalar,
                  self.nc.gpsimd, self.nc.sync):
            self.wait(e, "cst", 16)

    def constf(self, val):
        val = float(val)
        if val not in self._cvals:
            assert len(self._cvals) < NCONST
            self._cvals.append(val)
            idx = self._cvals.index(val)
            self.nc.const_aps.register(F32, val, self._csb[:, idx:idx + 1])
        return val

    def const_array(self):
        arr = np.zeros((128, NCONST), np.float32)
        arr[:, :len(self._cvals)] = np.asarray(self._cvals, np.float32)[None, :]
        return arr

    def name(self, prefix):
        self._uid += 1
        return f"{prefix}{self._uid}"

    def sbuf(self, shape, dt, name="t"):
        return self._stack[-1].enter_context(
            self.nc.sbuf_tensor(self.name(name), list(shape), dt))

    def psum(self, shape, dt=F32, name="p"):
        return self._stack[-1].enter_context(
            self.nc.psum_tensor(self.name(name), list(shape), dt))

    @contextmanager
    def scope(self):
        es = ExitStack()
        self._stack.append(es)
        try:
            yield
        finally:
            self.barrier()
            es.close()
            self._stack.pop()

    def _sem(self, sname):
        if sname not in self._sems:
            h = self.es.enter_context(self.nc.semaphore(self.name("s" + sname)))
            self._sems[sname] = [h, 0]
        return self._sems[sname]

    def inc(self, instr, sname, n=1):
        s = self._sem(sname)
        s[1] += n
        instr.then_inc(s[0], n)
        return s[1]

    def tail(self, sname):
        return self._sem(sname)[1]

    def wait(self, eng, sname, val=None):
        s = self._sem(sname)
        v = s[1] if val is None else val
        if v > 0:
            eng.wait_ge(s[0], v)

    def barrier(self):
        nc = self.nc
        engs = [nc.tensor, nc.vector, nc.scalar, nc.sync, nc.gpsimd]
        s = self._sem("bar")
        base = s[1]
        dmas = [(self._sems[n][0], self._sems[n][1])
                for n in self._dma_sems
                if n in self._sems and self._sems[n][1] > 0]
        for e in engs:
            for h, v in dmas:
                e.wait_ge(h, v)
            e.drain()
            self.inc(e.nop(), "bar", 1)
        for e in engs:
            e.wait_ge(s[0], base + len(engs))

    def load(self, out_ap, in_ap, sname="ld"):
        self._dma_sems.add(sname)
        return self.inc(self.nc.sync.dma_start(out=out_ap, in_=in_ap), sname, 16)

    def loadc(self, out_ap, in_ap, sname="ldg"):
        self._dma_sems.add(sname)
        return self.inc(self.nc.gpsimd.dma_start(out=out_ap, in_=in_ap), sname, 16)

    def store(self, out_ap, in_ap, sname="st"):
        self._dma_sems.add(sname)
        return self.inc(self.nc.sync.dma_start(out=out_ap, in_=in_ap), sname, 16)


def load_w(p, w_dram, K, M, sname="ldg"):
    wsb = p.sbuf([128, (K // 128) * M], BF16, "w")
    tok = None
    for kc in range(K // 128):
        tok = p.loadc(wsb[:, kc * M:(kc + 1) * M],
                      w_dram[kc * 128:(kc + 1) * 128, :], sname)
    return wsb, tok


def w_tile(wsb, M, kc, j):
    return wsb[:, kc * M + j * 128: kc * M + (j + 1) * 128]


def ln_to_xT(p, x_dram, xT, identb, eps=1e-5, gain=None, bias_=None,
             nchunk=DC, tag="ln"):
    nc = p.nc
    tag = "ln"
    s_ld, s_mu, s_xc, s_t, s_rstd, s_xn, s_mm, s_ev = (
        tag + k for k in ("ld", "mu", "xc", "t", "rstd", "xn", "mm", "ev"))
    B = {s: p.tail(s) for s in (s_xc, s_t, s_rstd, s_xn, s_mm, s_ev,
                                s_xn + "m", s_rstd + "w")}
    p._lnscope = p.scope()
    p._lnscope.__enter__()
    xt = [p.sbuf([128, D], F32, "lnx"), p.sbuf([128, D], F32, "lnx")]
    xc = p.sbuf([128, D], F32, "lnxc")
    tmid = p.sbuf([128, D], F32, "lnmid")
    xnb = p.sbuf([128, D], BF16, "lnxn")
    sq = p.sbuf([128, D], F32, "lnsq")
    s1 = p.sbuf([128, 1], F32, "s1")
    s2 = p.sbuf([128, 1], F32, "s2")
    mu = p.sbuf([128, 1], F32, "mu")
    lnv = p.sbuf([128, 1], F32, "lnv")
    rstd = p.sbuf([128, 1], F32, "rstd")
    tp = p.psum([128, D], BF16, "tp")
    npass = 2 if gain is not None else 1

    def one_pass(src_buf, dst_buf, k):
        nc.vector.drain()
        p.inc(nc.vector.reduce_sum(out=s1[:], in_=src_buf[:], axis=AX.X), s_mu)
        nc.vector.drain()
        nc.vector.tensor_scalar_mul(mu[:], s1[:], 1.0 / D)
        nc.vector.drain()
        p.inc(nc.vector.tensor_single_scalar(out=xc[:], in_=src_buf[:],
                                             scalar=mu[:], op=ALU.subtract), s_xc)
        p.wait(nc.scalar, s_xc, B[s_xc] + k + 1)
        p.wait(nc.scalar, s_rstd + "w", B[s_rstd + "w"] + k)
        p.inc(nc.scalar.activation(out=sq[:], in_=xc[:], func=AF.Square,
                                   accum_out=s2[:]), s_t)
        p.wait(nc.scalar, s_t)
        nc.scalar.activation(out=lnv[:], in_=s2[:], func=AF.Ln,
                             scale=1.0 / D, bias=p.constf(eps))
        nc.scalar.drain()
        p.inc(nc.scalar.activation(out=rstd[:], in_=lnv[:], func=AF.Exp,
                                   scale=-0.5), s_rstd)
        p.wait(nc.vector, s_rstd, B[s_rstd] + k + 1)
        nc.vector.drain()
        i = nc.vector.tensor_scalar_mul(dst_buf[:], xc[:], rstd[:])
        p.inc(i, s_rstd + "w")
        return i

    for tt in range(NT):
        buf = xt[tt % 2]
        par = s_ld + str(tt % 2)
        p.wait(nc.sync, s_xn, B[s_xn] + tt - 1)
        lt = p.load(buf[:], x_dram[tt * 128:(tt + 1) * 128, :], par)
        p.wait(nc.vector, par, lt)
        p.wait(nc.vector, s_mm, B[s_mm] + tt * nchunk)
        if gain is None:
            i5 = one_pass(buf, tmid, tt)
            nc.vector.drain()
            i5 = nc.vector.tensor_copy(out=xnb[:], in_=tmid[:])
        else:
            one_pass(buf, tmid, 2 * tt)
            one_pass(tmid, xc, 2 * tt + 1)
            nc.vector.drain()
            nc.vector.tensor_tensor(out=xnb[:], in0=xc[:], in1=gain[:],
                                    op=ALU.mult)
            nc.vector.drain()
            i5 = nc.vector.tensor_tensor(out=xnb[:], in0=xnb[:], in1=bias_[:],
                                         op=ALU.add)
        p.inc(i5, s_xn)
        p.wait(nc.tensor, s_xn, B[s_xn] + tt + 1)
        p.wait(nc.tensor, s_ev, B[s_ev] + tt)
        it = None
        for j in range(nchunk):
            it = nc.tensor.transpose(tp[:, j * 128:(j + 1) * 128],
                                     xnb[:, j * 128:(j + 1) * 128], identb[:])
        p.inc(it, s_mm, nchunk)
        p.wait(nc.vector, s_mm, B[s_mm] + (tt + 1) * nchunk)
        dst = xT[:, :].rearrange("p (c t) -> p c t", c=nchunk)[
            :, :, tt * 128:(tt + 1) * 128]
        srcv = tp[:, :].rearrange("p (c t) -> p c t", c=nchunk)
        p.inc(nc.vector.tensor_copy(out=dst, in_=srcv), s_ev)
    p._lnscope.__exit__(None, None, None)
    return p.tail(s_ev)


def mm_ff(p, wsb, K, M, rhsT, evac, rhs_T=T, tag="mm", pspool=None):
    nc = p.nc
    KC = K // 128
    nq = rhs_T // 512
    s_mm, s_ev = tag + "m", tag + "e"
    if pspool is None:
        NSL = 4
        ps = [p.psum([128, 512], F32, "mmps") for _ in range(NSL)]
        slot_tok = [0] * NSL
    else:
        ps, slot_tok = pspool
        NSL = len(ps)
    si = 0
    for j in range(M // 128):
        for n in range(nq):
            if slot_tok[si]:
                p.wait(nc.tensor, s_ev, slot_tok[si])
            im = None
            for kc in range(KC):
                im = nc.tensor.matmul(
                    ps[si][:], w_tile(wsb, M, kc, j),
                    rhsT[:, kc * rhs_T + n * 512:kc * rhs_T + (n + 1) * 512],
                    start=(kc == 0), stop=(kc == KC - 1))
            mtok = p.inc(im, s_mm)
            ie = evac(j, n, ps[si], mtok)
            slot_tok[si] = p.inc(ie, s_ev)
            si = (si + 1) % NSL
    return p.tail(s_ev)


def make_att_bufs(p, dh, alias=None):
    nc = p.nc
    KCH = T // 128
    hpp = 128 // dh
    if alias is not None:
        assert hpp == 1
        pt = [alias[:, 0:KCH * 512]]
        vp = [alias[:, KCH * 512:KCH * (512 + dh)]]
    else:
        pt = [p.sbuf([128, KCH * 512], BF16, "PT") for _ in range(hpp)]
        vp = [p.sbuf([128, KCH * dh], BF16, "Vp") for _ in range(hpp)]
    d = {
        "PTb": pt,
        "Vp": vp,
        "vtp": p.psum([128, 512], BF16, "vtp"),
        "sps": [p.psum([128, 512], F32, "sps"), p.psum([128, 512], F32, "sps")],
        "cps": p.psum([128, 512], F32, "cps"),
        "sm": [p.psum([1, 512], F32, "sm") for _ in range(hpp)],
        "rs": [p.sbuf([1, 512], F32, "rs") for _ in range(hpp)],
        "rsb": [p.sbuf([1, 512], BF16, "rsb") for _ in range(hpp)],
        "bcp": p.psum([128, 512], F32, "bcp"),
        "bcs": p.sbuf([128, 512], F32, "bcs"),
        "ones1": p.sbuf([1, 128], BF16, "ones1"),
        "onesk": p.sbuf([128, 1], BF16, "onesk"),
    }
    nc.vector.memset(d["ones1"][:], 1.0)
    p.inc(nc.vector.memset(d["onesk"][:], 1.0), "atve")
    p.wait(nc.tensor, "atve", p.tail("atve"))
    return d


def attention_c(p, qT, kT, vT, ctxT_out, H, dh, scale, identb, identq=None,
                shift=12.0, tag=None, shared=None):
    tag = "at"
    """identq: [128, dh] identity stack (eye(dh) tiled) for dh<128 transposes."""
    """Scheme-C attention. qT/kT/vT/ctxT_out: sbuf bf16 [128, (H*dh/128)*T]."""
    nc = p.nc
    KCH = T // 128
    hpp = 128 // dh
    s_vt, s_ve, s_sm, s_pe, s_cm, s_su, s_rs, s_bm, s_be, s_ce = (
        tag + k for k in ("vt", "ve", "sm", "pe", "cm", "su", "rs", "bm", "be",
                          "ce"))
    own = shared is None
    if own:
        _sc = p.scope()
        _sc.__enter__()
        shared = make_att_bufs(p, dh)
    if True:
        PTb, Vp, vtp, sps, cps, sm, rs, rsb, bcp, bcs, ones1, onesk = (
            shared["PTb"], shared["Vp"], shared["vtp"], shared["sps"],
            shared["cps"], shared["sm"], shared["rs"], shared["rsb"],
            shared["bcp"], shared["bcs"], shared["ones1"], shared["onesk"])

        vslots = 512 // dh
        for hp in range(H // hpp):
            heads = [hp * hpp + i for i in range(hpp)]
            for i, h in enumerate(heads):
                c0, r0 = (h * dh) // 128, (h * dh) % 128
                for g in range(KCH // vslots):
                    p.wait(nc.tensor, s_ve, p.tail(s_ve))
                    it = None
                    for s in range(vslots):
                        kc = g * vslots + s
                        idq = identb if dh == 128 else identq
                        it = nc.tensor.transpose(
                            vtp[:, s * dh:(s + 1) * dh],
                            vT[r0:r0 + dh,
                               c0 * T + kc * 128: c0 * T + (kc + 1) * 128],
                            idq[r0:r0 + dh, 0:dh])
                    vt = p.inc(it, s_vt)
                    p.wait(nc.vector, s_vt, vt)
                    p.inc(nc.vector.tensor_copy(
                        out=Vp[i][:, g * 512:(g + 1) * 512], in_=vtp[:]), s_ve)
            for n in range(NQ):
                for i, h in enumerate(heads):
                    c0, r0 = (h * dh) // 128, (h * dh) % 128
                    p.wait(nc.scalar, s_su, p.tail(s_su))
                    si = 0
                    for kc in range(KCH):
                        if p.tail(s_pe) >= 2:
                            p.wait(nc.tensor, s_pe, p.tail(s_pe) - 1)
                        im = nc.tensor.matmul(
                            sps[si][:],
                            kT[r0:r0 + dh,
                               c0 * T + kc * 128: c0 * T + (kc + 1) * 128],
                            qT[r0:r0 + dh,
                               c0 * T + n * 512: c0 * T + (n + 1) * 512],
                            start=True, stop=True)
                        tm = p.inc(im, s_sm)
                        p.wait(nc.scalar, s_sm, tm)
                        p.inc(nc.scalar.activation(
                            out=PTb[i][:, kc * 512:(kc + 1) * 512],
                            in_=sps[si][:], func=AF.Exp, scale=float(scale),
                            bias=p.constf(-shift)), s_pe)
                        si ^= 1
                    p.wait(nc.tensor, s_pe, p.tail(s_pe))
                    p.wait(nc.tensor, s_ce, p.tail(s_ce))
                    im = None
                    for kc in range(KCH):
                        im = nc.tensor.matmul(
                            cps[i * dh:(i + 1) * dh, :],
                            Vp[i][:, kc * dh:(kc + 1) * dh],
                            PTb[i][:, kc * 512:(kc + 1) * 512],
                            start=(kc == 0), stop=(kc == KCH - 1))
                    p.inc(im, s_cm)
                    p.wait(nc.tensor, s_rs, p.tail(s_rs))
                    im = None
                    for kc in range(KCH):
                        im = nc.tensor.matmul(
                            sm[i][:], onesk[:, 0:1],
                            PTb[i][:, kc * 512:(kc + 1) * 512],
                            start=(kc == 0), stop=(kc == KCH - 1))
                    sut = p.inc(im, s_su)
                    p.wait(nc.vector, s_su, sut)
                    p.wait(nc.vector, s_bm, p.tail(s_bm))
                    rt = p.inc(nc.vector.reciprocal(
                        out=rs[i][:], in_=sm[i][:]), s_rs)
                    p.wait(nc.scalar, s_rs, rt)
                    p.wait(nc.scalar, s_bm, p.tail(s_bm))
                    p.inc(nc.scalar.activation(out=rsb[i][:], in_=rs[i][:],
                                               func=AF.Copy), s_rs + "c")
                p.wait(nc.tensor, s_rs + "c", p.tail(s_rs + "c"))
                p.wait(nc.tensor, s_be, p.tail(s_be))
                ib = None
                for i in range(hpp):
                    ib = nc.tensor.matmul(bcp[i * dh:(i + 1) * dh, :],
                                          ones1[0:1, 0:dh], rsb[i][:],
                                          start=True, stop=True)
                bt = p.inc(ib, s_bm)
                p.wait(nc.vector, s_bm, bt)
                p.inc(nc.vector.tensor_copy(out=bcs[:], in_=bcp[:]), s_be)
                nc.vector.drain()
                p.wait(nc.vector, s_cm, p.tail(s_cm))
                c0 = (heads[0] * dh) // 128
                p.inc(nc.vector.tensor_tensor(
                    out=ctxT_out[:, c0 * T + n * 512: c0 * T + (n + 1) * 512],
                    in0=cps[:], in1=bcs[:], op=ALU.mult), s_ce)
    if own:
        _sc.__exit__(None, None, None)


def residual_store(p, srcT, x_dram, out_dram, identf, scale_src=1.0, tag="res"):
    nc = p.nc
    s_ld, s_mm, s_ev, s_st = "lnld0", "lnmm", "lnev", "lnst"
    B = {s: p.tail(s) for s in (s_mm, s_ev, s_st)}
    rp = p.psum([128, 1024], F32, "rp")
    xtile = p.sbuf([128, D], F32, "rx")
    otile = p.sbuf([128, D], F32, "ro")
    for tt in range(NT):
        p.wait(nc.sync, s_ev, B[s_ev] + tt)
        lt = p.load(xtile[:], x_dram[tt * 128:(tt + 1) * 128, :], s_ld)
        p.wait(nc.tensor, s_ev, B[s_ev] + tt)
        it = None
        for j in range(DC):
            it = nc.tensor.transpose(
                rp[:, j * 128:(j + 1) * 128],
                srcT[:, j * T + tt * 128: j * T + (tt + 1) * 128], identf[:])
        mt = p.inc(it, s_mm)
        p.wait(nc.vector, s_mm, mt)
        p.wait(nc.vector, s_ld, lt)
        p.wait(nc.vector, s_st, B[s_st] + tt * 16)
        ia = nc.vector.scalar_tensor_tensor(out=otile[:], in0=rp[:],
                                            scalar=float(scale_src), in1=xtile[:],
                                            op0=ALU.mult, op1=ALU.add)
        ev = p.inc(ia, s_ev)
        p.wait(nc.sync, s_ev, ev)
        p.store(out_dram[tt * 128:(tt + 1) * 128, :], otile[:], s_st)
    p.wait(nc.sync, s_st)


def sub_l0attn(p, x_dram, out_dram, w):
    nc = p.nc
    H, dh = 16, 64
    with p.scope():
        identb = p.sbuf([128, 128], BF16, "idb")
        identf = p.sbuf([128, 128], F32, "idf")
        identq = p.sbuf([128, 64], BF16, "idq")
        p.loadc(identb[:], w["ident"][:])
        p.load(identf[:], w["ident"][:])
        p.loadc(identq[:], w["ident64s"][:])
        p.barrier()

        ctxT = p.sbuf([128, 8 * T], BF16, "ctxT")
        with p.scope():
            qkvT = p.sbuf([128, 24 * T], BF16, "qkvT")
            with p.scope():
                xT = p.sbuf([128, DC * T], BF16, "xT")
                gainb = p.sbuf([128, D], BF16, "gain")
                biasb = p.sbuf([128, D], BF16, "bias")
                p.loadc(gainb[:], w["ln_gamma_bc"][:])
                p.loadc(biasb[:], w["ln_beta_bc"][:])
                p.barrier()
                ln_to_xT(p, x_dram, xT, identb, eps=1e-5, gain=gainb, bias_=biasb)
                p.barrier()

                def evac_qkv(half):
                    def f(j, n, ps, mtok):
                        jj = j + 12 * half
                        p.wait(nc.scalar, "mmm", mtok)
                        return nc.scalar.activation(
                            out=qkvT[:, jj * T + n * 512:jj * T + (n + 1) * 512],
                            in_=ps[:], func=AF.Copy)
                    return f
                for half in range(2):
                    with p.scope():
                        wq, _ = load_w(p, w["w_qkv"][:, half * 1536:(half + 1) * 1536],
                                       1024, 1536)
                        p.barrier()
                        mm_ff(p, wq, 1024, 1536, xT, evac_qkv(half))

            qT = qkvT[:, 0 * T: 8 * T]
            kT = qkvT[:, 8 * T: 16 * T]
            vT = qkvT[:, 16 * T: 24 * T]

            with p.scope():
                pib = p.sbuf([128, 128], BF16, "pib")
                cosF = p.sbuf([128, T], BF16, "cosF")
                sinF = p.sbuf([128, T], BF16, "sinF")
                p.loadc(pib[:], w["rope_pi"][:])
                p.loadc(cosF[:], w["rope_cos"][:])
                p.loadc(sinF[:], w["rope_sin"][:])
                p.barrier()
                swp = p.psum([128, 512], F32, "swp")
                sws = p.sbuf([128, 512], BF16, "sws")
                tmp = p.sbuf([128, 512], BF16, "ropet")
                for base in (0, 8 * T):
                    for c in range(8):
                        for n in range(NQ):
                            sl = slice(base + c * T + n * 512,
                                       base + c * T + (n + 1) * 512)
                            p.wait(nc.tensor, "rpsw", p.tail("rpsw"))
                            im = nc.tensor.matmul(swp[:], pib[:], qkvT[:, sl],
                                                  start=True, stop=True)
                            mt = p.inc(im, "rpmm")
                            p.wait(nc.scalar, "rpmm", mt)
                            p.wait(nc.scalar, "rpev", p.tail("rpev"))
                            isw = nc.scalar.activation(out=sws[:], in_=swp[:],
                                                       func=AF.Copy)
                            swt = p.inc(isw, "rpsw")
                            p.wait(nc.vector, "rpsw", swt)
                            nc.vector.tensor_tensor(
                                out=tmp[:], in0=sws[:],
                                in1=sinF[:, n * 512:(n + 1) * 512], op=ALU.mult)
                            nc.vector.tensor_tensor(
                                out=qkvT[:, sl], in0=qkvT[:, sl],
                                in1=cosF[:, n * 512:(n + 1) * 512], op=ALU.mult)
                            nc.vector.drain()
                            p.inc(nc.vector.tensor_tensor(
                                out=qkvT[:, sl], in0=qkvT[:, sl], in1=tmp[:],
                                op=ALU.add), "rpev")

            attention_c(p, qT, kT, vT, ctxT, H, dh, dh ** -0.5, identb, identq)

        with p.scope():
            oT = p.sbuf([128, DC * T], F32, "oT")
            wo, _ = load_w(p, w["w_out"], 1024, 1024)
            bout = p.sbuf([128, DC], F32, "bout")
            for j in range(DC):
                p.load(bout[:, j:j + 1], w["b_out"][j * 128:(j + 1) * 128, :], "ldb")
            p.barrier()

            def evac_o(j, n, ps, mtok):
                p.wait(nc.vector, "mmm", mtok)
                p.wait(nc.vector, "ldb", p.tail("ldb"))
                return nc.vector.tensor_scalar_add(
                    oT[:, j * T + n * 512:j * T + (n + 1) * 512],
                    ps[:], bout[:, j:j + 1])
            mm_ff(p, wo, 1024, 1024, ctxT, evac_o)
            p.barrier()
            residual_store(p, oT, x_dram, out_dram, identf, scale_src=1.0)


def rope_tables(n=T, head_dim=64):
    inv_freq = 1.0 / 10000.0 ** (np.arange(0, head_dim, 2, dtype=np.float64) / head_dim)
    pos = np.arange(n, dtype=np.float64)
    ang = pos[:, None] * inv_freq[None, :]
    sin, cos = np.sin(ang), np.cos(ang)
    cosF = np.zeros((128, n), np.float32)
    sinF = np.zeros((128, n), np.float32)
    for r in range(128):
        i = r % 64
        f = i % 32
        cosF[r] = cos[:, f]
        sinF[r] = sin[:, f] if i >= 32 else -sin[:, f]
    pi = np.zeros((128, 128), np.float32)
    for r in range(128):
        i = r % 64
        partner = r + 32 if i < 32 else r - 32
        pi[partner, r] = 1.0
    return cosF, sinF, pi


def l0attn_consts():
    cosF, sinF, pi = rope_tables()
    return {
        "ident": np.eye(128, dtype=np.float32),
        "ident64s": np.ascontiguousarray(np.tile(np.eye(64, dtype=np.float32), (2, 1))),
        "rope_pi": pi, "rope_cos": cosF, "rope_sin": sinF,
    }


L0F_EDGES = {0: (0, 0, 0), 1: (2, 1, 2), 2: (5, 3, 4),
             3: (9, 6, 8), 4: (14, 10, 12), 5: (19, 15, 17)}
L0F_INS = {0: (-1, -1, -1), 1: (0, -1, 0), 2: (1, -1, 0),
           3: (2, -1, 1), 4: (3, -1, 1), 5: (4, 0, 2)}


def ln_residual_store(p, srcT, x_dram, out_dram, identf, gain, bias_,
                      eps, scale_src, tag="lrs", src_bf16=False, identb=None):
    nc = p.nc
    s_ld, s_mm, s_xc, s_rstd, s_ev, s_st = (
        "lnld0", "lnmm", "lnxc", "lnrstd", "lnev", "lnst")
    B = {s: p.tail(s) for s in (s_mm, s_xc, s_rstd, s_ev, s_st,
                                s_rstd + "w")}
    rp_ps = p.psum([128, 1024], BF16 if src_bf16 else F32, "rp")
    rpf = p.sbuf([128, D], F32, "rpf") if src_bf16 else None
    xtile = p.sbuf([128, D], F32, "rx")
    xc = p.sbuf([128, D], F32, "rxc")
    sq = p.sbuf([128, D], F32, "rsq")
    otile = p.sbuf([128, D], F32, "ro")
    s1 = p.sbuf([128, 1], F32, "s1")
    s2 = p.sbuf([128, 1], F32, "s2")
    mu = p.sbuf([128, 1], F32, "mu")
    lnv = p.sbuf([128, 1], F32, "lnv")
    rstd = p.sbuf([128, 1], F32, "rstd")
    for tt in range(NT):
        p.wait(nc.sync, s_ev, B[s_ev] + tt)
        lt = p.load(xtile[:], x_dram[tt * 128:(tt + 1) * 128, :], s_ld)
        p.wait(nc.tensor, s_xc, B[s_xc] + tt)
        it = None
        for j in range(DC):
            it = nc.tensor.transpose(
                rp_ps[:, j * 128:(j + 1) * 128],
                srcT[:, j * T + tt * 128: j * T + (tt + 1) * 128],
                identb[:] if src_bf16 else identf[:])
        mt = p.inc(it, s_mm)
        p.wait(nc.vector, s_mm, mt)
        if src_bf16:
            nc.vector.tensor_copy(out=rpf[:], in_=rp_ps[:])
            nc.vector.drain()
            rp = rpf
        else:
            rp = rp_ps
        nc.vector.reduce_sum(out=s1[:], in_=rp[:], axis=AX.X)
        nc.vector.drain()
        nc.vector.tensor_scalar_mul(mu[:], s1[:], 1.0 / D)
        nc.vector.drain()
        p.inc(nc.vector.tensor_single_scalar(out=xc[:], in_=rp[:],
                                             scalar=mu[:], op=ALU.subtract), s_xc)
        p.wait(nc.scalar, s_xc, B[s_xc] + tt + 1)
        p.wait(nc.scalar, s_rstd + "w", B[s_rstd + "w"] + tt)
        p.inc(nc.scalar.activation(out=sq[:], in_=xc[:], func=AF.Square,
                                   accum_out=s2[:]), "lnt")
        p.wait(nc.scalar, "lnt")
        nc.scalar.activation(out=lnv[:], in_=s2[:], func=AF.Ln,
                             scale=1.0 / D, bias=p.constf(eps))
        nc.scalar.drain()
        p.inc(nc.scalar.activation(out=rstd[:], in_=lnv[:], func=AF.Exp,
                                   scale=-0.5), s_rstd)
        p.wait(nc.vector, s_rstd, B[s_rstd] + tt + 1)
        p.wait(nc.vector, s_ld, lt)
        p.wait(nc.vector, s_st, B[s_st] + tt * 16)
        nc.vector.drain()
        nc.vector.tensor_scalar_mul(otile[:], xc[:], rstd[:])
        nc.vector.drain()
        nc.vector.tensor_tensor(out=otile[:], in0=otile[:], in1=gain[:], op=ALU.mult)
        nc.vector.drain()
        p.inc(nc.vector.tensor_tensor(out=otile[:], in0=otile[:], in1=bias_[:],
                                      op=ALU.add), s_rstd + "w")
        nc.vector.drain()
        iv = nc.vector.scalar_tensor_tensor(out=otile[:], in0=otile[:],
                                            scalar=float(scale_src), in1=xtile[:],
                                            op0=ALU.mult, op1=ALU.add)
        ev = p.inc(iv, s_ev)
        p.wait(nc.sync, s_ev, ev)
        p.store(out_dram[tt * 128:(tt + 1) * 128, :], otile[:], s_st)
    p.wait(nc.sync, s_st)


def sub_l0ffn(p, x_dram, out_dram, w, tag):
    nc = p.nc
    H, dh = 8, 128

    def Wsl(name, e):
        return w[name][e * 1024:(e + 1) * 1024, :]

    with p.scope():
        identb = p.sbuf([128, 128], BF16, "idb")
        identf = p.sbuf([128, 128], F32, "idf")
        ewb = p.sbuf([128, 18], F32, "ewb")
        awb = p.sbuf([128, 6], F32, "awb")
        p.loadc(identb[:], w["ident"][:])
        p.load(identf[:], w["ident"][:])
        p.load(ewb[:], w["ew_bc"][:])
        p.load(awb[:], w["actw_bc"][:])
        nodes_d = {c: nc.dram_tensor(p.name(f"{tag}nd{c}"), [128, DC * T], BF16,
                                     kind="Internal").ap()
                   for c in range(-1, 6)}
        p.barrier()

        with p.scope():
            xT = p.sbuf([128, DC * T], BF16, "xT")
            ln_to_xT(p, x_dram, xT, identb, eps=1e-5, gain=None, tag=tag + "ln")
            p.barrier()
            p.store(nodes_d[-1][:], xT[:], tag + "nst")
            p.wait(nc.sync, tag + "nst")
        p.barrier()

        MK = p.sbuf([128, DC * T], BF16, "sMK")

        with p.scope():
            TMP = p.sbuf([128, DC * T], BF16, "sTMP")
            Q = p.sbuf([128, DC * T], BF16, "sQ")
            K = p.sbuf([128, DC * T], BF16, "sK")
            MQ = p.sbuf([128, DC * T], BF16, "sMQ")
            MV = TMP
            CTX = MQ
            bufs = {"Q": Q, "K": K, "MQ": MQ, "MK": MK, "MV": MV, "CTX": CTX}
            wpool = [p.sbuf([128, DC * 1024], BF16, "wpool") for _ in range(2)]
            bpool = [p.sbuf([128, DC], F32, "bpool") for _ in range(2)]
            tsc = p.sbuf([128, 512], BF16, "tsc")
            attb = make_att_bufs(p, dh, alias=K)
            mmpool = ([p.psum([128, 512], F32, "mmp") for _ in range(2)], [0, 0])

            stages = []
            for c in range(6):
                eq, ek, ev = L0F_EDGES[c]
                iq, ik, iv_ = L0F_INS[c]
                ew_q, ew_k, ew_v = 3 * c + 0, 3 * c + 1, 3 * c + 2
                qsrc = ("tmp", iq) if c == 0 else ("buf", "MK")
                stages.append(dict(kind="edge", wd=Wsl("We", eq), be=eq,
                                   src=qsrc, dest="Q", node=c))
                stages.append(dict(kind="proj", wd=Wsl("Wq", c), src=("buf", "Q"),
                                   dest="MQ", ew=ew_q, node=c))
                if (ek, ik) == (eq, iq):
                    ksrc = ("buf", "Q")
                else:
                    stages.append(dict(kind="edge", wd=Wsl("We", ek), be=ek,
                                       src=("tmp", ik), dest="K", node=c))
                    ksrc = ("buf", "K")
                stages.append(dict(kind="proj", wd=Wsl("Wk", c), src=ksrc,
                                   dest="MK", ew=ew_k, node=c))
                if (ev, iv_) == (eq, iq):
                    vsrc = ("buf", "Q")
                elif (ev, iv_) == (ek, ik):
                    vsrc = ksrc
                else:
                    stages.append(dict(kind="edge", wd=Wsl("We", ev), be=ev,
                                       src=("tmp", iv_), dest="K", node=c))
                    vsrc = ("buf", "K")
                stages.append(dict(kind="proj", wd=Wsl("Wv", c), src=vsrc,
                                   dest="MV", ew=ew_v, node=c))
                stages.append(dict(kind="att", node=c))
                stages.append(dict(kind="wo", wd=Wsl("Wo", c), src=("buf", "CTX"),
                                   dest="MK", node=c, ew_q=ew_q, last=(c == 5)))
            wlist = [st for st in stages if st["kind"] != "att"]
            for i, st in enumerate(wlist):
                st["wslot"] = i % 2
                st["wpos"] = i
            tmp_stages = [i for i, st in enumerate(stages)
                          if st.get("src", ("",))[0] == "tmp"]
            store_tok = {-1: p.tail(tag + "nst")}

            def issue_wload(st, wfree_tok):
                sl = st["wslot"]
                if wfree_tok:
                    p.wait(nc.gpsimd, "mmm", wfree_tok)
                tokl = None
                for kc in range(DC):
                    tokl = p.loadc(wpool[sl][:, kc * 1024:(kc + 1) * 1024],
                                   st["wd"][kc * 128:(kc + 1) * 128, :],
                                   "ldg")
                st["wtok"] = tokl
                if st.get("be") is not None:
                    e = st["be"]
                    p.wait(nc.sync, "mme", p.tail("mme"))
                    tb = None
                    for j in range(DC):
                        tb = p.load(
                            bpool[sl][:, j:j + 1],
                            w["be"][e * 1024 + j * 128: e * 1024 + (j + 1) * 128, :],
                            "ld")
                    st["btok"] = tb

            nxt = [0]

            def issue_tmp(wait_sem, wait_tok):
                if nxt[0] >= len(tmp_stages):
                    return
                st = stages[tmp_stages[nxt[0]]]
                n_src = st["src"][1]
                if wait_tok:
                    p.wait(nc.sync, wait_sem, wait_tok)
                p.wait(nc.sync, tag + "nst", store_tok[n_src])
                st["tmp_tok"] = p.load(TMP[:], nodes_d[n_src][:], "ld")
                nxt[0] += 1

            issue_tmp(None, 0)
            issue_wload(wlist[0], 0)
            for s_i, st in enumerate(stages):
                kind = st["kind"]
                c = st["node"]
                if kind == "att":
                    p.wait(nc.tensor, "mme", p.tail("mme"))
                    attention_c(p, MQ, MK, MV, CTX, H, dh, dh ** -0.5, identb,
                                tag=tag + f"a{c}", shared=attb)
                    if nxt[0] < len(tmp_stages) and \
                            stages[tmp_stages[nxt[0]]]["node"] > c:
                        issue_tmp("atvt", p.tail("atvt"))
                    continue
                wp = st["wpos"]
                if wp + 1 < len(wlist):
                    issue_wload(wlist[wp + 1],
                                wlist[wp - 1]["mmdone"] if wp >= 1 else 0)
                sl = st["wslot"]
                p.wait(nc.tensor, "ldg", st["wtok"])
                p.wait(nc.tensor, "mme", p.tail("mme"))
                if st["src"][0] == "tmp":
                    p.wait(nc.tensor, "ld", st["tmp_tok"])
                if kind == "edge":
                    p.wait(nc.vector, "ld", st["btok"])
                    dest = bufs[st["dest"]]

                    def evac(j, n, ps, mtok, dest=dest, beb=bpool[sl]):
                        p.wait(nc.vector, "mmm", mtok)
                        sl2 = slice(j * T + n * 512, j * T + (n + 1) * 512)
                        return nc.vector.tensor_scalar_add(dest[:, sl2], ps[:],
                                                           beb[:, j:j + 1])
                elif kind == "proj":
                    dest = bufs[st["dest"]]

                    def evac(j, n, ps, mtok, dest=dest, ewi=st["ew"]):
                        p.wait(nc.vector, "mmm", mtok)
                        sl2 = slice(j * T + n * 512, j * T + (n + 1) * 512)
                        return nc.vector.tensor_scalar_mul(
                            dest[:, sl2], ps[:], ewb[:, ewi:ewi + 1])
                else:
                    p.wait(nc.tensor, "atce", p.tail("atce"))

                    def evac(j, n, ps, mtok, c=c, ewq=st["ew_q"]):
                        sl2 = slice(j * T + n * 512, j * T + (n + 1) * 512)
                        p.wait(nc.vector, "mmm", mtok)
                        nc.vector.tensor_scalar_mul(tsc[:], ps[:],
                                                    awb[:, c:c + 1])
                        nc.vector.drain()
                        return nc.vector.scalar_tensor_tensor(
                            out=MK[:, sl2], in0=Q[:, sl2],
                            scalar=ewb[:, ewq:ewq + 1],
                            in1=tsc[:], op0=ALU.mult, op1=ALU.add)
                src_buf = TMP if st["src"][0] == "tmp" else bufs[st["src"][1]]
                mm_ff(p, wpool[sl], 1024, 1024, src_buf, evac, pspool=mmpool)
                st["mmdone"] = p.tail("mmm")
                if st["src"][0] == "tmp" and nxt[0] < len(tmp_stages) and \
                        stages[tmp_stages[nxt[0]]]["node"] == c:
                    issue_tmp("mmm", st["mmdone"])
                if kind == "wo" and not st["last"]:
                    p.wait(nc.sync, "mme", p.tail("mme"))
                    store_tok[c] = p.store(nodes_d[c][:], MK[:], tag + "nst")
        gf = p.sbuf([128, D], F32, "gf")
        bf = p.sbuf([128, D], F32, "bf")
        p.load(gf[:], w["lng_bc"][:])
        p.load(bf[:], w["lnb_bc"][:])
        p.barrier()
        ln_residual_store(p, MK, x_dram, out_dram, identf, gf, bf, 1e-6, 0.5,
                          tag=tag + "lrs", src_bf16=True, identb=identb)


def sub_l1ffn(p, x_dram, out_dram, w, tag=None):
    nc = p.nc
    tag = "mo"
    E, HD = 8, 512
    B = {s: p.tail(tag + s) for s in ("gsc", "gnm", "gex", "yz")}
    with p.scope():
        identb = p.sbuf([128, 128], BF16, "idb")
        p.loadc(identb[:], w["ident"][:])
        p.barrier()
        xT = p.sbuf([128, DC * T], BF16, "xT")
        ln_to_xT(p, x_dram, xT, identb, eps=1e-5, gain=None, tag=tag + "ln")
        p.barrier()
        if "dbg_xT" in w:
            xc32 = p.sbuf([128, T], F32, "dbgx")
            for cc in range(DC):
                p.wait(nc.vector, tag + "dbg", cc * 16)
                p.inc(nc.vector.tensor_copy(out=xc32[:], in_=xT[:, cc * T:(cc + 1) * T]), tag + "dx")
                p.wait(nc.sync, tag + "dx", cc + 1)
                p.store(w["dbg_xT"][cc * 128:(cc + 1) * 128, :], xc32[:], tag + "dbg")
            p.barrier()

        scale_sb = p.sbuf([128, NT * E], F32, "gsc")
        with p.scope():
            gwb = p.sbuf([128, E * E], BF16, "gw")
            for kc in range(DC):
                p.loadc(gwb[:, kc * E:(kc + 1) * E],
                        w["gate_w"][kc * 128:(kc + 1) * 128, :], "ldg")
            gps = p.psum([128, E], F32, "gps")
            lg = p.sbuf([128, E], F32, "lg")
            ex = p.sbuf([128, E], F32, "ex")
            sm = p.sbuf([128, E], F32, "sm")
            nmx = p.sbuf([128, 1], F32, "nmx")
            ssum = p.sbuf([128, 1], F32, "ssum")
            rsum = p.sbuf([128, 1], F32, "rs")
            m1 = p.sbuf([128, 1], F32, "m1")
            m2 = p.sbuf([128, 1], F32, "m2")
            msk = p.sbuf([128, E], F32, "msk")
            msk2 = p.sbuf([128, E], F32, "msk2")
            dbgg = p.sbuf([128, 32], F32, "dbgg") if "dbg_gate" in w else None
            dbglg = p.sbuf([128, NT * 8], F32, "dbglg") if "dbg_lg" in w else None
            p.barrier()
            if "dbg_gw" in w:
                gwf = p.sbuf([128, 64], F32, "gwf")
                p.inc(nc.vector.tensor_copy(out=gwf[:], in_=gwb[:]), tag + "dgw")
                p.wait(nc.sync, tag + "dgw")
                p.store(w["dbg_gw"][:], gwf[:], tag + "dbg")
                p.barrier()
            for tt in range(NT):
                p.wait(nc.tensor, tag + "gsc", B["gsc"] + tt)
                im = None
                for kc in range(DC):
                    im = nc.tensor.matmul(gps[:], xT[:, kc * T + tt * 128:
                                                     kc * T + (tt + 1) * 128],
                                          gwb[:, kc * E:(kc + 1) * E],
                                          start=(kc == 0), stop=(kc == DC - 1))
                gm = p.inc(im, tag + "gm")
                p.wait(nc.vector, tag + "gm", gm)
                nc.vector.tensor_copy(out=lg[:], in_=gps[:])
                if "dbg_lg" in w:
                    nc.vector.tensor_copy(out=dbglg[:, tt * 8:(tt + 1) * 8], in_=gps[:])
                nc.vector.drain()
                nc.vector.reduce_max(out=nmx[:], in_=lg[:], axis=AX.X, negate=True)
                nc.vector.drain()
                p.inc(nc.vector.engine_nop(), tag + "gnm")
                p.wait(nc.scalar, tag + "gnm", B["gnm"] + tt + 1)
                p.inc(nc.scalar.activation(out=ex[:], in_=lg[:], func=AF.Exp,
                                           bias=nmx[:]), tag + "gex")
                p.wait(nc.vector, tag + "gex", B["gex"] + tt + 1)
                nc.vector.reduce_sum(out=ssum[:], in_=ex[:], axis=AX.X)
                nc.vector.drain()
                nc.vector.reciprocal(out=rsum[:], in_=ssum[:])
                nc.vector.drain()
                nc.vector.tensor_scalar_mul(sm[:], ex[:], rsum[:])
                nc.vector.drain()
                if "dbg_gate" in w and tt == 0:
                    nc.vector.tensor_copy(out=dbgg[:, 0:8], in_=lg[:])
                    nc.vector.tensor_copy(out=dbgg[:, 8:16], in_=ex[:])
                    nc.vector.tensor_copy(out=dbgg[:, 16:17], in_=ssum[:])
                    nc.vector.tensor_copy(out=dbgg[:, 17:18], in_=rsum[:])
                    nc.vector.tensor_copy(out=dbgg[:, 24:32], in_=sm[:])
                    nc.vector.tensor_copy(out=dbgg[:, 18:19], in_=nmx[:])
                nc.vector.reduce_max(out=m1[:], in_=sm[:], axis=AX.X)
                nc.vector.drain()
                nc.vector.tensor_single_scalar(out=msk[:], in_=sm[:], scalar=m1[:],
                                               op=ALU.is_ge)
                nc.vector.drain()
                nc.vector.scalar_tensor_tensor(out=msk2[:], in0=msk[:], scalar=-1.0,
                                               in1=sm[:], op0=ALU.mult,
                                               op1=ALU.mult)
                nc.vector.drain()
                nc.vector.tensor_add(msk2[:], msk2[:], sm[:])
                nc.vector.drain()
                nc.vector.reduce_max(out=m2[:], in_=msk2[:], axis=AX.X)
                nc.vector.drain()
                nc.vector.tensor_single_scalar(out=msk2[:], in_=msk2[:],
                                               scalar=m2[:], op=ALU.is_ge)
                nc.vector.drain()
                nc.vector.tensor_add(msk[:], msk[:], msk2[:])
                nc.vector.drain()
                p.inc(nc.vector.tensor_mul(scale_sb[:, tt * E:(tt + 1) * E],
                                           sm[:], msk[:]), tag + "gsc")
            if "dbg_gate" in w:
                p.store(w["dbg_gate"][:], dbgg[:], tag + "dbg")
            if "dbg_lg" in w:
                p.store(w["dbg_lg"][:], dbglg[:], tag + "dbg")
        p.barrier()
        if "dbg_scale" in w:
            p.store(w["dbg_scale"][:], scale_sb[:], tag + "dbg")
            p.wait(nc.sync, tag + "dbg")

        y = p.sbuf([128, NT * D], F32, "yacc")
        p.inc(nc.vector.memset(y[:], 0.0), tag + "yz")
        sa = p.sbuf([128, HD], F32, "sa")
        hb = p.sbuf([128, HD], F32, "hb")
        hh = p.sbuf([128, HD], BF16, "hh")
        hTs = p.sbuf([128, HD], BF16, "hTs")
        aps = p.psum([128, HD], F32, "aps")
        bps = p.psum([128, HD], F32, "bps")
        tps = p.psum([128, HD], BF16, "tps")
        yps = p.psum([128, D], F32, "yps")
        w1s = [p.sbuf([128, DC * HD], BF16, "w1b") for _ in range(2)]
        w3s = [p.sbuf([128, DC * HD], BF16, "w3b") for _ in range(2)]
        w2s = [p.sbuf([128, 4 * D], BF16, "w2b") for _ in range(2)]

        def exp_load(e, par):
            sn = tag + "wl" + str(par)
            tok = None
            for kc in range(DC):
                r = slice(e * 1024 + kc * 128, e * 1024 + (kc + 1) * 128)
                tok = p.loadc(w1s[par][:, kc * HD:(kc + 1) * HD], w["w1"][r, :], sn)
                tok = p.loadc(w3s[par][:, kc * HD:(kc + 1) * HD], w["w3"][r, :], sn)
            for kc in range(4):
                r = slice(e * HD + kc * 128, e * HD + (kc + 1) * 128)
                tok = p.loadc(w2s[par][:, kc * D:(kc + 1) * D], w["w2"][r, :], sn)
            return tok

        wtok = [exp_load(0, 0), 0]
        lastmm = [0, 0]
        for e in range(E):
            par = e % 2
            if e + 1 < E:
                if lastmm[(e + 1) % 2]:
                    p.wait(nc.gpsimd, "mot", lastmm[(e + 1) % 2])
                wtok[(e + 1) % 2] = exp_load(e + 1, (e + 1) % 2)
            p.wait(nc.tensor, tag + "wl" + str(par), wtok[par])
            w1b, w3b, w2b = w1s[par], w3s[par], w2s[par]
            if True:
                for tt in range(NT):
                    sc = scale_sb[:, tt * E + e: tt * E + e + 1]
                    p.wait(nc.tensor, "mov", p.tail("mov"))
                    ia = ib = None
                    for kc in range(DC):
                        lhs = xT[:, kc * T + tt * 128: kc * T + (tt + 1) * 128]
                        ia = nc.tensor.matmul(aps[:], lhs, w1b[:, kc * HD:(kc + 1) * HD],
                                              start=(kc == 0), stop=(kc == DC - 1))
                    for kc in range(DC):
                        lhs = xT[:, kc * T + tt * 128: kc * T + (tt + 1) * 128]
                        ib = nc.tensor.matmul(bps[:], lhs, w3b[:, kc * HD:(kc + 1) * HD],
                                              start=(kc == 0), stop=(kc == DC - 1))
                    abm = p.inc(ib, "mot")
                    p.wait(nc.scalar, "mot", abm)
                    p.inc(nc.scalar.activation(out=sa[:], in_=aps[:], func=AF.Silu,
                                               scale=sc), tag + "hsa")
                    p.wait(nc.vector, "mot", abm)
                    nc.vector.tensor_scalar_mul(hb[:], bps[:], sc)
                    nc.vector.drain()
                    p.wait(nc.vector, tag + "hsa", p.tail(tag + "hsa"))
                    ih = nc.vector.tensor_mul(hh[:], sa[:], hb[:])
                    hse = p.inc(ih, "mov")
                    p.wait(nc.tensor, "mov", hse)
                    p.wait(nc.tensor, "mov", p.tail("mov"))
                    it = None
                    for c4 in range(4):
                        it = nc.tensor.transpose(tps[:, c4 * 128:(c4 + 1) * 128],
                                                 hh[:, c4 * 128:(c4 + 1) * 128],
                                                 identb[:])
                    htm = p.inc(it, "mot")
                    p.wait(nc.vector, "mot", htm)
                    ihe = nc.vector.tensor_copy(out=hTs[:], in_=tps[:])
                    hte = p.inc(ihe, "mov")
                    p.wait(nc.tensor, "mov", hte)
                    p.wait(nc.tensor, "mov", p.tail("mov"))
                    iy = None
                    for hc in range(4):
                        for half in range(2):
                            iy = nc.tensor.matmul(
                                yps[:, half * 512:(half + 1) * 512],
                                hTs[:, hc * 128:(hc + 1) * 128],
                                w2b[:, hc * D + half * 512: hc * D + (half + 1) * 512],
                                start=(hc == 0), stop=(hc == 3))
                    ym = p.inc(iy, "mot")
                    p.wait(nc.vector, "mot", ym)
                    if e == 0 and tt == 0:
                        p.wait(nc.vector, tag + "yz", B["yz"] + 1)
                    iya = nc.vector.tensor_add(y[:, tt * D:(tt + 1) * D],
                                               y[:, tt * D:(tt + 1) * D], yps[:])
                    p.inc(iya, "mov")
            lastmm[par] = p.tail("mot")
        p.barrier()
        xtile = p.sbuf([128, D], F32, "rx")
        otile = p.sbuf([128, D], F32, "ro")
        Brev = p.tail("mov")
        Brst = p.tail("lnst")
        for tt in range(NT):
            p.wait(nc.sync, "mov", Brev + tt)
            lt = p.load(xtile[:], x_dram[tt * 128:(tt + 1) * 128, :], "lnld0")
            p.wait(nc.vector, "lnld0", lt)
            p.wait(nc.vector, "lnst", Brst + tt * 16)
            iv = nc.vector.scalar_tensor_tensor(out=otile[:],
                                                in0=y[:, tt * D:(tt + 1) * D],
                                                scalar=0.5, in1=xtile[:],
                                                op0=ALU.mult, op1=ALU.add)
            rev = p.inc(iv, "mov")
            p.wait(nc.sync, "mov", rev)
            p.store(out_dram[tt * 128:(tt + 1) * 128, :], otile[:], "lnst")
        p.wait(nc.sync, "lnst")


def mamba_branch(p, xz_d, ocum_d, w, pfx, tag=None, reverse=False,
                 zmax_sb=None):
    tag = "mb"
    p._dma_sems.add(tag + "oc")
    B = {s: p.tail(tag + s) for s in ("cse", "cvv", "xcw", "dld", "zs", "yd",
                                     "cvd")}
    """One mamba branch. xz_d: DRAM [4096, 2048] fp32 (feature-major).
    ocum_d: DRAM [2048, 2048] fp32 accumulator (write if first=not reverse,
    DMA-add otherwise). reverse: process xz reversed along l (bwd branch).
    zmax_sb: [128, 16] sbuf to fill with per-chunk max_l silu(z) (fwd only).
    """
    nc = p.nc
    DI, DS, DR = 2048, 16, 64
    KC = DI // 128

    with p.scope():
        convw = p.sbuf([128, KC * 4], F32, "cvw")
        convb = p.sbuf([128, KC], F32, "cvb")
        Asb = p.sbuf([128, KC * DS], F32, "A")
        Dsb = p.sbuf([128, KC], F32, "D")
        dtb = p.sbuf([128, KC], F32, "dtb")
        for kc in range(KC):
            r0 = kc * 128
            p.load(convw[:, kc * 4:(kc + 1) * 4], w[pfx + "conv_w"][r0:r0 + 128, :], "ld")
            p.load(convb[:, kc:kc + 1], w[pfx + "conv_b"][r0:r0 + 128, :], "ld")
            p.load(Asb[:, kc * DS:(kc + 1) * DS], w[pfx + "A_log"][r0:r0 + 128, :], "ld")
            p.load(Dsb[:, kc:kc + 1], w[pfx + "Dp"][r0:r0 + 128, :], "ld")
            p.load(dtb[:, kc:kc + 1], w[pfx + "dt_b"][r0:r0 + 128, :], "ld")
        p.barrier()
        p.inc(nc.scalar.activation(out=Asb[:], in_=Asb[:], func=AF.Exp),
              "mbs")
        p.wait(nc.vector, "mbs")
        nc.vector.drain()
        p.inc(nc.vector.tensor_scalar_mul(Asb[:], Asb[:], -1.0), "mbv")
        p.barrier()

        xconv_d = nc.dram_tensor(p.name(tag + "xc"), [DI, T], F32, kind="Internal").ap()
        delta_d = nc.dram_tensor(p.name(tag + "dl"), [DI, T], F32, kind="Internal").ap()
        bc_d = nc.dram_tensor(p.name(tag + "bc"), [32, T], F32, kind="Internal").ap()

        ph12 = p.scope()
        ph12.__enter__()
        xcb = p.sbuf([128, KC * T], BF16, "xcb")
        xpT = p.sbuf([128, KC * 96], BF16, "xpT")
        dtT = p.sbuf([64, DI], BF16, "dtT")
        for kc in range(KC):
            r0 = kc * 128
            p.loadc(xpT[:, kc * 96:(kc + 1) * 96], w[pfx + "x_projT"][r0:r0 + 128, :], "ldg")
        p.loadc(dtT[:, :], w[pfx + "dt_projT"][:, :], "ldg")
        ldg_tok = p.tail("ldg")
        for e in (nc.tensor, nc.vector, nc.scalar):
            p.wait(e, "ldg", ldg_tok)
        with p.scope():
            xzc = p.sbuf([128, T], F32, "xzc")
            xrv = p.sbuf([128, T], F32, "xrv")
            cacc = p.sbuf([128, T], F32, "cacc")
            xcv = p.sbuf([128, T], F32, "xcv")
            for kc in range(KC):
                par = tag + "zl" + str(kc % 2)
                p.wait(nc.sync, tag + "cse", B["cse"] + kc - 1)
                lt = p.load(xzc[:], xz_d[kc * 128:(kc + 1) * 128, :], par)
                p.wait(nc.vector, par, lt)
                if reverse:
                    nc.vector.tensor_copy(out=xrv[:], in_=xzc[:, ::-1])
                    nc.vector.drain()
                    src = xrv
                else:
                    src = xzc
                wc = convw[:, kc * 4:kc * 4 + 4]
                p.wait(nc.vector, tag + "cse", B["cse"] + kc)
                nc.vector.tensor_scalar_mul(cacc[:], src[:], wc[:, 3:4])
                for t_ in (2, 1, 0):
                    sh = 3 - t_
                    nc.vector.drain()
                    nc.vector.scalar_tensor_tensor(
                        out=cacc[:, sh:T], in0=src[:, 0:T - sh],
                        scalar=wc[:, t_:t_ + 1], in1=cacc[:, sh:T],
                        op0=ALU.mult, op1=ALU.add)
                iv = nc.vector.drain()
                p.inc(nc.vector.engine_nop(), tag + "cvv")
                p.wait(nc.scalar, tag + "cvv", B["cvv"] + kc + 1)
                p.wait(nc.scalar, tag + "xcw", B["xcw"] + kc)
                isl = nc.scalar.activation(out=xcv[:], in_=cacc[:], func=AF.Silu,
                                           bias=convb[:, kc:kc + 1])
                cse = p.inc(isl, tag + "cse")
                p.wait(nc.vector, tag + "cse", cse)
                nc.vector.drain()
                ic = nc.vector.tensor_copy(out=xcb[:, kc * T:(kc + 1) * T], in_=xcv[:])
                p.inc(ic, tag + "xcw")
                p.wait(nc.sync, tag + "xcw", p.tail(tag + "xcw"))
                p.store(xconv_d[kc * 128:(kc + 1) * 128, :], xcv[:], tag + "cvd")
            p.wait(nc.sync, tag + "cvd")
        p.barrier()

        dtsb = p.sbuf([64, T], BF16, "dts")
        with p.scope():
            dt_ps = p.psum([64, 512], F32, "dtp")
            b_ps = p.psum([16, 512], F32, "bp")
            c_ps = p.psum([16, 512], F32, "cp")
            Bsb0 = p.sbuf([16, T], F32, "bsb")
            Csb0 = p.sbuf([16, T], F32, "csb")
            for n in range(NQ):
                sl = slice(n * 512, (n + 1) * 512)
                p.wait(nc.tensor, tag + "dbe", p.tail(tag + "dbe"))
                for kc in range(KC):
                    i1 = nc.tensor.matmul(dt_ps[:], xpT[:, kc * 96:kc * 96 + 64],
                                          xcb[:, kc * T + n * 512:kc * T + (n + 1) * 512],
                                          start=(kc == 0), stop=(kc == KC - 1))
                for kc in range(KC):
                    i2 = nc.tensor.matmul(b_ps[:], xpT[:, kc * 96 + 64:kc * 96 + 80],
                                          xcb[:, kc * T + n * 512:kc * T + (n + 1) * 512],
                                          start=(kc == 0), stop=(kc == KC - 1))
                for kc in range(KC):
                    i3 = nc.tensor.matmul(c_ps[:], xpT[:, kc * 96 + 80:kc * 96 + 96],
                                          xcb[:, kc * T + n * 512:kc * T + (n + 1) * 512],
                                          start=(kc == 0), stop=(kc == KC - 1))
                mt = p.inc(i3, tag + "dbm")
                p.wait(nc.vector, tag + "dbm", mt)
                nc.vector.tensor_copy(out=dtsb[:, sl], in_=dt_ps[:])
                nc.vector.tensor_copy(out=Bsb0[:, sl], in_=b_ps[:])
                p.inc(nc.vector.tensor_copy(out=Csb0[:, sl], in_=c_ps[:]),
                      tag + "dbe")
            p.barrier()
            p.store(bc_d[0:16, :], Bsb0[:, :], "st")
            p.store(bc_d[16:32, :], Csb0[:, :], "st")
            p.wait(nc.sync, "st")
        with p.scope():
            d_ps = [p.psum([128, 512], F32, "dps"), p.psum([128, 512], F32, "dps")]
            sabs = p.sbuf([128, 512], F32, "sab")
            sexp = p.sbuf([128, 512], F32, "sex")
            srel = p.sbuf([128, 512], F32, "srl")
            dchunk = p.sbuf([128, T], F32, "dch")
            si = 0
            for jd in range(KC):
                p.wait(nc.vector, tag + "dld", B["dld"] + jd * 16)
                for n in range(NQ):
                    sl = slice(n * 512, (n + 1) * 512)
                    p.wait(nc.tensor, tag + "dle", p.tail(tag + "dle") - 1)
                    im = nc.tensor.matmul(d_ps[si][:],
                                          dtT[:, jd * 128:(jd + 1) * 128],
                                          dtsb[:, sl], start=True, stop=True)
                    mt = p.inc(im, tag + "dlm")
                    p.wait(nc.scalar, tag + "dlm", mt)
                    p.wait(nc.scalar, tag + "dlv", p.tail(tag + "dlv"))
                    nc.scalar.activation(out=srel[:], in_=d_ps[si][:], func=AF.Relu,
                                         bias=dtb[:, jd:jd + 1])
                    nc.scalar.drain()
                    nc.scalar.activation(out=sabs[:], in_=d_ps[si][:], func=AF.Abs,
                                         bias=dtb[:, jd:jd + 1])
                    nc.scalar.drain()
                    nc.scalar.activation(out=sexp[:], in_=sabs[:], func=AF.Exp,
                                         scale=-1.0)
                    nc.scalar.drain()
                    ie = nc.scalar.activation(out=sabs[:], in_=sexp[:], func=AF.Ln,
                                              bias=p.constf(1.0))
                    se = p.inc(ie, tag + "dle")
                    p.wait(nc.vector, tag + "dle", se)
                    iv = nc.vector.tensor_add(dchunk[:, sl], srel[:], sabs[:])
                    p.inc(iv, tag + "dlv")
                    si ^= 1
                p.wait(nc.sync, tag + "dlv")
                p.store(delta_d[jd * 128:(jd + 1) * 128, :], dchunk[:], tag + "dld")
                p.wait(nc.sync, tag + "dld")
        ph12.__exit__(None, None, None)
        p.barrier()

        with p.scope():
            Bball = p.sbuf([128, DS * T], BF16, "Bb")
            Cball = p.sbuf([128, DS * T], BF16, "Cb")
            with p.scope():
                rowst = p.sbuf([128, T], F32, "rwst")
                for n in range(DS):
                    for (dst, roff) in ((Bball, 0), (Cball, 16)):
                        bc_in = bass.AP(tensor=bc_d.tensor, offset=(roff + n) * T,
                                        ap=[[0, 128], [1, T]])
                        lt = p.load(rowst[:], bc_in, "ld")
                        p.wait(nc.vector, "ld", lt)
                        p.inc(nc.vector.tensor_copy(out=dst[:, n * T:(n + 1) * T],
                                                    in_=rowst[:]), tag + "bcc")
                        p.wait(nc.sync, tag + "bcc", p.tail(tag + "bcc"))
            dch = p.sbuf([128, T], F32, "dch")
            xcvc = p.sbuf([128, T], F32, "xcv")
            ub = p.sbuf([128, T], BF16, "ub")
            dAb = p.sbuf([128, T], F32, "dA")
            dBu = [p.sbuf([128, T], F32, "dBu"), p.sbuf([128, T], F32, "dBu")]
            hsb = [p.sbuf([128, T], BF16, "hs"), p.sbuf([128, T], BF16, "hs")]
            prod = [p.sbuf([128, T], F32, "prod"), p.sbuf([128, T], F32, "prod")]
            yacc = p.sbuf([128, T], F32, "y")
            zch = dch
            zsl = dBu[0]
            for kc in range(KC):
                p.wait(nc.sync, tag + "zs", B["zs"] + kc)
                p.wait(nc.sync, tag + "yd", B["yd"] + kc)
                l1 = p.load(dch[:], delta_d[kc * 128:(kc + 1) * 128, :], "mbl" + str(kc % 2))
                l2 = p.load(xcvc[:], xconv_d[kc * 128:(kc + 1) * 128, :], "mbl" + str(kc % 2))
                p.wait(nc.vector, "mbl" + str(kc % 2), l1)
                p.wait(nc.vector, "mbl" + str(kc % 2), l2)
                iu = nc.vector.tensor_mul(ub[:], dch[:], xcvc[:])
                uu = p.inc(iu, "mbv")
                p.wait(nc.scalar, "mbv", uu)
                nc.vector.memset(yacc[:], 0.0)
                p.inc(nc.vector.tensor_mul(dBu[0][:], ub[:],
                                           Bball[:, 0 * T:1 * T]), "mbv")
                p.inc(nc.vector.tensor_mul(dBu[1][:], ub[:],
                                           Bball[:, 1 * T:2 * T]), "mbv")
                sc_tok = 0
                for n in range(DS):
                    if n > 0:
                        p.wait(nc.scalar, "mbv", sc_tok)
                    ia = nc.scalar.activation(out=dAb[:], in_=dch[:], func=AF.Exp,
                                              scale=Asb[:, kc * DS + n:kc * DS + n + 1])
                    da = p.inc(ia, tag + "sdA")
                    p.wait(nc.vector, tag + "sdA", da)
                    isc = nc.vector.tensor_tensor_scan(
                        out=hsb[n % 2][:], data0=dAb[:], data1=dBu[n % 2][:],
                        initial=0.0, op0=ALU.mult, op1=ALU.add)
                    sc_tok = p.inc(isc, "mbv")
                    if n + 2 < DS:
                        p.inc(nc.vector.tensor_mul(
                            dBu[n % 2][:], ub[:],
                            Bball[:, (n + 2) * T:(n + 3) * T]), "mbv")
                    if n >= 1:
                        p.inc(nc.vector.tensor_mul(
                            prod[(n - 1) % 2][:], hsb[(n - 1) % 2][:],
                            Cball[:, (n - 1) * T:n * T]), "mbv")
                    if n >= 2:
                        p.inc(nc.vector.tensor_add(yacc[:], yacc[:],
                                                   prod[(n - 2) % 2][:]), "mbv")
                p.inc(nc.vector.tensor_mul(prod[1][:], hsb[1][:],
                                           Cball[:, (DS - 1) * T:DS * T]), "mbv")
                p.inc(nc.vector.tensor_add(yacc[:], yacc[:], prod[0][:]), "mbv")
                nc.vector.drain()
                p.inc(nc.vector.tensor_add(yacc[:], yacc[:], prod[1][:]), "mbv")
                nc.vector.drain()
                iy = nc.vector.scalar_tensor_tensor(
                    out=yacc[:], in0=xcvc[:], scalar=Dsb[:, kc:kc + 1], in1=yacc[:],
                    op0=ALU.mult, op1=ALU.add)
                p.inc(iy, tag + "yd")
                p.wait(nc.sync, "mbv", p.tail("mbv"))
                lz = p.load(zch[:], xz_d[DI + kc * 128: DI + (kc + 1) * 128, :],
                            "mbl" + str(kc % 2))
                p.wait(nc.scalar, "mbl" + str(kc % 2), lz)
                p.wait(nc.scalar, "mbv", p.tail("mbv"))
                if reverse:
                    p.wait(nc.vector, "mbl" + str(kc % 2), lz)
                    nc.vector.drain()
                    izr = nc.vector.tensor_copy(out=zsl[:], in_=zch[:, ::-1])
                    p.inc(izr, "mbv")
                    p.wait(nc.scalar, "mbv", p.tail("mbv"))
                    iz = nc.scalar.activation(out=zsl[:], in_=zsl[:], func=AF.Silu)
                else:
                    p.wait(nc.scalar, "mbv", p.tail("mbv"))
                    iz = nc.scalar.activation(out=zsl[:], in_=zch[:], func=AF.Silu)
                zs = p.inc(iz, tag + "zs")
                p.wait(nc.vector, tag + "zs", zs)
                nc.vector.drain()
                if (zmax_sb is not None) and not reverse:
                    nc.vector.reduce_max(out=zmax_sb[:, kc:kc + 1], in_=zsl[:],
                                         axis=AX.X)
                    nc.vector.drain()
                ig = nc.vector.tensor_mul(yacc[:], yacc[:], zsl[:])
                p.inc(ig, "mbv")
                nc.vector.drain()
                if reverse:
                    nc.vector.tensor_copy(out=prod[0][:], in_=yacc[:, ::-1])
                    out_src = prod[0]
                else:
                    out_src = yacc
                nc.vector.drain()
                gv = p.inc(nc.vector.engine_nop(), "mbv")
                p.wait(nc.gpsimd, "mbv", gv)
                if reverse:
                    p.inc(nc.gpsimd.dma_start(
                        out=ocum_d[kc * 128:(kc + 1) * 128, :], in_=out_src[:],
                        accum_op=ALU.add), tag + "oc", 16)
                else:
                    p.inc(nc.gpsimd.dma_start(
                        out=ocum_d[kc * 128:(kc + 1) * 128, :], in_=out_src[:]),
                        tag + "oc", 16)
                p.wait(nc.vector, tag + "oc", p.tail(tag + "oc"))
        p.barrier()


def sub_l1attn(p, x_dram, out_dram, w, tag=None, with_fft=True,
               branches=("f", "b")):
    nc = p.nc
    tag = "la"
    DI = 2048
    B = {s: p.tail(tag + s) for s in ("ipd", "ybc")}
    with p.scope():
        identb = p.sbuf([128, 128], BF16, "idb")
        identf = p.sbuf([128, 128], F32, "idf")
        p.loadc(identb[:], w["ident"][:])
        p.load(identf[:], w["ident"][:])
        zmax = p.sbuf([128, 16], F32, "zmax")
        p.barrier()
        xz_d = nc.dram_tensor(p.name(tag + "xz"), [2 * DI, T], F32,
                              kind="Internal").ap()
        ocum_d = nc.dram_tensor(p.name(tag + "oc"), [DI, T], F32,
                                kind="Internal").ap()

        with p.scope():
            xT = p.sbuf([128, DC * T], BF16, "xT")
            ln_to_xT(p, x_dram, xT, identb, eps=1e-5, gain=None, tag=tag + "ln")
            p.barrier()
            stg = p.sbuf([128, T], F32, "stg")
            with p.scope():
                wq, _ = load_w(p, w["in_projT"], 1024, 4096)
                p.barrier()

                KCW = 4096 // 128
                ps2 = [p.psum([128, 512], F32, "ip"), p.psum([128, 512], F32, "ip")]
                slot_tok = [0, 0]
                si = 0
                for j in range(KCW):
                    for n in range(NQ):
                        if slot_tok[si]:
                            p.wait(nc.tensor, tag + "ipe", slot_tok[si])
                        im = None
                        for kc in range(DC):
                            im = nc.tensor.matmul(
                                ps2[si][:], w_tile(wq, 4096, kc, j),
                                xT[:, kc * T + n * 512:kc * T + (n + 1) * 512],
                                start=(kc == 0), stop=(kc == DC - 1))
                        mtok = p.inc(im, tag + "ipm")
                        p.wait(nc.scalar, tag + "ipm", mtok)
                        p.wait(nc.scalar, tag + "ipd", B["ipd"] + j * 16)
                        ie = nc.scalar.activation(
                            out=stg[:, n * 512:(n + 1) * 512], in_=ps2[si][:],
                            func=AF.Copy)
                        slot_tok[si] = p.inc(ie, tag + "ipe")
                        si ^= 1
                    p.wait(nc.sync, tag + "ipe", p.tail(tag + "ipe"))
                    p.store(xz_d[j * 128:(j + 1) * 128, :], stg[:], tag + "ipd")
        p.barrier()

        if "f" in branches:
            mamba_branch(p, xz_d, ocum_d, w, "f_", tag + "mf", reverse=False,
                         zmax_sb=zmax)
        if "b" in branches:
            mamba_branch(p, xz_d, ocum_d, w, "b_", tag + "mb",
                         reverse=True)
        if "f" not in branches:
            pass

        if with_fft:
            fft_branch(p, xz_d, ocum_d, w, zmax, identb, tag + "ff")

        with p.scope():
            oT = p.sbuf([128, DC * T], F32, "oT")
            wo, _ = load_w(p, w["out_projT"], 2048, 1024)
            ych = p.sbuf([128, T], BF16, "ych")
            yb = p.sbuf([128, 16 * T], BF16, "yb")
            for kc in range(16):
                par = tag + "yl" + str(kc % 2)
                p.wait(nc.gpsimd, tag + "ybc", B["ybc"] + kc)
                lt = p.loadc(ych[:], ocum_d[kc * 128:(kc + 1) * 128, :], par)
                p.wait(nc.vector, par, lt)
                nc.vector.drain()
                p.inc(nc.vector.tensor_copy(out=yb[:, kc * T:(kc + 1) * T],
                                            in_=ych[:]), tag + "ybc")
            p.barrier()

            def evac_o(j, n, ps, mtok):
                p.wait(nc.scalar, "mmm", mtok)
                return nc.scalar.activation(
                    out=oT[:, j * T + n * 512:j * T + (n + 1) * 512],
                    in_=ps[:], func=AF.Copy)
            mm_ff(p, wo, 2048, 1024, yb, evac_o, rhs_T=T)
            p.barrier()
            residual_store(p, oT, x_dram, out_dram, identf, scale_src=1.0,
                           tag=tag + "res")


def chunk_major(dram_ap, col0, ncols, nrows=2048):
    W = dram_ap.ap[-1][1]
    base = dram_ap.offset + col0
    return bass.AP(tensor=dram_ap.tensor, offset=base,
                   ap=[[W, 128], [128 * W, nrows // 128], [1, ncols]])


def dft_pass(p, in_re_d, in_im_d, out_re_d, out_im_d, fre_d, fim_d, fimn_d,
             identb, tag=None, in_fp32=False):
    tag = "df"
    """Complex DFT along the partition axis of in (2048), with output tiles
    transposed on evacuation so out dims are [transform-axis-free, other].
    in_im_d None => real input. F matrices dram bf16 [2048, 2048].
    All in/out DRAM [2048, 2048]; in bf16 (or fp32 with cast), out bf16."""
    nc = p.nc
    cplx = in_im_d is not None
    with p.scope():
        slabs = 2 if cplx else 1
        inslab = p.sbuf([128, 16 * 512 * slabs], BF16, "dfin")

        def in_sl(kc, which):
            off = which * 16 * 512
            return inslab[:, off + kc * 512: off + (kc + 1) * 512]
        fsl = p.sbuf([128, 16 * 128 * 3], BF16, "dfF")

        def f_sl(kc, which):
            off = which * 16 * 128
            return fsl[:, off + kc * 128: off + (kc + 1) * 128]
        pre = p.psum([128, 512], F32, "dfr")
        pim = p.psum([128, 512], F32, "dfi")
        ptr = p.psum([128, 512], BF16, "dft")
        ev1 = p.sbuf([128, 512], BF16, "dfe")
        ev2 = p.sbuf([128, 512], BF16, "dfe")
        def slab_ap(dram, cols):
            return dram.rearrange("(kc p) c -> p (kc c)", p=128)[:, :] \
                if cols is None else dram
        for n in range(NQ):
            p.wait(nc.gpsimd, tag + "mm", p.tail(tag + "mm"))
            p.loadc(inslab[:, 0:16 * 512], chunk_major(in_re_d, n * 512, 512),
                    tag + "il")
            if cplx:
                p.loadc(inslab[:, 16 * 512:32 * 512],
                        chunk_major(in_im_d, n * 512, 512), tag + "il")
            for j in range(16):
                p.wait(nc.gpsimd, tag + "mm", p.tail(tag + "mm"))
                for wi, fd in enumerate((fre_d, fim_d) + ((fimn_d,) if cplx else ())):
                    p.loadc(fsl[:, wi * 16 * 128:(wi + 1) * 16 * 128],
                            chunk_major(fd, j * 128, 128), tag + "fl")
                p.wait(nc.tensor, tag + "il", p.tail(tag + "il"))
                p.wait(nc.tensor, tag + "fl", p.tail(tag + "fl"))
                p.wait(nc.tensor, tag + "e1", p.tail(tag + "e1"))
                im1 = im2 = None
                for kc in range(16):
                    im1 = nc.tensor.matmul(pre[:], f_sl(kc, 0), in_sl(kc, 0),
                                           start=(kc == 0), stop=(kc == 15) and not cplx)
                if cplx:
                    for kc in range(16):
                        im1 = nc.tensor.matmul(pre[:], f_sl(kc, 2), in_sl(kc, 1),
                                               start=False, stop=(kc == 15))
                for kc in range(16):
                    im2 = nc.tensor.matmul(pim[:], f_sl(kc, 1), in_sl(kc, 0),
                                           start=(kc == 0), stop=(kc == 15) and not cplx)
                if cplx:
                    for kc in range(16):
                        im2 = nc.tensor.matmul(pim[:], f_sl(kc, 0), in_sl(kc, 1),
                                               start=False, stop=(kc == 15))
                mt = p.inc(im2, tag + "mm")
                for (ps, od) in ((pre, out_re_d), (pim, out_im_d)):
                    p.wait(nc.vector, tag + "mm", mt)
                    p.wait(nc.vector, tag + "tt", p.tail(tag + "tt"))
                    e1 = nc.vector.tensor_copy(out=ev1[:], in_=ps[:])
                    ee = p.inc(e1, tag + "e1")
                    p.wait(nc.tensor, tag + "e1", ee)
                    p.wait(nc.tensor, tag + "e2", p.tail(tag + "e2"))
                    it = None
                    for s in range(4):
                        it = nc.tensor.transpose(ptr[:, s * 128:(s + 1) * 128],
                                                 ev1[:, s * 128:(s + 1) * 128],
                                                 identb[:])
                    tt2 = p.inc(it, tag + "tt")
                    p.wait(nc.vector, tag + "tt", tt2)
                    p.wait(nc.vector, tag + "st", p.tail(tag + "st"))
                    e2 = nc.vector.tensor_copy(out=ev2[:], in_=ptr[:])
                    e2t = p.inc(e2, tag + "e2")
                    p.wait(nc.sync, tag + "e2", e2t)
                    for s in range(4):
                        p.store(od[n * 512 + s * 128: n * 512 + (s + 1) * 128,
                                   j * 128:(j + 1) * 128],
                                ev2[:, s * 128:(s + 1) * 128], tag + "st")
    p.barrier()


def fft_branch(p, xz_d, ocum_d, w, zmax, identb, tag=None):
    nc = p.nc
    tag = "ff"
    N2 = float(T) * float(T)
    X1r = nc.dram_tensor(p.name(tag + "x1r"), [T, T], BF16, kind="Internal").ap()
    X1i = nc.dram_tensor(p.name(tag + "x1i"), [T, T], BF16, kind="Internal").ap()
    X2r = nc.dram_tensor(p.name(tag + "x2r"), [T, T], BF16, kind="Internal").ap()
    X2i = nc.dram_tensor(p.name(tag + "x2i"), [T, T], BF16, kind="Internal").ap()
    YCr = nc.dram_tensor(p.name(tag + "ycr"), [T, T], BF16, kind="Internal").ap()
    YCi = nc.dram_tensor(p.name(tag + "yci"), [T, T], BF16, kind="Internal").ap()
    fre, fim, fimn = w["fft_fre"], w["fft_fim"], w["fft_fimn"]

    dft_pass(p, xz_d[0:T, :], None, X1r, X1i, fre, fim, fimn, identb,
             tag + "p1")
    dft_pass(p, X1r, X1i, X2r, X2i, fre, fim, fimn, identb, tag + "p2")

    with p.scope():
        KJ = 4096 // 128
        KI = 4096 // 128
        HW = T // 2
        xch = p.sbuf([128, KI * (HW + 4)], BF16, "xch")
        wslab = p.sbuf([128, 5 * KI * 128], BF16, "wcv")

        def xc_sl(kc, c0, cn):
            return xch[:, kc * (HW + 4) + c0: kc * (HW + 4) + c0 + cn]
        cps = [p.psum([128, 512], F32, "cvp"), p.psum([128, 512], F32, "cvp")]
        cevs = p.sbuf([128, 512], BF16, "cve")
        fcb = p.sbuf([128, KJ], F32, "fcb")
        for j in range(KJ):
            p.load(fcb[:, j:j + 1], w["fft_b"][j * 128:(j + 1) * 128, :], tag + "cbl")
        si = 0
        for half in range(2):
            base = half * HW
            p.barrier()
            for kc in range(KI):
                src_d = X2r if kc < 16 else X2i
                row0 = (kc % 16) * 128
                lo = max(0, base - 2)
                hi = min(T, base + HW + 2)
                off = lo - (base - 2)
                p.loadc(xc_sl(kc, off, hi - lo),
                        src_d[row0:row0 + 128, lo:hi], tag + "cxl")
                if base - 2 < 0:
                    p.inc(nc.vector.memset(xc_sl(kc, 0, 2), 0.0), tag + "cxz")
                if base + HW + 2 > T:
                    p.inc(nc.vector.memset(xc_sl(kc, HW + 2, 2), 0.0), tag + "cxz")
            p.barrier()
            for j in range(KJ):
                p.wait(nc.gpsimd, tag + "cmm", p.tail(tag + "cmm"))
                for t_ in range(5):
                    p.loadc(wslab[:, t_ * KI * 128:(t_ + 1) * KI * 128],
                            chunk_major(w["fftw_t%d" % t_], j * 128, 128,
                                        nrows=4096), tag + "cwl")
                p.wait(nc.tensor, tag + "cwl", p.tail(tag + "cwl"))
                for n in range(HW // 512):
                    p.wait(nc.tensor, tag + "cev", max(0, p.tail(tag + "cev") - 1))
                    first = True
                    imm = None
                    for t_ in range(5):
                        for kc in range(KI):
                            imm = nc.tensor.matmul(
                                cps[si][:],
                                wslab[:, (t_ * KI + kc) * 128:(t_ * KI + kc + 1) * 128],
                                xc_sl(kc, n * 512 + t_, 512),
                                start=first, stop=(t_ == 4 and kc == KI - 1))
                            first = False
                    cmt = p.inc(imm, tag + "cmm")
                    p.wait(nc.scalar, tag + "cmm", cmt)
                    p.wait(nc.scalar, tag + "cst", p.tail(tag + "cst"))
                    ie = nc.scalar.activation(out=cevs[:], in_=cps[si][:],
                                              func=AF.Silu, bias=fcb[:, j:j + 1])
                    cev = p.inc(ie, tag + "cev")
                    p.wait(nc.sync, tag + "cev", cev)
                    od = YCr if j < 16 else YCi
                    p.store(od[(j % 16) * 128:(j % 16 + 1) * 128,
                               base + n * 512: base + (n + 1) * 512],
                            cevs[:], tag + "cst")
                    si ^= 1
        p.wait(nc.sync, tag + "cst")
    p.barrier()

    dft_pass(p, YCr, YCi, X1r, X1i, fre, fimn, fim, identb, tag + "q1")
    dft_pass(p, X1r, X1i, X2r, X2i, fre, fimn, fim, identb, tag + "q2")

    with p.scope():
        rech = p.sbuf([128, T], BF16, "rech")
        imch = p.sbuf([128, T], BF16, "imch")
        sq1 = p.sbuf([128, T], F32, "sq1")
        sq2 = p.sbuf([128, T], F32, "sq2")
        vch = p.sbuf([128, T], F32, "vch")
        xcn = p.sbuf([128, T], F32, "xcn")
        zab = p.sbuf([128, 16], F32, "zab")
        s1 = p.sbuf([128, 1], F32, "s1")
        s2 = p.sbuf([128, 1], F32, "s2")
        mu = p.sbuf([128, 1], F32, "mu")
        lnv = p.sbuf([128, 1], F32, "lnv")
        rstd = p.sbuf([128, 1], F32, "rstd")
        p.inc(nc.scalar.activation(out=zab[:], in_=zmax[:], func=AF.Abs),
              tag + "zab")
        p.barrier()
        for kc in range(16):
            p.wait(nc.gpsimd, tag + "asq", p.tail(tag + "asq"))
            l1 = p.loadc(rech[:], X2r[kc * 128:(kc + 1) * 128, :], tag + "ar" + str(kc % 2))
            l2 = p.loadc(imch[:], X2i[kc * 128:(kc + 1) * 128, :], tag + "ai" + str(kc % 2))
            p.wait(nc.scalar, tag + "ar" + str(kc % 2), l1)
            p.wait(nc.scalar, tag + "ai" + str(kc % 2), l2)
            p.wait(nc.scalar, tag + "asq", p.tail(tag + "asq"))
            nc.scalar.activation(out=sq1[:], in_=rech[:], func=AF.Square)
            nc.scalar.drain()
            i2 = nc.scalar.activation(out=sq2[:], in_=imch[:], func=AF.Square)
            sqt = p.inc(i2, tag + "asq")
            p.wait(nc.vector, tag + "asq", sqt)
            nc.vector.drain()
            iv = nc.vector.tensor_add(vch[:], sq1[:], sq2[:])
            p.inc(iv, tag + "avv")
            p.wait(nc.scalar, tag + "avv", p.tail(tag + "avv"))
            isq = nc.scalar.activation(out=vch[:], in_=vch[:], func=AF.Sqrt)
            vt = p.inc(isq, tag + "avs")
            p.wait(nc.vector, tag + "avs", vt)
            nc.vector.drain()
            nc.vector.tensor_scalar(out=vch[:], in0=vch[:],
                                    scalar1=zab[:, kc:kc + 1],
                                    scalar2=1.0 / N2, op0=ALU.mult, op1=ALU.mult)
            nc.vector.drain()
            p.inc(nc.vector.reduce_sum(out=s1[:], in_=vch[:], axis=AX.X), tag + "as1")
            nc.vector.drain()
            nc.vector.tensor_scalar_mul(mu[:], s1[:], 1.0 / T)
            nc.vector.drain()
            ixc = nc.vector.tensor_single_scalar(out=xcn[:], in_=vch[:],
                                                 scalar=mu[:], op=ALU.subtract)
            xct = p.inc(ixc, tag + "axc")
            p.wait(nc.scalar, tag + "axc", xct)
            p.wait(nc.scalar, tag + "alw", p.tail(tag + "alw"))
            p.inc(nc.scalar.activation(out=sq1[:], in_=xcn[:], func=AF.Square,
                                       accum_out=s2[:]), tag + "asq2")
            p.wait(nc.scalar, tag + "asq2")
            nc.scalar.activation(out=lnv[:], in_=s2[:], func=AF.Ln,
                                 scale=1.0 / T, bias=p.constf(1e-5))
            nc.scalar.drain()
            ir = nc.scalar.activation(out=rstd[:], in_=lnv[:], func=AF.Exp,
                                      scale=-0.5)
            rt = p.inc(ir, tag + "arstd")
            p.wait(nc.vector, tag + "arstd", rt)
            nc.vector.drain()
            io = nc.vector.tensor_scalar_mul(vch[:], xcn[:], rstd[:])
            ot = p.inc(io, tag + "alw")
            p.wait(nc.gpsimd, tag + "alw", ot)
            p.inc(nc.gpsimd.dma_start(out=ocum_d[kc * 128:(kc + 1) * 128, :],
                                      in_=vch[:], accum_op=ALU.add), tag + "aoc", 16)
            p.wait(nc.vector, tag + "aoc", p.tail(tag + "aoc"))
    p._dma_sems.add(tag + "aoc")
    p.barrier()


def _host_consts():
    cosF, sinF, pi = rope_tables()
    ang = -2.0 * np.pi * np.outer(np.arange(T), np.arange(T)) / T
    FRE = np.cos(ang).astype(np.float32)
    FIM = np.sin(ang).astype(np.float32)
    return {
        "ident": np.eye(128, dtype=np.float32),
        "ident64s": np.ascontiguousarray(np.tile(np.eye(64, dtype=np.float32), (2, 1))),
        "rope_pi": pi, "rope_cos": cosF, "rope_sin": sinF,
        "fft_fre": FRE, "fft_fim": FIM, "fft_fimn": np.ascontiguousarray(-FIM),
    }


def _launch(build_fn, host, xs):
    p = Prog()
    nc = p.nc
    xd = nc.dram_tensor("x", [T, D], F32, kind="ExternalInput").ap()
    od = nc.dram_tensor("out", [T, D], F32, kind="ExternalOutput").ap()
    w = {k: nc.dram_tensor(k, list(np.asarray(a).shape), F32,
                           kind="ExternalInput").ap() for k, a in host.items()}
    build_fn(p, xd, od, w)
    p.es.close()
    hostc = dict(host)
    hostc["fconsts"] = p.const_array()
    in_maps = [dict(hostc, x=np.ascontiguousarray(xs[i])) for i in range(8)]
    res = run_bass_kernel_spmd(nc, in_maps, core_ids=list(range(8)))
    return [res.results[i]["out"] for i in range(8)]


FUSED = True


def kernel(**inputs):
    A = np.ascontiguousarray
    g = lambda k: np.asarray(inputs[k], dtype=np.float32)
    x = A(g("L0_attn_x"))
    C = _host_consts()

    h_l0f = {
        "ident": C["ident"], "ident64s": C["ident64s"],
        "We": A(g("L0_ffn_We").reshape(20 * 1024, 1024)),
        "be": A(g("L0_ffn_be").reshape(-1, 1)),
        "Wq": A(g("L0_ffn_Wq").reshape(6 * 1024, 1024)),
        "Wk": A(g("L0_ffn_Wk").reshape(6 * 1024, 1024)),
        "Wv": A(g("L0_ffn_Wv").reshape(6 * 1024, 1024)),
        "Wo": A(g("L0_ffn_Wo").reshape(6 * 1024, 1024)),
        "ew_bc": A(np.broadcast_to(g("L0_ffn_ew").reshape(1, 18), (128, 18))),
        "actw_bc": A(np.broadcast_to(g("L0_ffn_act_w").reshape(1, 6), (128, 6))),
        "lng_bc": A(np.broadcast_to(g("L0_ffn_ln_g"), (128, D))),
        "lnb_bc": A(np.broadcast_to(g("L0_ffn_ln_b"), (128, D))),
    }
    h_l0a = {
        "ident": C["ident"], "ident64s": C["ident64s"],
        "rope_pi": C["rope_pi"], "rope_cos": C["rope_cos"],
        "rope_sin": C["rope_sin"],
        "ln_gamma_bc": A(np.broadcast_to(g("L0_attn_ln_gamma"), (128, D))),
        "ln_beta_bc": A(np.broadcast_to(g("L0_attn_ln_beta"), (128, D))),
        "w_qkv": g("L0_attn_w_qkv"), "w_out": g("L0_attn_w_out"),
        "b_out": A(g("L0_attn_b_out").reshape(D, 1)),
    }
    h_l1f = {
        "ident": C["ident"], "gate_w": g("L1_ffn_gate_w"),
        "w1": A(g("L1_ffn_w1").reshape(8 * 1024, 512)),
        "w3": A(g("L1_ffn_w3").reshape(8 * 1024, 512)),
        "w2": A(g("L1_ffn_w2").reshape(8 * 512, 1024)),
    }
    h_l1a = {
        "ident": C["ident"],
        "in_projT": A(g("L1_attn_in_proj_w").T),
        "out_projT": A(g("L1_attn_out_proj_w").T),
        "f_conv_w": A(g("L1_attn_conv1d_w")[:, 0, :]),
        "f_conv_b": A(g("L1_attn_conv1d_bias").reshape(-1, 1)),
        "f_x_projT": A(g("L1_attn_x_proj_w").T),
        "f_dt_projT": A(g("L1_attn_dt_proj_w").T),
        "f_dt_b": A(g("L1_attn_dt_proj_b").reshape(-1, 1)),
        "f_A_log": g("L1_attn_A_log"),
        "f_Dp": A(g("L1_attn_D").reshape(-1, 1)),
        "b_conv_w": A(g("L1_attn_conv1db_w")[:, 0, :]),
        "b_conv_b": A(g("L1_attn_conv1db_bias").reshape(-1, 1)),
        "b_x_projT": A(g("L1_attn_x_projb_w").T),
        "b_dt_projT": A(g("L1_attn_dt_projb_w").T),
        "b_dt_b": A(g("L1_attn_dt_projb_b").reshape(-1, 1)),
        "b_A_log": A(g("L1_attn_A_b_log")),
        "b_Dp": A(g("L1_attn_D_b").reshape(-1, 1)),
        "fft_fre": C["fft_fre"], "fft_fim": C["fft_fim"],
        "fft_fimn": C["fft_fimn"],
        "fft_b": A(g("L1_attn_fft_conv_b").reshape(-1, 1)),
    }
    fcw = g("L1_attn_fft_conv_w")
    for t_ in range(5):
        h_l1a["fftw_t%d" % t_] = A(fcw[:, :, t_].T)

    xs = [x[i] for i in range(8)]
    if not FUSED:
        xs = _launch(lambda p, xd, od, w: sub_l0ffn(p, xd, od, w, tag="lf"),
                     h_l0f, xs)
        xs = _launch(sub_l0attn, h_l0a, xs)
        xs = _launch(lambda p, xd, od, w: sub_l0ffn(p, xd, od, w, tag="lf"),
                     h_l0f, xs)
        xs = _launch(lambda p, xd, od, w: sub_l1ffn(p, xd, od, w), h_l1f, xs)
        xs = _launch(lambda p, xd, od, w: sub_l1attn(p, xd, od, w), h_l1a, xs)
        xs = _launch(lambda p, xd, od, w: sub_l1ffn(p, xd, od, w), h_l1f, xs)
        return np.stack([np.asarray(o, dtype=np.float32) for o in xs])

    host = {}
    wmaps = {}
    for pre, h in (("A", h_l0f), ("B", h_l0a), ("C", h_l1f), ("D", h_l1a)):
        wmaps[pre] = {}
        for k, a in h.items():
            a = np.asarray(a)
            key = pre + "_" + k
            if k in ("ident", "ident64s", "rope_pi", "rope_cos", "rope_sin",
                     "fft_fre", "fft_fim", "fft_fimn"):
                key = "K_" + k
            if key not in host:
                host[key] = a
            wmaps[pre][k] = key

    p = Prog()
    nc = p.nc
    xd = nc.dram_tensor("x", [T, D], F32, kind="ExternalInput").ap()
    od = nc.dram_tensor("out", [T, D], F32, kind="ExternalOutput").ap()
    dr = {k: nc.dram_tensor(k, list(np.asarray(a).shape), F32,
                            kind="ExternalInput").ap() for k, a in host.items()}
    s0 = nc.dram_tensor("scr0", [T, D], F32, kind="Internal").ap()
    s1 = nc.dram_tensor("scr1", [T, D], F32, kind="Internal").ap()

    def wview(pre):
        return {k: dr[key] for k, key in wmaps[pre].items()}
    sub_l0ffn(p, xd, s0, wview("A"), tag="lf")
    sub_l0attn(p, s0, s1, wview("B"))
    sub_l0ffn(p, s1, s0, wview("A"), tag="lf")
    sub_l1ffn(p, s0, s1, wview("C"))
    sub_l1attn(p, s1, s0, wview("D"))
    sub_l1ffn(p, s0, od, wview("C"))
    p.es.close()
    hostc = dict(host)
    hostc["fconsts"] = p.const_array()
    in_maps = [dict(hostc, x=np.ascontiguousarray(x[i])) for i in range(8)]
    res = run_bass_kernel_spmd(nc, in_maps, core_ids=list(range(8)))
    return np.stack([np.asarray(res.results[i]["out"], dtype=np.float32)
                     for i in range(8)])


# version 07
# speedup vs baseline: 1.3903x; 1.1528x over previous
import numpy as np
from contextlib import ExitStack, contextmanager

import concourse.bass as bass
import concourse.mybir as mybir
from concourse.bass_utils import run_bass_kernel_spmd

F32 = mybir.dt.float32
BF16 = mybir.dt.bfloat16
AF = mybir.ActivationFunctionType
ALU = mybir.AluOpType
AX = mybir.AxisListType

T = 2048
D = 1024
DC = D // 128
NT = T // 128
NQ = T // 512


NCONST = 64


class Prog:
    def __init__(self, detect_races=True):
        self.nc = bass.Bass("TRN2", target_bir_lowering=False,
                            detect_race_conditions=detect_races)
        self.es = ExitStack()
        self._stack = [self.es]
        self._sems = {}
        self._dma_sems = set()
        self._uid = 0
        self._cvals = []
        self._cdram = self.nc.dram_tensor("fconsts", [128, NCONST], F32,
                                          kind="ExternalInput").ap()
        self._csb = self.sbuf([128, NCONST], F32, "fconsts")
        self.load(self._csb[:], self._cdram[:], "cst")
        for e in (self.nc.tensor, self.nc.vector, self.nc.scalar,
                  self.nc.gpsimd, self.nc.sync):
            self.wait(e, "cst", 16)

    def constf(self, val):
        val = float(val)
        if val not in self._cvals:
            assert len(self._cvals) < NCONST
            self._cvals.append(val)
            idx = self._cvals.index(val)
            self.nc.const_aps.register(F32, val, self._csb[:, idx:idx + 1])
        return val

    def const_array(self):
        arr = np.zeros((128, NCONST), np.float32)
        arr[:, :len(self._cvals)] = np.asarray(self._cvals, np.float32)[None, :]
        return arr

    def name(self, prefix):
        self._uid += 1
        return f"{prefix}{self._uid}"

    def sbuf(self, shape, dt, name="t"):
        return self._stack[-1].enter_context(
            self.nc.sbuf_tensor(self.name(name), list(shape), dt))

    def psum(self, shape, dt=F32, name="p"):
        return self._stack[-1].enter_context(
            self.nc.psum_tensor(self.name(name), list(shape), dt))

    @contextmanager
    def scope(self):
        es = ExitStack()
        self._stack.append(es)
        try:
            yield
        finally:
            self.barrier()
            es.close()
            self._stack.pop()

    def _sem(self, sname):
        if sname not in self._sems:
            h = self.es.enter_context(self.nc.semaphore(self.name("s" + sname)))
            self._sems[sname] = [h, 0]
        return self._sems[sname]

    def inc(self, instr, sname, n=1):
        s = self._sem(sname)
        s[1] += n
        instr.then_inc(s[0], n)
        return s[1]

    def tail(self, sname):
        return self._sem(sname)[1]

    def wait(self, eng, sname, val=None):
        s = self._sem(sname)
        v = s[1] if val is None else val
        if v > 0:
            eng.wait_ge(s[0], v)

    def barrier(self):
        nc = self.nc
        engs = [nc.tensor, nc.vector, nc.scalar, nc.sync, nc.gpsimd]
        s = self._sem("bar")
        base = s[1]
        dmas = [(self._sems[n][0], self._sems[n][1])
                for n in self._dma_sems
                if n in self._sems and self._sems[n][1] > 0]
        for e in engs:
            for h, v in dmas:
                e.wait_ge(h, v)
            e.drain()
            self.inc(e.nop(), "bar", 1)
        for e in engs:
            e.wait_ge(s[0], base + len(engs))

    def load(self, out_ap, in_ap, sname="ld"):
        self._dma_sems.add(sname)
        return self.inc(self.nc.sync.dma_start(out=out_ap, in_=in_ap), sname, 16)

    def loadc(self, out_ap, in_ap, sname="ldg"):
        self._dma_sems.add(sname)
        return self.inc(self.nc.gpsimd.dma_start(out=out_ap, in_=in_ap), sname, 16)

    def store(self, out_ap, in_ap, sname="st"):
        self._dma_sems.add(sname)
        return self.inc(self.nc.sync.dma_start(out=out_ap, in_=in_ap), sname, 16)


def load_w(p, w_dram, K, M, sname="ldg"):
    wsb = p.sbuf([128, (K // 128) * M], BF16, "w")
    tok = None
    for kc in range(K // 128):
        tok = p.loadc(wsb[:, kc * M:(kc + 1) * M],
                      w_dram[kc * 128:(kc + 1) * 128, :], sname)
    return wsb, tok


def w_tile(wsb, M, kc, j):
    return wsb[:, kc * M + j * 128: kc * M + (j + 1) * 128]


def ln_to_xT(p, x_dram, xT, identb, eps=1e-5, gain=None, bias_=None,
             nchunk=DC, tag="ln"):
    nc = p.nc
    tag = "ln"
    s_ld, s_mu, s_xc, s_t, s_rstd, s_xn, s_mm, s_ev = (
        tag + k for k in ("ld", "mu", "xc", "t", "rstd", "xn", "mm", "ev"))
    B = {s: p.tail(s) for s in (s_xc, s_t, s_rstd, s_xn, s_mm, s_ev,
                                s_xn + "m", s_rstd + "w")}
    p._lnscope = p.scope()
    p._lnscope.__enter__()
    xt = [p.sbuf([128, D], F32, "lnx"), p.sbuf([128, D], F32, "lnx")]
    xc = p.sbuf([128, D], F32, "lnxc")
    tmid = p.sbuf([128, D], F32, "lnmid")
    xnb = p.sbuf([128, D], BF16, "lnxn")
    sq = p.sbuf([128, D], F32, "lnsq")
    s1 = p.sbuf([128, 1], F32, "s1")
    s2 = p.sbuf([128, 1], F32, "s2")
    mu = p.sbuf([128, 1], F32, "mu")
    lnv = p.sbuf([128, 1], F32, "lnv")
    rstd = p.sbuf([128, 1], F32, "rstd")
    tp = p.psum([128, D], BF16, "tp")
    npass = 2 if gain is not None else 1

    def one_pass(src_buf, dst_buf, k):
        nc.vector.drain()
        p.inc(nc.vector.reduce_sum(out=s1[:], in_=src_buf[:], axis=AX.X), s_mu)
        nc.vector.drain()
        nc.vector.tensor_scalar_mul(mu[:], s1[:], 1.0 / D)
        nc.vector.drain()
        p.inc(nc.vector.tensor_single_scalar(out=xc[:], in_=src_buf[:],
                                             scalar=mu[:], op=ALU.subtract), s_xc)
        p.wait(nc.scalar, s_xc, B[s_xc] + k + 1)
        p.wait(nc.scalar, s_rstd + "w", B[s_rstd + "w"] + k)
        p.inc(nc.scalar.activation(out=sq[:], in_=xc[:], func=AF.Square,
                                   accum_out=s2[:]), s_t)
        p.wait(nc.scalar, s_t)
        nc.scalar.activation(out=lnv[:], in_=s2[:], func=AF.Ln,
                             scale=1.0 / D, bias=p.constf(eps))
        nc.scalar.drain()
        p.inc(nc.scalar.activation(out=rstd[:], in_=lnv[:], func=AF.Exp,
                                   scale=-0.5), s_rstd)
        p.wait(nc.vector, s_rstd, B[s_rstd] + k + 1)
        nc.vector.drain()
        i = nc.vector.tensor_scalar_mul(dst_buf[:], xc[:], rstd[:])
        p.inc(i, s_rstd + "w")
        return i

    for tt in range(NT):
        buf = xt[tt % 2]
        par = s_ld + str(tt % 2)
        p.wait(nc.sync, s_xn, B[s_xn] + tt - 1)
        lt = p.load(buf[:], x_dram[tt * 128:(tt + 1) * 128, :], par)
        p.wait(nc.vector, par, lt)
        p.wait(nc.vector, s_mm, B[s_mm] + tt * nchunk)
        if gain is None:
            i5 = one_pass(buf, tmid, tt)
            nc.vector.drain()
            i5 = nc.vector.tensor_copy(out=xnb[:], in_=tmid[:])
        else:
            one_pass(buf, tmid, 2 * tt)
            one_pass(tmid, xc, 2 * tt + 1)
            nc.vector.drain()
            nc.vector.tensor_tensor(out=xnb[:], in0=xc[:], in1=gain[:],
                                    op=ALU.mult)
            nc.vector.drain()
            i5 = nc.vector.tensor_tensor(out=xnb[:], in0=xnb[:], in1=bias_[:],
                                         op=ALU.add)
        p.inc(i5, s_xn)
        p.wait(nc.tensor, s_xn, B[s_xn] + tt + 1)
        p.wait(nc.tensor, s_ev, B[s_ev] + tt)
        it = None
        for j in range(nchunk):
            it = nc.tensor.transpose(tp[:, j * 128:(j + 1) * 128],
                                     xnb[:, j * 128:(j + 1) * 128], identb[:])
        p.inc(it, s_mm, nchunk)
        p.wait(nc.vector, s_mm, B[s_mm] + (tt + 1) * nchunk)
        dst = xT[:, :].rearrange("p (c t) -> p c t", c=nchunk)[
            :, :, tt * 128:(tt + 1) * 128]
        srcv = tp[:, :].rearrange("p (c t) -> p c t", c=nchunk)
        p.inc(nc.vector.tensor_copy(out=dst, in_=srcv), s_ev)
    p._lnscope.__exit__(None, None, None)
    return p.tail(s_ev)


def mm_ff(p, wsb, K, M, rhsT, evac, rhs_T=T, tag="mm", pspool=None):
    nc = p.nc
    KC = K // 128
    nq = rhs_T // 512
    s_mm, s_ev = tag + "m", tag + "e"
    if pspool is None:
        NSL = 4
        ps = [p.psum([128, 512], F32, "mmps") for _ in range(NSL)]
        slot_tok = [0] * NSL
    else:
        ps, slot_tok = pspool
        NSL = len(ps)
    si = 0
    for j in range(M // 128):
        for n in range(nq):
            if slot_tok[si]:
                p.wait(nc.tensor, s_ev, slot_tok[si])
            im = None
            for kc in range(KC):
                im = nc.tensor.matmul(
                    ps[si][:], w_tile(wsb, M, kc, j),
                    rhsT[:, kc * rhs_T + n * 512:kc * rhs_T + (n + 1) * 512],
                    start=(kc == 0), stop=(kc == KC - 1))
            mtok = p.inc(im, s_mm)
            ie = evac(j, n, ps[si], mtok)
            slot_tok[si] = p.inc(ie, s_ev)
            si = (si + 1) % NSL
    return p.tail(s_ev)


def make_att_bufs(p, dh, alias=None):
    nc = p.nc
    KCH = T // 128
    hpp = 128 // dh
    if alias is not None:
        assert hpp == 1
        pt = [alias[:, 0:KCH * 512]]
        vp = [alias[:, KCH * 512:KCH * (512 + dh)]]
    else:
        pt = [p.sbuf([128, KCH * 512], BF16, "PT") for _ in range(hpp)]
        vp = [p.sbuf([128, KCH * dh], BF16, "Vp") for _ in range(hpp)]
    d = {
        "PTb": pt,
        "Vp": vp,
        "vtp": p.psum([128, 512], BF16, "vtp"),
        "sps": [p.psum([128, 512], F32, "sps"), p.psum([128, 512], F32, "sps")],
        "cps": p.psum([128, 512], F32, "cps"),
        "sm": [p.psum([1, 512], F32, "sm") for _ in range(hpp)],
        "rs": [p.sbuf([1, 512], F32, "rs") for _ in range(hpp)],
        "rsb": [p.sbuf([1, 512], BF16, "rsb") for _ in range(hpp)],
        "bcp": p.psum([128, 512], F32, "bcp"),
        "bcs": p.sbuf([128, 512], F32, "bcs"),
        "ones1": p.sbuf([1, 128], BF16, "ones1"),
        "onesk": p.sbuf([128, 1], BF16, "onesk"),
    }
    nc.vector.memset(d["ones1"][:], 1.0)
    p.inc(nc.vector.memset(d["onesk"][:], 1.0), "atve")
    p.wait(nc.tensor, "atve", p.tail("atve"))
    return d


def attention_c(p, qT, kT, vT, ctxT_out, H, dh, scale, identb, identq=None,
                shift=12.0, tag=None, shared=None):
    tag = "at"
    """identq: [128, dh] identity stack (eye(dh) tiled) for dh<128 transposes."""
    """Scheme-C attention. qT/kT/vT/ctxT_out: sbuf bf16 [128, (H*dh/128)*T]."""
    nc = p.nc
    KCH = T // 128
    hpp = 128 // dh
    s_vt, s_ve, s_sm, s_pe, s_cm, s_su, s_rs, s_bm, s_be, s_ce = (
        tag + k for k in ("vt", "ve", "sm", "pe", "cm", "su", "rs", "bm", "be",
                          "ce"))
    own = shared is None
    if own:
        _sc = p.scope()
        _sc.__enter__()
        shared = make_att_bufs(p, dh)
    if True:
        PTb, Vp, vtp, sps, cps, sm, rs, rsb, bcp, bcs, ones1, onesk = (
            shared["PTb"], shared["Vp"], shared["vtp"], shared["sps"],
            shared["cps"], shared["sm"], shared["rs"], shared["rsb"],
            shared["bcp"], shared["bcs"], shared["ones1"], shared["onesk"])

        vslots = 512 // dh
        for hp in range(H // hpp):
            heads = [hp * hpp + i for i in range(hpp)]
            for i, h in enumerate(heads):
                c0, r0 = (h * dh) // 128, (h * dh) % 128
                for g in range(KCH // vslots):
                    p.wait(nc.tensor, s_ve, p.tail(s_ve))
                    it = None
                    for s in range(vslots):
                        kc = g * vslots + s
                        idq = identb if dh == 128 else identq
                        it = nc.tensor.transpose(
                            vtp[:, s * dh:(s + 1) * dh],
                            vT[r0:r0 + dh,
                               c0 * T + kc * 128: c0 * T + (kc + 1) * 128],
                            idq[r0:r0 + dh, 0:dh])
                    vt = p.inc(it, s_vt)
                    p.wait(nc.vector, s_vt, vt)
                    p.inc(nc.vector.tensor_copy(
                        out=Vp[i][:, g * 512:(g + 1) * 512], in_=vtp[:]), s_ve)
            for n in range(NQ):
                for i, h in enumerate(heads):
                    c0, r0 = (h * dh) // 128, (h * dh) % 128
                    p.wait(nc.scalar, s_su, p.tail(s_su))
                    si = 0
                    for kc in range(KCH):
                        if p.tail(s_pe) >= 2:
                            p.wait(nc.tensor, s_pe, p.tail(s_pe) - 1)
                        im = nc.tensor.matmul(
                            sps[si][:],
                            kT[r0:r0 + dh,
                               c0 * T + kc * 128: c0 * T + (kc + 1) * 128],
                            qT[r0:r0 + dh,
                               c0 * T + n * 512: c0 * T + (n + 1) * 512],
                            start=True, stop=True)
                        tm = p.inc(im, s_sm)
                        p.wait(nc.scalar, s_sm, tm)
                        p.inc(nc.scalar.activation(
                            out=PTb[i][:, kc * 512:(kc + 1) * 512],
                            in_=sps[si][:], func=AF.Exp, scale=float(scale),
                            bias=p.constf(-shift)), s_pe)
                        si ^= 1
                    p.wait(nc.tensor, s_pe, p.tail(s_pe))
                    p.wait(nc.tensor, s_ce, p.tail(s_ce))
                    im = None
                    for kc in range(KCH):
                        im = nc.tensor.matmul(
                            cps[i * dh:(i + 1) * dh, :],
                            Vp[i][:, kc * dh:(kc + 1) * dh],
                            PTb[i][:, kc * 512:(kc + 1) * 512],
                            start=(kc == 0), stop=(kc == KCH - 1))
                    p.inc(im, s_cm)
                    p.wait(nc.tensor, s_rs, p.tail(s_rs))
                    im = None
                    for kc in range(KCH):
                        im = nc.tensor.matmul(
                            sm[i][:], onesk[:, 0:1],
                            PTb[i][:, kc * 512:(kc + 1) * 512],
                            start=(kc == 0), stop=(kc == KCH - 1))
                    sut = p.inc(im, s_su)
                    p.wait(nc.vector, s_su, sut)
                    p.wait(nc.vector, s_bm, p.tail(s_bm))
                    rt = p.inc(nc.vector.reciprocal(
                        out=rs[i][:], in_=sm[i][:]), s_rs)
                    p.wait(nc.scalar, s_rs, rt)
                    p.wait(nc.scalar, s_bm, p.tail(s_bm))
                    p.inc(nc.scalar.activation(out=rsb[i][:], in_=rs[i][:],
                                               func=AF.Copy), s_rs + "c")
                p.wait(nc.tensor, s_rs + "c", p.tail(s_rs + "c"))
                p.wait(nc.tensor, s_be, p.tail(s_be))
                ib = None
                for i in range(hpp):
                    ib = nc.tensor.matmul(bcp[i * dh:(i + 1) * dh, :],
                                          ones1[0:1, 0:dh], rsb[i][:],
                                          start=True, stop=True)
                bt = p.inc(ib, s_bm)
                p.wait(nc.vector, s_bm, bt)
                p.inc(nc.vector.tensor_copy(out=bcs[:], in_=bcp[:]), s_be)
                nc.vector.drain()
                p.wait(nc.vector, s_cm, p.tail(s_cm))
                c0 = (heads[0] * dh) // 128
                p.inc(nc.vector.tensor_tensor(
                    out=ctxT_out[:, c0 * T + n * 512: c0 * T + (n + 1) * 512],
                    in0=cps[:], in1=bcs[:], op=ALU.mult), s_ce)
    if own:
        _sc.__exit__(None, None, None)


def residual_store(p, srcT, x_dram, out_dram, identf, scale_src=1.0, tag="res"):
    nc = p.nc
    s_ld, s_mm, s_ev, s_st = "lnld0", "lnmm", "lnev", "lnst"
    B = {s: p.tail(s) for s in (s_mm, s_ev, s_st)}
    rp = p.psum([128, 1024], F32, "rp")
    xtile = p.sbuf([128, D], F32, "rx")
    otile = p.sbuf([128, D], F32, "ro")
    for tt in range(NT):
        p.wait(nc.sync, s_ev, B[s_ev] + tt)
        lt = p.load(xtile[:], x_dram[tt * 128:(tt + 1) * 128, :], s_ld)
        p.wait(nc.tensor, s_ev, B[s_ev] + tt)
        it = None
        for j in range(DC):
            it = nc.tensor.transpose(
                rp[:, j * 128:(j + 1) * 128],
                srcT[:, j * T + tt * 128: j * T + (tt + 1) * 128], identf[:])
        mt = p.inc(it, s_mm)
        p.wait(nc.vector, s_mm, mt)
        p.wait(nc.vector, s_ld, lt)
        p.wait(nc.vector, s_st, B[s_st] + tt * 16)
        ia = nc.vector.scalar_tensor_tensor(out=otile[:], in0=rp[:],
                                            scalar=float(scale_src), in1=xtile[:],
                                            op0=ALU.mult, op1=ALU.add)
        ev = p.inc(ia, s_ev)
        p.wait(nc.sync, s_ev, ev)
        p.store(out_dram[tt * 128:(tt + 1) * 128, :], otile[:], s_st)
    p.wait(nc.sync, s_st)


def sub_l0attn(p, x_dram, out_dram, w):
    nc = p.nc
    H, dh = 16, 64
    with p.scope():
        identb = p.sbuf([128, 128], BF16, "idb")
        identf = p.sbuf([128, 128], F32, "idf")
        identq = p.sbuf([128, 64], BF16, "idq")
        p.loadc(identb[:], w["ident"][:])
        p.load(identf[:], w["ident"][:])
        p.loadc(identq[:], w["ident64s"][:])
        p.barrier()

        ctxT = p.sbuf([128, 8 * T], BF16, "ctxT")
        with p.scope():
            qkvT = p.sbuf([128, 24 * T], BF16, "qkvT")
            with p.scope():
                xT = p.sbuf([128, DC * T], BF16, "xT")
                gainb = p.sbuf([128, D], BF16, "gain")
                biasb = p.sbuf([128, D], BF16, "bias")
                p.loadc(gainb[:], w["ln_gamma_bc"][:])
                p.loadc(biasb[:], w["ln_beta_bc"][:])
                p.barrier()
                ln_to_xT(p, x_dram, xT, identb, eps=1e-5, gain=gainb, bias_=biasb)
                p.barrier()

                def evac_qkv(half):
                    def f(j, n, ps, mtok):
                        jj = j + 12 * half
                        p.wait(nc.scalar, "mmm", mtok)
                        return nc.scalar.activation(
                            out=qkvT[:, jj * T + n * 512:jj * T + (n + 1) * 512],
                            in_=ps[:], func=AF.Copy)
                    return f
                for half in range(2):
                    with p.scope():
                        wq, _ = load_w(p, w["w_qkv"][:, half * 1536:(half + 1) * 1536],
                                       1024, 1536)
                        p.barrier()
                        mm_ff(p, wq, 1024, 1536, xT, evac_qkv(half))

            qT = qkvT[:, 0 * T: 8 * T]
            kT = qkvT[:, 8 * T: 16 * T]
            vT = qkvT[:, 16 * T: 24 * T]

            with p.scope():
                pib = p.sbuf([128, 128], BF16, "pib")
                cosF = p.sbuf([128, T], BF16, "cosF")
                sinF = p.sbuf([128, T], BF16, "sinF")
                p.loadc(pib[:], w["rope_pi"][:])
                p.loadc(cosF[:], w["rope_cos"][:])
                p.loadc(sinF[:], w["rope_sin"][:])
                p.barrier()
                swp = p.psum([128, 512], F32, "swp")
                sws = p.sbuf([128, 512], BF16, "sws")
                tmp = p.sbuf([128, 512], BF16, "ropet")
                for base in (0, 8 * T):
                    for c in range(8):
                        for n in range(NQ):
                            sl = slice(base + c * T + n * 512,
                                       base + c * T + (n + 1) * 512)
                            p.wait(nc.tensor, "rpsw", p.tail("rpsw"))
                            im = nc.tensor.matmul(swp[:], pib[:], qkvT[:, sl],
                                                  start=True, stop=True)
                            mt = p.inc(im, "rpmm")
                            p.wait(nc.scalar, "rpmm", mt)
                            p.wait(nc.scalar, "rpev", p.tail("rpev"))
                            isw = nc.scalar.activation(out=sws[:], in_=swp[:],
                                                       func=AF.Copy)
                            swt = p.inc(isw, "rpsw")
                            p.wait(nc.vector, "rpsw", swt)
                            nc.vector.tensor_tensor(
                                out=tmp[:], in0=sws[:],
                                in1=sinF[:, n * 512:(n + 1) * 512], op=ALU.mult)
                            nc.vector.tensor_tensor(
                                out=qkvT[:, sl], in0=qkvT[:, sl],
                                in1=cosF[:, n * 512:(n + 1) * 512], op=ALU.mult)
                            nc.vector.drain()
                            p.inc(nc.vector.tensor_tensor(
                                out=qkvT[:, sl], in0=qkvT[:, sl], in1=tmp[:],
                                op=ALU.add), "rpev")

            attention_c(p, qT, kT, vT, ctxT, H, dh, dh ** -0.5, identb, identq)

        with p.scope():
            oT = p.sbuf([128, DC * T], F32, "oT")
            wo, _ = load_w(p, w["w_out"], 1024, 1024)
            bout = p.sbuf([128, DC], F32, "bout")
            for j in range(DC):
                p.load(bout[:, j:j + 1], w["b_out"][j * 128:(j + 1) * 128, :], "ldb")
            p.barrier()

            def evac_o(j, n, ps, mtok):
                p.wait(nc.vector, "mmm", mtok)
                p.wait(nc.vector, "ldb", p.tail("ldb"))
                return nc.vector.tensor_scalar_add(
                    oT[:, j * T + n * 512:j * T + (n + 1) * 512],
                    ps[:], bout[:, j:j + 1])
            mm_ff(p, wo, 1024, 1024, ctxT, evac_o)
            p.barrier()
            residual_store(p, oT, x_dram, out_dram, identf, scale_src=1.0)


def rope_tables(n=T, head_dim=64):
    inv_freq = 1.0 / 10000.0 ** (np.arange(0, head_dim, 2, dtype=np.float64) / head_dim)
    pos = np.arange(n, dtype=np.float64)
    ang = pos[:, None] * inv_freq[None, :]
    sin, cos = np.sin(ang), np.cos(ang)
    cosF = np.zeros((128, n), np.float32)
    sinF = np.zeros((128, n), np.float32)
    for r in range(128):
        i = r % 64
        f = i % 32
        cosF[r] = cos[:, f]
        sinF[r] = sin[:, f] if i >= 32 else -sin[:, f]
    pi = np.zeros((128, 128), np.float32)
    for r in range(128):
        i = r % 64
        partner = r + 32 if i < 32 else r - 32
        pi[partner, r] = 1.0
    return cosF, sinF, pi


def l0attn_consts():
    cosF, sinF, pi = rope_tables()
    return {
        "ident": np.eye(128, dtype=np.float32),
        "ident64s": np.ascontiguousarray(np.tile(np.eye(64, dtype=np.float32), (2, 1))),
        "rope_pi": pi, "rope_cos": cosF, "rope_sin": sinF,
    }


L0F_EDGES = {0: (0, 0, 0), 1: (2, 1, 2), 2: (5, 3, 4),
             3: (9, 6, 8), 4: (14, 10, 12), 5: (19, 15, 17)}
L0F_INS = {0: (-1, -1, -1), 1: (0, -1, 0), 2: (1, -1, 0),
           3: (2, -1, 1), 4: (3, -1, 1), 5: (4, 0, 2)}


def ln_residual_store(p, srcT, x_dram, out_dram, identf, gain, bias_,
                      eps, scale_src, tag="lrs", src_bf16=False, identb=None):
    nc = p.nc
    s_ld, s_mm, s_xc, s_rstd, s_ev, s_st = (
        "lnld0", "lnmm", "lnxc", "lnrstd", "lnev", "lnst")
    B = {s: p.tail(s) for s in (s_mm, s_xc, s_rstd, s_ev, s_st,
                                s_rstd + "w")}
    rp_ps = p.psum([128, 1024], BF16 if src_bf16 else F32, "rp")
    rpf = p.sbuf([128, D], F32, "rpf") if src_bf16 else None
    xtile = p.sbuf([128, D], F32, "rx")
    xc = p.sbuf([128, D], F32, "rxc")
    sq = p.sbuf([128, D], F32, "rsq")
    otile = p.sbuf([128, D], F32, "ro")
    s1 = p.sbuf([128, 1], F32, "s1")
    s2 = p.sbuf([128, 1], F32, "s2")
    mu = p.sbuf([128, 1], F32, "mu")
    lnv = p.sbuf([128, 1], F32, "lnv")
    rstd = p.sbuf([128, 1], F32, "rstd")
    for tt in range(NT):
        p.wait(nc.sync, s_ev, B[s_ev] + tt)
        lt = p.load(xtile[:], x_dram[tt * 128:(tt + 1) * 128, :], s_ld)
        p.wait(nc.tensor, s_xc, B[s_xc] + tt)
        it = None
        for j in range(DC):
            it = nc.tensor.transpose(
                rp_ps[:, j * 128:(j + 1) * 128],
                srcT[:, j * T + tt * 128: j * T + (tt + 1) * 128],
                identb[:] if src_bf16 else identf[:])
        mt = p.inc(it, s_mm)
        p.wait(nc.vector, s_mm, mt)
        if src_bf16:
            nc.vector.tensor_copy(out=rpf[:], in_=rp_ps[:])
            nc.vector.drain()
            rp = rpf
        else:
            rp = rp_ps
        nc.vector.reduce_sum(out=s1[:], in_=rp[:], axis=AX.X)
        nc.vector.drain()
        nc.vector.tensor_scalar_mul(mu[:], s1[:], 1.0 / D)
        nc.vector.drain()
        p.inc(nc.vector.tensor_single_scalar(out=xc[:], in_=rp[:],
                                             scalar=mu[:], op=ALU.subtract), s_xc)
        p.wait(nc.scalar, s_xc, B[s_xc] + tt + 1)
        p.wait(nc.scalar, s_rstd + "w", B[s_rstd + "w"] + tt)
        p.inc(nc.scalar.activation(out=sq[:], in_=xc[:], func=AF.Square,
                                   accum_out=s2[:]), "lnt")
        p.wait(nc.scalar, "lnt")
        nc.scalar.activation(out=lnv[:], in_=s2[:], func=AF.Ln,
                             scale=1.0 / D, bias=p.constf(eps))
        nc.scalar.drain()
        p.inc(nc.scalar.activation(out=rstd[:], in_=lnv[:], func=AF.Exp,
                                   scale=-0.5), s_rstd)
        p.wait(nc.vector, s_rstd, B[s_rstd] + tt + 1)
        p.wait(nc.vector, s_ld, lt)
        p.wait(nc.vector, s_st, B[s_st] + tt * 16)
        nc.vector.drain()
        nc.vector.tensor_scalar_mul(otile[:], xc[:], rstd[:])
        nc.vector.drain()
        nc.vector.tensor_tensor(out=otile[:], in0=otile[:], in1=gain[:], op=ALU.mult)
        nc.vector.drain()
        p.inc(nc.vector.tensor_tensor(out=otile[:], in0=otile[:], in1=bias_[:],
                                      op=ALU.add), s_rstd + "w")
        nc.vector.drain()
        iv = nc.vector.scalar_tensor_tensor(out=otile[:], in0=otile[:],
                                            scalar=float(scale_src), in1=xtile[:],
                                            op0=ALU.mult, op1=ALU.add)
        ev = p.inc(iv, s_ev)
        p.wait(nc.sync, s_ev, ev)
        p.store(out_dram[tt * 128:(tt + 1) * 128, :], otile[:], s_st)
    p.wait(nc.sync, s_st)


def sub_l0ffn(p, x_dram, out_dram, w, tag):
    nc = p.nc
    H, dh = 8, 128

    def Wsl(name, e):
        return w[name][e * 1024:(e + 1) * 1024, :]

    with p.scope():
        identb = p.sbuf([128, 128], BF16, "idb")
        identf = p.sbuf([128, 128], F32, "idf")
        ewb = p.sbuf([128, 18], F32, "ewb")
        awb = p.sbuf([128, 6], F32, "awb")
        p.loadc(identb[:], w["ident"][:])
        p.load(identf[:], w["ident"][:])
        p.load(ewb[:], w["ew_bc"][:])
        p.load(awb[:], w["actw_bc"][:])
        nodes_d = {c: nc.dram_tensor(p.name(f"{tag}nd{c}"), [128, DC * T], BF16,
                                     kind="Internal").ap()
                   for c in range(-1, 6)}
        p.barrier()

        with p.scope():
            xT = p.sbuf([128, DC * T], BF16, "xT")
            ln_to_xT(p, x_dram, xT, identb, eps=1e-5, gain=None, tag=tag + "ln")
            p.barrier()
            p.store(nodes_d[-1][:], xT[:], tag + "nst")
            p.wait(nc.sync, tag + "nst")
        p.barrier()

        MK = p.sbuf([128, DC * T], BF16, "sMK")

        with p.scope():
            TMP = p.sbuf([128, DC * T], BF16, "sTMP")
            Q = p.sbuf([128, DC * T], BF16, "sQ")
            K = p.sbuf([128, DC * T], BF16, "sK")
            MQ = p.sbuf([128, DC * T], BF16, "sMQ")
            MV = TMP
            CTX = MQ
            bufs = {"Q": Q, "K": K, "MQ": MQ, "MK": MK, "MV": MV, "CTX": CTX}
            wpool = [p.sbuf([128, DC * 1024], BF16, "wpool") for _ in range(2)]
            bpool = [p.sbuf([128, DC], F32, "bpool") for _ in range(2)]
            tsc = p.sbuf([128, 512], BF16, "tsc")
            attb = make_att_bufs(p, dh, alias=K)
            mmpool = ([p.psum([128, 512], F32, "mmp") for _ in range(2)], [0, 0])

            stages = []
            for c in range(6):
                eq, ek, ev = L0F_EDGES[c]
                iq, ik, iv_ = L0F_INS[c]
                ew_q, ew_k, ew_v = 3 * c + 0, 3 * c + 1, 3 * c + 2
                qsrc = ("tmp", iq) if c == 0 else ("buf", "MK")
                stages.append(dict(kind="edge", wd=Wsl("We", eq), be=eq,
                                   src=qsrc, dest="Q", node=c))
                stages.append(dict(kind="proj", wd=Wsl("Wq", c), src=("buf", "Q"),
                                   dest="MQ", ew=ew_q, node=c))
                if (ek, ik) == (eq, iq):
                    ksrc = ("buf", "Q")
                else:
                    stages.append(dict(kind="edge", wd=Wsl("We", ek), be=ek,
                                       src=("tmp", ik), dest="K", node=c))
                    ksrc = ("buf", "K")
                stages.append(dict(kind="proj", wd=Wsl("Wk", c), src=ksrc,
                                   dest="MK", ew=ew_k, node=c))
                if (ev, iv_) == (eq, iq):
                    vsrc = ("buf", "Q")
                elif (ev, iv_) == (ek, ik):
                    vsrc = ksrc
                else:
                    stages.append(dict(kind="edge", wd=Wsl("We", ev), be=ev,
                                       src=("tmp", iv_), dest="K", node=c))
                    vsrc = ("buf", "K")
                stages.append(dict(kind="proj", wd=Wsl("Wv", c), src=vsrc,
                                   dest="MV", ew=ew_v, node=c))
                stages.append(dict(kind="att", node=c))
                stages.append(dict(kind="wo", wd=Wsl("Wo", c), src=("buf", "CTX"),
                                   dest="MK", node=c, ew_q=ew_q, last=(c == 5)))
            wlist = [st for st in stages if st["kind"] != "att"]
            for i, st in enumerate(wlist):
                st["wslot"] = i % 2
                st["wpos"] = i
            tmp_stages = [i for i, st in enumerate(stages)
                          if st.get("src", ("",))[0] == "tmp"]
            store_tok = {-1: p.tail(tag + "nst")}

            def issue_wload(st, wfree_tok):
                sl = st["wslot"]
                if wfree_tok:
                    p.wait(nc.gpsimd, "mmm", wfree_tok)
                tokl = None
                for kc in range(DC):
                    tokl = p.loadc(wpool[sl][:, kc * 1024:(kc + 1) * 1024],
                                   st["wd"][kc * 128:(kc + 1) * 128, :],
                                   "ldg")
                st["wtok"] = tokl
                if st.get("be") is not None:
                    e = st["be"]
                    p.wait(nc.sync, "mme", p.tail("mme"))
                    tb = None
                    for j in range(DC):
                        tb = p.load(
                            bpool[sl][:, j:j + 1],
                            w["be"][e * 1024 + j * 128: e * 1024 + (j + 1) * 128, :],
                            "ld")
                    st["btok"] = tb

            nxt = [0]

            def issue_tmp(wait_sem, wait_tok):
                if nxt[0] >= len(tmp_stages):
                    return
                st = stages[tmp_stages[nxt[0]]]
                n_src = st["src"][1]
                if wait_tok:
                    p.wait(nc.sync, wait_sem, wait_tok)
                p.wait(nc.sync, tag + "nst", store_tok[n_src])
                st["tmp_tok"] = p.load(TMP[:], nodes_d[n_src][:], "ld")
                nxt[0] += 1

            issue_tmp(None, 0)
            issue_wload(wlist[0], 0)
            for s_i, st in enumerate(stages):
                kind = st["kind"]
                c = st["node"]
                if kind == "att":
                    p.wait(nc.tensor, "mme", p.tail("mme"))
                    attention_c(p, MQ, MK, MV, CTX, H, dh, dh ** -0.5, identb,
                                tag=tag + f"a{c}", shared=attb)
                    if nxt[0] < len(tmp_stages) and \
                            stages[tmp_stages[nxt[0]]]["node"] > c:
                        issue_tmp("atvt", p.tail("atvt"))
                    continue
                wp = st["wpos"]
                if wp + 1 < len(wlist):
                    issue_wload(wlist[wp + 1],
                                wlist[wp - 1]["mmdone"] if wp >= 1 else 0)
                sl = st["wslot"]
                p.wait(nc.tensor, "ldg", st["wtok"])
                p.wait(nc.tensor, "mme", p.tail("mme"))
                if st["src"][0] == "tmp":
                    p.wait(nc.tensor, "ld", st["tmp_tok"])
                if kind == "edge":
                    p.wait(nc.vector, "ld", st["btok"])
                    dest = bufs[st["dest"]]

                    def evac(j, n, ps, mtok, dest=dest, beb=bpool[sl]):
                        p.wait(nc.vector, "mmm", mtok)
                        sl2 = slice(j * T + n * 512, j * T + (n + 1) * 512)
                        return nc.vector.tensor_scalar_add(dest[:, sl2], ps[:],
                                                           beb[:, j:j + 1])
                elif kind == "proj":
                    dest = bufs[st["dest"]]

                    def evac(j, n, ps, mtok, dest=dest, ewi=st["ew"]):
                        p.wait(nc.vector, "mmm", mtok)
                        sl2 = slice(j * T + n * 512, j * T + (n + 1) * 512)
                        return nc.vector.tensor_scalar_mul(
                            dest[:, sl2], ps[:], ewb[:, ewi:ewi + 1])
                else:
                    p.wait(nc.tensor, "atce", p.tail("atce"))

                    def evac(j, n, ps, mtok, c=c, ewq=st["ew_q"]):
                        sl2 = slice(j * T + n * 512, j * T + (n + 1) * 512)
                        p.wait(nc.vector, "mmm", mtok)
                        nc.vector.tensor_scalar_mul(tsc[:], ps[:],
                                                    awb[:, c:c + 1])
                        nc.vector.drain()
                        return nc.vector.scalar_tensor_tensor(
                            out=MK[:, sl2], in0=Q[:, sl2],
                            scalar=ewb[:, ewq:ewq + 1],
                            in1=tsc[:], op0=ALU.mult, op1=ALU.add)
                src_buf = TMP if st["src"][0] == "tmp" else bufs[st["src"][1]]
                mm_ff(p, wpool[sl], 1024, 1024, src_buf, evac, pspool=mmpool)
                st["mmdone"] = p.tail("mmm")
                if st["src"][0] == "tmp" and nxt[0] < len(tmp_stages) and \
                        stages[tmp_stages[nxt[0]]]["node"] == c:
                    issue_tmp("mmm", st["mmdone"])
                if kind == "wo" and not st["last"]:
                    p.wait(nc.sync, "mme", p.tail("mme"))
                    store_tok[c] = p.store(nodes_d[c][:], MK[:], tag + "nst")
        gf = p.sbuf([128, D], F32, "gf")
        bf = p.sbuf([128, D], F32, "bf")
        p.load(gf[:], w["lng_bc"][:])
        p.load(bf[:], w["lnb_bc"][:])
        p.barrier()
        ln_residual_store(p, MK, x_dram, out_dram, identf, gf, bf, 1e-6, 0.5,
                          tag=tag + "lrs", src_bf16=True, identb=identb)


def sub_l1ffn(p, x_dram, out_dram, w, tag=None):
    nc = p.nc
    tag = "mo"
    E, HD = 8, 512
    B = {s: p.tail(tag + s) for s in ("gsc", "gnm", "gex", "yz")}
    with p.scope():
        identb = p.sbuf([128, 128], BF16, "idb")
        p.loadc(identb[:], w["ident"][:])
        p.barrier()
        xT = p.sbuf([128, DC * T], BF16, "xT")
        ln_to_xT(p, x_dram, xT, identb, eps=1e-5, gain=None, tag=tag + "ln")
        p.barrier()
        if "dbg_xT" in w:
            xc32 = p.sbuf([128, T], F32, "dbgx")
            for cc in range(DC):
                p.wait(nc.vector, tag + "dbg", cc * 16)
                p.inc(nc.vector.tensor_copy(out=xc32[:], in_=xT[:, cc * T:(cc + 1) * T]), tag + "dx")
                p.wait(nc.sync, tag + "dx", cc + 1)
                p.store(w["dbg_xT"][cc * 128:(cc + 1) * 128, :], xc32[:], tag + "dbg")
            p.barrier()

        scale_sb = p.sbuf([128, NT * E], F32, "gsc")
        with p.scope():
            gwb = p.sbuf([128, E * E], BF16, "gw")
            for kc in range(DC):
                p.loadc(gwb[:, kc * E:(kc + 1) * E],
                        w["gate_w"][kc * 128:(kc + 1) * 128, :], "ldg")
            gps = p.psum([128, E], F32, "gps")
            lg = p.sbuf([128, E], F32, "lg")
            ex = p.sbuf([128, E], F32, "ex")
            sm = p.sbuf([128, E], F32, "sm")
            nmx = p.sbuf([128, 1], F32, "nmx")
            ssum = p.sbuf([128, 1], F32, "ssum")
            rsum = p.sbuf([128, 1], F32, "rs")
            m1 = p.sbuf([128, 1], F32, "m1")
            m2 = p.sbuf([128, 1], F32, "m2")
            msk = p.sbuf([128, E], F32, "msk")
            msk2 = p.sbuf([128, E], F32, "msk2")
            dbgg = p.sbuf([128, 32], F32, "dbgg") if "dbg_gate" in w else None
            dbglg = p.sbuf([128, NT * 8], F32, "dbglg") if "dbg_lg" in w else None
            p.barrier()
            if "dbg_gw" in w:
                gwf = p.sbuf([128, 64], F32, "gwf")
                p.inc(nc.vector.tensor_copy(out=gwf[:], in_=gwb[:]), tag + "dgw")
                p.wait(nc.sync, tag + "dgw")
                p.store(w["dbg_gw"][:], gwf[:], tag + "dbg")
                p.barrier()
            for tt in range(NT):
                p.wait(nc.tensor, tag + "gsc", B["gsc"] + tt)
                im = None
                for kc in range(DC):
                    im = nc.tensor.matmul(gps[:], xT[:, kc * T + tt * 128:
                                                     kc * T + (tt + 1) * 128],
                                          gwb[:, kc * E:(kc + 1) * E],
                                          start=(kc == 0), stop=(kc == DC - 1))
                gm = p.inc(im, tag + "gm")
                p.wait(nc.vector, tag + "gm", gm)
                nc.vector.tensor_copy(out=lg[:], in_=gps[:])
                if "dbg_lg" in w:
                    nc.vector.tensor_copy(out=dbglg[:, tt * 8:(tt + 1) * 8], in_=gps[:])
                nc.vector.drain()
                nc.vector.reduce_max(out=nmx[:], in_=lg[:], axis=AX.X, negate=True)
                nc.vector.drain()
                p.inc(nc.vector.engine_nop(), tag + "gnm")
                p.wait(nc.scalar, tag + "gnm", B["gnm"] + tt + 1)
                p.inc(nc.scalar.activation(out=ex[:], in_=lg[:], func=AF.Exp,
                                           bias=nmx[:]), tag + "gex")
                p.wait(nc.vector, tag + "gex", B["gex"] + tt + 1)
                nc.vector.reduce_sum(out=ssum[:], in_=ex[:], axis=AX.X)
                nc.vector.drain()
                nc.vector.reciprocal(out=rsum[:], in_=ssum[:])
                nc.vector.drain()
                nc.vector.tensor_scalar_mul(sm[:], ex[:], rsum[:])
                nc.vector.drain()
                if "dbg_gate" in w and tt == 0:
                    nc.vector.tensor_copy(out=dbgg[:, 0:8], in_=lg[:])
                    nc.vector.tensor_copy(out=dbgg[:, 8:16], in_=ex[:])
                    nc.vector.tensor_copy(out=dbgg[:, 16:17], in_=ssum[:])
                    nc.vector.tensor_copy(out=dbgg[:, 17:18], in_=rsum[:])
                    nc.vector.tensor_copy(out=dbgg[:, 24:32], in_=sm[:])
                    nc.vector.tensor_copy(out=dbgg[:, 18:19], in_=nmx[:])
                nc.vector.reduce_max(out=m1[:], in_=sm[:], axis=AX.X)
                nc.vector.drain()
                nc.vector.tensor_single_scalar(out=msk[:], in_=sm[:], scalar=m1[:],
                                               op=ALU.is_ge)
                nc.vector.drain()
                nc.vector.scalar_tensor_tensor(out=msk2[:], in0=msk[:], scalar=-1.0,
                                               in1=sm[:], op0=ALU.mult,
                                               op1=ALU.mult)
                nc.vector.drain()
                nc.vector.tensor_add(msk2[:], msk2[:], sm[:])
                nc.vector.drain()
                nc.vector.reduce_max(out=m2[:], in_=msk2[:], axis=AX.X)
                nc.vector.drain()
                nc.vector.tensor_single_scalar(out=msk2[:], in_=msk2[:],
                                               scalar=m2[:], op=ALU.is_ge)
                nc.vector.drain()
                nc.vector.tensor_add(msk[:], msk[:], msk2[:])
                nc.vector.drain()
                p.inc(nc.vector.tensor_mul(scale_sb[:, tt * E:(tt + 1) * E],
                                           sm[:], msk[:]), tag + "gsc")
            if "dbg_gate" in w:
                p.store(w["dbg_gate"][:], dbgg[:], tag + "dbg")
            if "dbg_lg" in w:
                p.store(w["dbg_lg"][:], dbglg[:], tag + "dbg")
        p.barrier()
        if "dbg_scale" in w:
            p.store(w["dbg_scale"][:], scale_sb[:], tag + "dbg")
            p.wait(nc.sync, tag + "dbg")

        y = p.sbuf([128, NT * D], F32, "yacc")
        p.inc(nc.vector.memset(y[:], 0.0), tag + "yz")
        sa = p.sbuf([128, HD], F32, "sa")
        hb = p.sbuf([128, HD], F32, "hb")
        hh = p.sbuf([128, HD], BF16, "hh")
        hTs = p.sbuf([128, HD], BF16, "hTs")
        aps = p.psum([128, HD], F32, "aps")
        bps = p.psum([128, HD], F32, "bps")
        tps = p.psum([128, HD], BF16, "tps")
        yps = p.psum([128, D], F32, "yps")
        w1s = [p.sbuf([128, DC * HD], BF16, "w1b") for _ in range(2)]
        w3s = [p.sbuf([128, DC * HD], BF16, "w3b") for _ in range(2)]
        w2s = [p.sbuf([128, 4 * D], BF16, "w2b") for _ in range(2)]

        def exp_load(e, par):
            sn = tag + "wl" + str(par)
            tok = None
            for kc in range(DC):
                r = slice(e * 1024 + kc * 128, e * 1024 + (kc + 1) * 128)
                tok = p.loadc(w1s[par][:, kc * HD:(kc + 1) * HD], w["w1"][r, :], sn)
                tok = p.loadc(w3s[par][:, kc * HD:(kc + 1) * HD], w["w3"][r, :], sn)
            for kc in range(4):
                r = slice(e * HD + kc * 128, e * HD + (kc + 1) * 128)
                tok = p.loadc(w2s[par][:, kc * D:(kc + 1) * D], w["w2"][r, :], sn)
            return tok

        wtok = [exp_load(0, 0), 0]
        lastmm = [0, 0]
        for e in range(E):
            par = e % 2
            if e + 1 < E:
                if lastmm[(e + 1) % 2]:
                    p.wait(nc.gpsimd, "mot", lastmm[(e + 1) % 2])
                wtok[(e + 1) % 2] = exp_load(e + 1, (e + 1) % 2)
            p.wait(nc.tensor, tag + "wl" + str(par), wtok[par])
            w1b, w3b, w2b = w1s[par], w3s[par], w2s[par]
            if True:
                for tt in range(NT):
                    sc = scale_sb[:, tt * E + e: tt * E + e + 1]
                    p.wait(nc.tensor, "mov", p.tail("mov"))
                    ia = ib = None
                    for kc in range(DC):
                        lhs = xT[:, kc * T + tt * 128: kc * T + (tt + 1) * 128]
                        ia = nc.tensor.matmul(aps[:], lhs, w1b[:, kc * HD:(kc + 1) * HD],
                                              start=(kc == 0), stop=(kc == DC - 1))
                    for kc in range(DC):
                        lhs = xT[:, kc * T + tt * 128: kc * T + (tt + 1) * 128]
                        ib = nc.tensor.matmul(bps[:], lhs, w3b[:, kc * HD:(kc + 1) * HD],
                                              start=(kc == 0), stop=(kc == DC - 1))
                    abm = p.inc(ib, "mot")
                    p.wait(nc.scalar, "mot", abm)
                    p.inc(nc.scalar.activation(out=sa[:], in_=aps[:], func=AF.Silu,
                                               scale=sc), tag + "hsa")
                    p.wait(nc.vector, "mot", abm)
                    nc.vector.tensor_scalar_mul(hb[:], bps[:], sc)
                    nc.vector.drain()
                    p.wait(nc.vector, tag + "hsa", p.tail(tag + "hsa"))
                    ih = nc.vector.tensor_mul(hh[:], sa[:], hb[:])
                    hse = p.inc(ih, "mov")
                    p.wait(nc.tensor, "mov", hse)
                    p.wait(nc.tensor, "mov", p.tail("mov"))
                    it = None
                    for c4 in range(4):
                        it = nc.tensor.transpose(tps[:, c4 * 128:(c4 + 1) * 128],
                                                 hh[:, c4 * 128:(c4 + 1) * 128],
                                                 identb[:])
                    htm = p.inc(it, "mot")
                    p.wait(nc.vector, "mot", htm)
                    ihe = nc.vector.tensor_copy(out=hTs[:], in_=tps[:])
                    hte = p.inc(ihe, "mov")
                    p.wait(nc.tensor, "mov", hte)
                    p.wait(nc.tensor, "mov", p.tail("mov"))
                    iy = None
                    for hc in range(4):
                        for half in range(2):
                            iy = nc.tensor.matmul(
                                yps[:, half * 512:(half + 1) * 512],
                                hTs[:, hc * 128:(hc + 1) * 128],
                                w2b[:, hc * D + half * 512: hc * D + (half + 1) * 512],
                                start=(hc == 0), stop=(hc == 3))
                    ym = p.inc(iy, "mot")
                    p.wait(nc.vector, "mot", ym)
                    if e == 0 and tt == 0:
                        p.wait(nc.vector, tag + "yz", B["yz"] + 1)
                    iya = nc.vector.tensor_add(y[:, tt * D:(tt + 1) * D],
                                               y[:, tt * D:(tt + 1) * D], yps[:])
                    p.inc(iya, "mov")
            lastmm[par] = p.tail("mot")
        p.barrier()
        xtile = p.sbuf([128, D], F32, "rx")
        otile = p.sbuf([128, D], F32, "ro")
        Brev = p.tail("mov")
        Brst = p.tail("lnst")
        for tt in range(NT):
            p.wait(nc.sync, "mov", Brev + tt)
            lt = p.load(xtile[:], x_dram[tt * 128:(tt + 1) * 128, :], "lnld0")
            p.wait(nc.vector, "lnld0", lt)
            p.wait(nc.vector, "lnst", Brst + tt * 16)
            iv = nc.vector.scalar_tensor_tensor(out=otile[:],
                                                in0=y[:, tt * D:(tt + 1) * D],
                                                scalar=0.5, in1=xtile[:],
                                                op0=ALU.mult, op1=ALU.add)
            rev = p.inc(iv, "mov")
            p.wait(nc.sync, "mov", rev)
            p.store(out_dram[tt * 128:(tt + 1) * 128, :], otile[:], "lnst")
        p.wait(nc.sync, "lnst")


def mamba_branch(p, xz_d, ocum_d, w, pfx, tag=None, reverse=False,
                 zmax_sb=None):
    tag = "mb"
    p._dma_sems.add(tag + "oc")
    B = {s: p.tail(tag + s) for s in ("cse", "cvv", "xcw", "dld", "zs", "yd",
                                     "cvd")}
    """One mamba branch. xz_d: DRAM [4096, 2048] fp32 (feature-major).
    ocum_d: DRAM [2048, 2048] fp32 accumulator (write if first=not reverse,
    DMA-add otherwise). reverse: process xz reversed along l (bwd branch).
    zmax_sb: [128, 16] sbuf to fill with per-chunk max_l silu(z) (fwd only).
    """
    nc = p.nc
    DI, DS, DR = 2048, 16, 64
    KC = DI // 128

    with p.scope():
        convw = p.sbuf([128, KC * 4], F32, "cvw")
        convb = p.sbuf([128, KC], F32, "cvb")
        Asb = p.sbuf([128, KC * DS], F32, "A")
        Dsb = p.sbuf([128, KC], F32, "D")
        dtb = p.sbuf([128, KC], F32, "dtb")
        for kc in range(KC):
            r0 = kc * 128
            p.load(convw[:, kc * 4:(kc + 1) * 4], w[pfx + "conv_w"][r0:r0 + 128, :], "ld")
            p.load(convb[:, kc:kc + 1], w[pfx + "conv_b"][r0:r0 + 128, :], "ld")
            p.load(Asb[:, kc * DS:(kc + 1) * DS], w[pfx + "A_log"][r0:r0 + 128, :], "ld")
            p.load(Dsb[:, kc:kc + 1], w[pfx + "Dp"][r0:r0 + 128, :], "ld")
            p.load(dtb[:, kc:kc + 1], w[pfx + "dt_b"][r0:r0 + 128, :], "ld")
        p.barrier()
        p.inc(nc.scalar.activation(out=Asb[:], in_=Asb[:], func=AF.Exp),
              "mbs")
        p.wait(nc.vector, "mbs")
        nc.vector.drain()
        p.inc(nc.vector.tensor_scalar_mul(Asb[:], Asb[:], -1.0), "mbv")
        p.barrier()

        xconv_d = nc.dram_tensor(p.name(tag + "xc"), [DI, T], F32, kind="Internal").ap()
        delta_d = nc.dram_tensor(p.name(tag + "dl"), [DI, T], F32, kind="Internal").ap()
        bc_d = nc.dram_tensor(p.name(tag + "bc"), [32, T], F32, kind="Internal").ap()

        ph12 = p.scope()
        ph12.__enter__()
        xcb = p.sbuf([128, KC * T], BF16, "xcb")
        xpT = p.sbuf([128, KC * 96], BF16, "xpT")
        dtT = p.sbuf([64, DI], BF16, "dtT")
        for kc in range(KC):
            r0 = kc * 128
            p.loadc(xpT[:, kc * 96:(kc + 1) * 96], w[pfx + "x_projT"][r0:r0 + 128, :], "ldg")
        p.loadc(dtT[:, :], w[pfx + "dt_projT"][:, :], "ldg")
        ldg_tok = p.tail("ldg")
        for e in (nc.tensor, nc.vector, nc.scalar):
            p.wait(e, "ldg", ldg_tok)
        with p.scope():
            xzc = p.sbuf([128, T], F32, "xzc")
            xrv = p.sbuf([128, T], F32, "xrv")
            cacc = p.sbuf([128, T], F32, "cacc")
            xcv = p.sbuf([128, T], F32, "xcv")
            for kc in range(KC):
                par = tag + "zl" + str(kc % 2)
                p.wait(nc.sync, tag + "cse", B["cse"] + kc - 1)
                lt = p.load(xzc[:], xz_d[kc * 128:(kc + 1) * 128, :], par)
                p.wait(nc.vector, par, lt)
                if reverse:
                    nc.vector.tensor_copy(out=xrv[:], in_=xzc[:, ::-1])
                    nc.vector.drain()
                    src = xrv
                else:
                    src = xzc
                wc = convw[:, kc * 4:kc * 4 + 4]
                p.wait(nc.vector, tag + "cse", B["cse"] + kc)
                nc.vector.tensor_scalar_mul(cacc[:], src[:], wc[:, 3:4])
                for t_ in (2, 1, 0):
                    sh = 3 - t_
                    nc.vector.drain()
                    nc.vector.scalar_tensor_tensor(
                        out=cacc[:, sh:T], in0=src[:, 0:T - sh],
                        scalar=wc[:, t_:t_ + 1], in1=cacc[:, sh:T],
                        op0=ALU.mult, op1=ALU.add)
                iv = nc.vector.drain()
                p.inc(nc.vector.engine_nop(), tag + "cvv")
                p.wait(nc.scalar, tag + "cvv", B["cvv"] + kc + 1)
                p.wait(nc.scalar, tag + "xcw", B["xcw"] + kc)
                isl = nc.scalar.activation(out=xcv[:], in_=cacc[:], func=AF.Silu,
                                           bias=convb[:, kc:kc + 1])
                cse = p.inc(isl, tag + "cse")
                p.wait(nc.vector, tag + "cse", cse)
                nc.vector.drain()
                ic = nc.vector.tensor_copy(out=xcb[:, kc * T:(kc + 1) * T], in_=xcv[:])
                p.inc(ic, tag + "xcw")
                p.wait(nc.sync, tag + "xcw", p.tail(tag + "xcw"))
                p.store(xconv_d[kc * 128:(kc + 1) * 128, :], xcv[:], tag + "cvd")
            p.wait(nc.sync, tag + "cvd")
        p.barrier()

        dtsb = p.sbuf([64, T], BF16, "dts")
        with p.scope():
            dt_ps = p.psum([64, 512], F32, "dtp")
            b_ps = p.psum([16, 512], F32, "bp")
            c_ps = p.psum([16, 512], F32, "cp")
            Bsb0 = p.sbuf([16, T], F32, "bsb")
            Csb0 = p.sbuf([16, T], F32, "csb")
            for n in range(NQ):
                sl = slice(n * 512, (n + 1) * 512)
                p.wait(nc.tensor, tag + "dbe", p.tail(tag + "dbe"))
                for kc in range(KC):
                    i1 = nc.tensor.matmul(dt_ps[:], xpT[:, kc * 96:kc * 96 + 64],
                                          xcb[:, kc * T + n * 512:kc * T + (n + 1) * 512],
                                          start=(kc == 0), stop=(kc == KC - 1))
                for kc in range(KC):
                    i2 = nc.tensor.matmul(b_ps[:], xpT[:, kc * 96 + 64:kc * 96 + 80],
                                          xcb[:, kc * T + n * 512:kc * T + (n + 1) * 512],
                                          start=(kc == 0), stop=(kc == KC - 1))
                for kc in range(KC):
                    i3 = nc.tensor.matmul(c_ps[:], xpT[:, kc * 96 + 80:kc * 96 + 96],
                                          xcb[:, kc * T + n * 512:kc * T + (n + 1) * 512],
                                          start=(kc == 0), stop=(kc == KC - 1))
                mt = p.inc(i3, tag + "dbm")
                p.wait(nc.vector, tag + "dbm", mt)
                nc.vector.tensor_copy(out=dtsb[:, sl], in_=dt_ps[:])
                nc.vector.tensor_copy(out=Bsb0[:, sl], in_=b_ps[:])
                p.inc(nc.vector.tensor_copy(out=Csb0[:, sl], in_=c_ps[:]),
                      tag + "dbe")
            p.barrier()
            p.store(bc_d[0:16, :], Bsb0[:, :], "st")
            p.store(bc_d[16:32, :], Csb0[:, :], "st")
            p.wait(nc.sync, "st")
        with p.scope():
            d_ps = [p.psum([128, 512], F32, "dps"), p.psum([128, 512], F32, "dps")]
            r3 = [p.sbuf([128, 512], F32, "srl") for _ in range(3)]
            a2 = [p.sbuf([128, 512], F32, "sab") for _ in range(2)]
            e2 = [p.sbuf([128, 512], F32, "sex") for _ in range(2)]
            l2 = [p.sbuf([128, 512], F32, "sln") for _ in range(2)]
            dch2 = [p.sbuf([128, T], F32, "dch") for _ in range(2)]
            NTI = KC * NQ
            atok = {}
            ltok = {}
            addtok = {}
            stok = {-1: 0, -2: 0}

            def sp_tail(i):
                if i - 1 >= 0 and i - 1 < NTI:
                    j1 = (i - 1) % 2
                    nc.scalar.activation(out=e2[j1][:], in_=a2[j1][:],
                                         func=AF.Exp, scale=-1.0)
                if i - 2 >= 0 and i - 2 < NTI:
                    k = i - 2
                    j2 = k % 2
                    ie = nc.scalar.activation(out=l2[j2][:], in_=e2[j2][:],
                                              func=AF.Ln, bias=p.constf(1.0))
                    ltok[k] = p.inc(ie, tag + "dle")
                    jd2, n2 = k // NQ, k % NQ
                    p.wait(nc.vector, tag + "dle", ltok[k])
                    if n2 == 0:
                        p.wait(nc.vector, tag + "dld", stok.get(jd2 - 2, 0))
                    iv = nc.vector.tensor_add(
                        dch2[jd2 % 2][:, n2 * 512:(n2 + 1) * 512],
                        r3[k % 3][:], l2[j2][:])
                    addtok[k] = p.inc(iv, tag + "dlv")
                    if n2 == NQ - 1:
                        p.wait(nc.sync, tag + "dlv", addtok[k])
                        stok[jd2] = p.store(
                            delta_d[jd2 * 128:(jd2 + 1) * 128, :],
                            dch2[jd2 % 2][:], tag + "dld")

            for i in range(NTI):
                jd, n = i // NQ, i % NQ
                si = i % 2
                if i - 2 in atok:
                    p.wait(nc.tensor, tag + "dle", atok[i - 2])
                im = nc.tensor.matmul(d_ps[si][:],
                                      dtT[:, jd * 128:(jd + 1) * 128],
                                      dtsb[:, n * 512:(n + 1) * 512],
                                      start=True, stop=True)
                mt = p.inc(im, tag + "dlm")
                p.wait(nc.scalar, tag + "dlm", mt)
                if i - 3 in addtok:
                    p.wait(nc.scalar, tag + "dlv", addtok[i - 3])
                nc.scalar.activation(out=r3[i % 3][:], in_=d_ps[si][:],
                                     func=AF.Relu, bias=dtb[:, jd:jd + 1])
                ia = nc.scalar.activation(out=a2[si][:], in_=d_ps[si][:],
                                          func=AF.Abs, bias=dtb[:, jd:jd + 1])
                atok[i] = p.inc(ia, tag + "dle")
                sp_tail(i)
            sp_tail(NTI)
            sp_tail(NTI + 1)
            p.wait(nc.sync, tag + "dld")
        ph12.__exit__(None, None, None)
        p.barrier()

        with p.scope():
            Bball = p.sbuf([128, DS * T], BF16, "Bb")
            Cball = p.sbuf([128, DS * T], BF16, "Cb")
            with p.scope():
                rowst = p.sbuf([128, T], F32, "rwst")
                for n in range(DS):
                    for (dst, roff) in ((Bball, 0), (Cball, 16)):
                        bc_in = bass.AP(tensor=bc_d.tensor, offset=(roff + n) * T,
                                        ap=[[0, 128], [1, T]])
                        lt = p.load(rowst[:], bc_in, "ld")
                        p.wait(nc.vector, "ld", lt)
                        p.inc(nc.vector.tensor_copy(out=dst[:, n * T:(n + 1) * T],
                                                    in_=rowst[:]), tag + "bcc")
                        p.wait(nc.sync, tag + "bcc", p.tail(tag + "bcc"))
            dch = p.sbuf([128, T], F32, "dch")
            xcvc = p.sbuf([128, T], F32, "xcv")
            ub = p.sbuf([128, T], BF16, "ub")
            dAb = p.sbuf([128, T], F32, "dA")
            dBu = [p.sbuf([128, T], F32, "dBu"), p.sbuf([128, T], F32, "dBu")]
            hsb = [p.sbuf([128, T], BF16, "hs"), p.sbuf([128, T], BF16, "hs")]
            prod = [p.sbuf([128, T], F32, "prod"), p.sbuf([128, T], F32, "prod")]
            yacc = p.sbuf([128, T], F32, "y")
            zch = dch
            zsl = dBu[0]
            for kc in range(KC):
                p.wait(nc.sync, tag + "zs", B["zs"] + kc)
                p.wait(nc.sync, tag + "yd", B["yd"] + kc)
                l1 = p.load(dch[:], delta_d[kc * 128:(kc + 1) * 128, :], "mbl" + str(kc % 2))
                l2 = p.load(xcvc[:], xconv_d[kc * 128:(kc + 1) * 128, :], "mbl" + str(kc % 2))
                p.wait(nc.vector, "mbl" + str(kc % 2), l1)
                p.wait(nc.vector, "mbl" + str(kc % 2), l2)
                iu = nc.vector.tensor_mul(ub[:], dch[:], xcvc[:])
                uu = p.inc(iu, "mbv")
                p.wait(nc.scalar, "mbv", uu)
                nc.vector.memset(yacc[:], 0.0)
                p.inc(nc.vector.tensor_mul(dBu[0][:], ub[:],
                                           Bball[:, 0 * T:1 * T]), "mbv")
                p.inc(nc.vector.tensor_mul(dBu[1][:], ub[:],
                                           Bball[:, 1 * T:2 * T]), "mbv")
                sc_tok = 0
                for n in range(DS):
                    if n > 0:
                        p.wait(nc.scalar, "mbv", sc_tok)
                    ia = nc.scalar.activation(out=dAb[:], in_=dch[:], func=AF.Exp,
                                              scale=Asb[:, kc * DS + n:kc * DS + n + 1])
                    da = p.inc(ia, tag + "sdA")
                    p.wait(nc.vector, tag + "sdA", da)
                    isc = nc.vector.tensor_tensor_scan(
                        out=hsb[n % 2][:], data0=dAb[:], data1=dBu[n % 2][:],
                        initial=0.0, op0=ALU.mult, op1=ALU.add)
                    sc_tok = p.inc(isc, "mbv")
                    if n + 2 < DS:
                        p.inc(nc.vector.tensor_mul(
                            dBu[n % 2][:], ub[:],
                            Bball[:, (n + 2) * T:(n + 3) * T]), "mbv")
                    if n >= 1:
                        p.inc(nc.vector.tensor_mul(
                            prod[(n - 1) % 2][:], hsb[(n - 1) % 2][:],
                            Cball[:, (n - 1) * T:n * T]), "mbv")
                    if n >= 2:
                        p.inc(nc.vector.tensor_add(yacc[:], yacc[:],
                                                   prod[(n - 2) % 2][:]), "mbv")
                p.inc(nc.vector.tensor_mul(prod[1][:], hsb[1][:],
                                           Cball[:, (DS - 1) * T:DS * T]), "mbv")
                p.inc(nc.vector.tensor_add(yacc[:], yacc[:], prod[0][:]), "mbv")
                nc.vector.drain()
                p.inc(nc.vector.tensor_add(yacc[:], yacc[:], prod[1][:]), "mbv")
                nc.vector.drain()
                iy = nc.vector.scalar_tensor_tensor(
                    out=yacc[:], in0=xcvc[:], scalar=Dsb[:, kc:kc + 1], in1=yacc[:],
                    op0=ALU.mult, op1=ALU.add)
                p.inc(iy, tag + "yd")
                p.wait(nc.sync, "mbv", p.tail("mbv"))
                lz = p.load(zch[:], xz_d[DI + kc * 128: DI + (kc + 1) * 128, :],
                            "mbl" + str(kc % 2))
                p.wait(nc.scalar, "mbl" + str(kc % 2), lz)
                p.wait(nc.scalar, "mbv", p.tail("mbv"))
                if reverse:
                    p.wait(nc.vector, "mbl" + str(kc % 2), lz)
                    nc.vector.drain()
                    izr = nc.vector.tensor_copy(out=zsl[:], in_=zch[:, ::-1])
                    p.inc(izr, "mbv")
                    p.wait(nc.scalar, "mbv", p.tail("mbv"))
                    iz = nc.scalar.activation(out=zsl[:], in_=zsl[:], func=AF.Silu)
                else:
                    p.wait(nc.scalar, "mbv", p.tail("mbv"))
                    iz = nc.scalar.activation(out=zsl[:], in_=zch[:], func=AF.Silu)
                zs = p.inc(iz, tag + "zs")
                p.wait(nc.vector, tag + "zs", zs)
                nc.vector.drain()
                if (zmax_sb is not None) and not reverse:
                    nc.vector.reduce_max(out=zmax_sb[:, kc:kc + 1], in_=zsl[:],
                                         axis=AX.X)
                    nc.vector.drain()
                ig = nc.vector.tensor_mul(yacc[:], yacc[:], zsl[:])
                p.inc(ig, "mbv")
                nc.vector.drain()
                if reverse:
                    nc.vector.tensor_copy(out=prod[0][:], in_=yacc[:, ::-1])
                    out_src = prod[0]
                else:
                    out_src = yacc
                nc.vector.drain()
                gv = p.inc(nc.vector.engine_nop(), "mbv")
                p.wait(nc.gpsimd, "mbv", gv)
                if reverse:
                    p.inc(nc.gpsimd.dma_start(
                        out=ocum_d[kc * 128:(kc + 1) * 128, :], in_=out_src[:],
                        accum_op=ALU.add), tag + "oc", 16)
                else:
                    p.inc(nc.gpsimd.dma_start(
                        out=ocum_d[kc * 128:(kc + 1) * 128, :], in_=out_src[:]),
                        tag + "oc", 16)
                p.wait(nc.vector, tag + "oc", p.tail(tag + "oc"))
        p.barrier()


def sub_l1attn(p, x_dram, out_dram, w, tag=None, with_fft=True,
               branches=("f", "b")):
    nc = p.nc
    tag = "la"
    DI = 2048
    B = {s: p.tail(tag + s) for s in ("ipd", "ybc")}
    with p.scope():
        identb = p.sbuf([128, 128], BF16, "idb")
        identf = p.sbuf([128, 128], F32, "idf")
        p.loadc(identb[:], w["ident"][:])
        p.load(identf[:], w["ident"][:])
        zmax = p.sbuf([128, 16], F32, "zmax")
        p.barrier()
        xz_d = nc.dram_tensor(p.name(tag + "xz"), [2 * DI, T], F32,
                              kind="Internal").ap()
        ocum_d = nc.dram_tensor(p.name(tag + "oc"), [DI, T], F32,
                                kind="Internal").ap()

        with p.scope():
            xT = p.sbuf([128, DC * T], BF16, "xT")
            ln_to_xT(p, x_dram, xT, identb, eps=1e-5, gain=None, tag=tag + "ln")
            p.barrier()
            stg = p.sbuf([128, T], F32, "stg")
            with p.scope():
                wq, _ = load_w(p, w["in_projT"], 1024, 4096)
                p.barrier()

                KCW = 4096 // 128
                ps2 = [p.psum([128, 512], F32, "ip"), p.psum([128, 512], F32, "ip")]
                slot_tok = [0, 0]
                si = 0
                for j in range(KCW):
                    for n in range(NQ):
                        if slot_tok[si]:
                            p.wait(nc.tensor, tag + "ipe", slot_tok[si])
                        im = None
                        for kc in range(DC):
                            im = nc.tensor.matmul(
                                ps2[si][:], w_tile(wq, 4096, kc, j),
                                xT[:, kc * T + n * 512:kc * T + (n + 1) * 512],
                                start=(kc == 0), stop=(kc == DC - 1))
                        mtok = p.inc(im, tag + "ipm")
                        p.wait(nc.scalar, tag + "ipm", mtok)
                        p.wait(nc.scalar, tag + "ipd", B["ipd"] + j * 16)
                        ie = nc.scalar.activation(
                            out=stg[:, n * 512:(n + 1) * 512], in_=ps2[si][:],
                            func=AF.Copy)
                        slot_tok[si] = p.inc(ie, tag + "ipe")
                        si ^= 1
                    p.wait(nc.sync, tag + "ipe", p.tail(tag + "ipe"))
                    p.store(xz_d[j * 128:(j + 1) * 128, :], stg[:], tag + "ipd")
        p.barrier()

        if "f" in branches:
            mamba_branch(p, xz_d, ocum_d, w, "f_", tag + "mf", reverse=False,
                         zmax_sb=zmax)
        if "b" in branches:
            mamba_branch(p, xz_d, ocum_d, w, "b_", tag + "mb",
                         reverse=True)
        if "f" not in branches:
            pass

        if with_fft:
            fft_branch(p, xz_d, ocum_d, w, zmax, identb, tag + "ff")

        with p.scope():
            oT = p.sbuf([128, DC * T], F32, "oT")
            wo, _ = load_w(p, w["out_projT"], 2048, 1024)
            ych = p.sbuf([128, T], BF16, "ych")
            yb = p.sbuf([128, 16 * T], BF16, "yb")
            for kc in range(16):
                par = tag + "yl" + str(kc % 2)
                p.wait(nc.gpsimd, tag + "ybc", B["ybc"] + kc)
                lt = p.loadc(ych[:], ocum_d[kc * 128:(kc + 1) * 128, :], par)
                p.wait(nc.vector, par, lt)
                nc.vector.drain()
                p.inc(nc.vector.tensor_copy(out=yb[:, kc * T:(kc + 1) * T],
                                            in_=ych[:]), tag + "ybc")
            p.barrier()

            def evac_o(j, n, ps, mtok):
                p.wait(nc.scalar, "mmm", mtok)
                return nc.scalar.activation(
                    out=oT[:, j * T + n * 512:j * T + (n + 1) * 512],
                    in_=ps[:], func=AF.Copy)
            mm_ff(p, wo, 2048, 1024, yb, evac_o, rhs_T=T)
            p.barrier()
            residual_store(p, oT, x_dram, out_dram, identf, scale_src=1.0,
                           tag=tag + "res")


def chunk_major(dram_ap, col0, ncols, nrows=2048):
    W = dram_ap.ap[-1][1]
    base = dram_ap.offset + col0
    return bass.AP(tensor=dram_ap.tensor, offset=base,
                   ap=[[W, 128], [128 * W, nrows // 128], [1, ncols]])


def dft_pass(p, in_re_d, in_im_d, out_re_d, out_im_d, fre_d, fim_d, fimn_d,
             identb, tag=None, in_fp32=False):
    tag = "df"
    """Complex DFT along the partition axis of in (2048), with output tiles
    transposed on evacuation so out dims are [transform-axis-free, other].
    in_im_d None => real input. F matrices dram bf16 [2048, 2048].
    All in/out DRAM [2048, 2048]; in bf16 (or fp32 with cast), out bf16."""
    nc = p.nc
    cplx = in_im_d is not None
    with p.scope():
        slabs = 2 if cplx else 1
        inslab = p.sbuf([128, 16 * 512 * slabs], BF16, "dfin")

        def in_sl(kc, which):
            off = which * 16 * 512
            return inslab[:, off + kc * 512: off + (kc + 1) * 512]
        fsl2 = [p.sbuf([128, 16 * 128 * 3], BF16, "dfF") for _ in range(2)]

        def f_sl(fs, kc, which):
            off = which * 16 * 128
            return fs[:, off + kc * 128: off + (kc + 1) * 128]
        pre2 = [p.psum([128, 512], F32, "dfr") for _ in range(2)]
        pim2 = [p.psum([128, 512], F32, "dfi") for _ in range(2)]
        ptr = p.psum([128, 512], BF16, "dft")
        ev1 = p.sbuf([128, 512], BF16, "dfe")
        ev2 = p.sbuf([128, 512], BF16, "dfe")
        fds = (fre_d, fim_d) + ((fimn_d,) if cplx else ())
        NTJ = 16
        mmtok = {}
        e1tok = {-1: 0, -2: 0}
        fltok = {}

        def issue_fsl(t):
            if t >= NQ * NTJ:
                return
            jt = t % NTJ
            if t - 2 in mmtok:
                p.wait(nc.gpsimd, tag + "mm", mmtok[t - 2])
            tk = None
            for wi, fd in enumerate(fds):
                tk = p.loadc(fsl2[t % 2][:, wi * 16 * 128:(wi + 1) * 16 * 128],
                             chunk_major(fd, jt * 128, 128), tag + "fl")
            fltok[t] = tk

        def emit_evac(t, n, j):
            si = t % 2
            mt = mmtok[t]
            last = 0
            for (ps, od) in ((pre2[si], out_re_d), (pim2[si], out_im_d)):
                p.wait(nc.vector, tag + "mm", mt)
                p.wait(nc.vector, tag + "tt", p.tail(tag + "tt"))
                e1 = nc.vector.tensor_copy(out=ev1[:], in_=ps[:])
                ee = p.inc(e1, tag + "e1")
                last = ee
                p.wait(nc.tensor, tag + "e1", ee)
                p.wait(nc.tensor, tag + "e2", p.tail(tag + "e2"))
                it = None
                for s in range(4):
                    it = nc.tensor.transpose(ptr[:, s * 128:(s + 1) * 128],
                                             ev1[:, s * 128:(s + 1) * 128],
                                             identb[:])
                tt2 = p.inc(it, tag + "tt")
                p.wait(nc.vector, tag + "tt", tt2)
                p.wait(nc.vector, tag + "st", p.tail(tag + "st"))
                e2 = nc.vector.tensor_copy(out=ev2[:], in_=ptr[:])
                e2t = p.inc(e2, tag + "e2")
                p.wait(nc.sync, tag + "e2", e2t)
                for s in range(4):
                    p.store(od[n * 512 + s * 128: n * 512 + (s + 1) * 128,
                               j * 128:(j + 1) * 128],
                            ev2[:, s * 128:(s + 1) * 128], tag + "st")
            e1tok[t] = last

        issue_fsl(0)
        pend = None
        for n in range(NQ):
            p.wait(nc.gpsimd, tag + "mm", p.tail(tag + "mm"))
            p.loadc(inslab[:, 0:16 * 512], chunk_major(in_re_d, n * 512, 512),
                    tag + "il")
            if cplx:
                p.loadc(inslab[:, 16 * 512:32 * 512],
                        chunk_major(in_im_d, n * 512, 512), tag + "il")
            for j in range(NTJ):
                t = n * NTJ + j
                si = t % 2
                issue_fsl(t + 1)
                p.wait(nc.tensor, tag + "il", p.tail(tag + "il"))
                p.wait(nc.tensor, tag + "fl", fltok[t])
                p.wait(nc.tensor, tag + "e1", e1tok.get(t - 2, 0))
                fs = fsl2[si]
                pre, pim = pre2[si], pim2[si]
                im1 = im2 = None
                for kc in range(16):
                    im1 = nc.tensor.matmul(pre[:], f_sl(fs, kc, 0), in_sl(kc, 0),
                                           start=(kc == 0), stop=(kc == 15) and not cplx)
                if cplx:
                    for kc in range(16):
                        im1 = nc.tensor.matmul(pre[:], f_sl(fs, kc, 2), in_sl(kc, 1),
                                               start=False, stop=(kc == 15))
                for kc in range(16):
                    im2 = nc.tensor.matmul(pim[:], f_sl(fs, kc, 1), in_sl(kc, 0),
                                           start=(kc == 0), stop=(kc == 15) and not cplx)
                if cplx:
                    for kc in range(16):
                        im2 = nc.tensor.matmul(pim[:], f_sl(fs, kc, 0), in_sl(kc, 1),
                                               start=False, stop=(kc == 15))
                mmtok[t] = p.inc(im2, tag + "mm")
                if pend is not None:
                    emit_evac(*pend)
                pend = (t, n, j)
        emit_evac(*pend)
    p.barrier()


def fft_branch(p, xz_d, ocum_d, w, zmax, identb, tag=None):
    nc = p.nc
    tag = "ff"
    N2 = float(T) * float(T)
    X1r = nc.dram_tensor(p.name(tag + "x1r"), [T, T], BF16, kind="Internal").ap()
    X1i = nc.dram_tensor(p.name(tag + "x1i"), [T, T], BF16, kind="Internal").ap()
    X2r = nc.dram_tensor(p.name(tag + "x2r"), [T, T], BF16, kind="Internal").ap()
    X2i = nc.dram_tensor(p.name(tag + "x2i"), [T, T], BF16, kind="Internal").ap()
    YCr = nc.dram_tensor(p.name(tag + "ycr"), [T, T], BF16, kind="Internal").ap()
    YCi = nc.dram_tensor(p.name(tag + "yci"), [T, T], BF16, kind="Internal").ap()
    fre, fim, fimn = w["fft_fre"], w["fft_fim"], w["fft_fimn"]

    dft_pass(p, xz_d[0:T, :], None, X1r, X1i, fre, fim, fimn, identb,
             tag + "p1")
    dft_pass(p, X1r, X1i, X2r, X2i, fre, fim, fimn, identb, tag + "p2")

    with p.scope():
        KJ = 4096 // 128
        KI = 4096 // 128
        HW = T // 2
        xch = p.sbuf([128, KI * (HW + 4)], BF16, "xch")
        wslab2 = [p.sbuf([128, 5 * KI * 128], BF16, "wcv") for _ in range(2)]

        def xc_sl(kc, c0, cn):
            return xch[:, kc * (HW + 4) + c0: kc * (HW + 4) + c0 + cn]
        cps = [p.psum([128, 512], F32, "cvp"), p.psum([128, 512], F32, "cvp")]
        cevs = p.sbuf([128, 512], BF16, "cve")
        fcb = p.sbuf([128, KJ], F32, "fcb")
        for j in range(KJ):
            p.load(fcb[:, j:j + 1], w["fft_b"][j * 128:(j + 1) * 128, :], tag + "cbl")
        si = 0
        cwtok = {}
        cmtok = {}

        def issue_wslab(u):
            if u >= 2 * KJ:
                return
            ju = u % KJ
            if u - 2 in cmtok:
                p.wait(nc.gpsimd, tag + "cmm", cmtok[u - 2])
            tk = None
            for t_ in range(5):
                tk = p.loadc(wslab2[u % 2][:, t_ * KI * 128:(t_ + 1) * KI * 128],
                             chunk_major(w["fftw_t%d" % t_], ju * 128, 128,
                                         nrows=4096), tag + "cwl")
            cwtok[u] = tk

        issue_wslab(0)
        for half in range(2):
            base = half * HW
            p.barrier()
            for kc in range(KI):
                src_d = X2r if kc < 16 else X2i
                row0 = (kc % 16) * 128
                lo = max(0, base - 2)
                hi = min(T, base + HW + 2)
                off = lo - (base - 2)
                p.loadc(xc_sl(kc, off, hi - lo),
                        src_d[row0:row0 + 128, lo:hi], tag + "cxl")
                if base - 2 < 0:
                    p.inc(nc.vector.memset(xc_sl(kc, 0, 2), 0.0), tag + "cxz")
                if base + HW + 2 > T:
                    p.inc(nc.vector.memset(xc_sl(kc, HW + 2, 2), 0.0), tag + "cxz")
            p.barrier()
            for j in range(KJ):
                u = half * KJ + j
                issue_wslab(u + 1)
                wslab = wslab2[u % 2]
                p.wait(nc.tensor, tag + "cwl", cwtok[u])
                for n in range(HW // 512):
                    p.wait(nc.tensor, tag + "cev", max(0, p.tail(tag + "cev") - 1))
                    first = True
                    imm = None
                    for t_ in range(5):
                        for kc in range(KI):
                            imm = nc.tensor.matmul(
                                cps[si][:],
                                wslab[:, (t_ * KI + kc) * 128:(t_ * KI + kc + 1) * 128],
                                xc_sl(kc, n * 512 + t_, 512),
                                start=first, stop=(t_ == 4 and kc == KI - 1))
                            first = False
                    cmt = p.inc(imm, tag + "cmm")
                    cmtok[u] = cmt
                    p.wait(nc.scalar, tag + "cmm", cmt)
                    p.wait(nc.scalar, tag + "cst", p.tail(tag + "cst"))
                    ie = nc.scalar.activation(out=cevs[:], in_=cps[si][:],
                                              func=AF.Silu, bias=fcb[:, j:j + 1])
                    cev = p.inc(ie, tag + "cev")
                    p.wait(nc.sync, tag + "cev", cev)
                    od = YCr if j < 16 else YCi
                    p.store(od[(j % 16) * 128:(j % 16 + 1) * 128,
                               base + n * 512: base + (n + 1) * 512],
                            cevs[:], tag + "cst")
                    si ^= 1
        p.wait(nc.sync, tag + "cst")
    p.barrier()

    dft_pass(p, YCr, YCi, X1r, X1i, fre, fimn, fim, identb, tag + "q1")
    dft_pass(p, X1r, X1i, X2r, X2i, fre, fimn, fim, identb, tag + "q2")

    with p.scope():
        rech = p.sbuf([128, T], BF16, "rech")
        imch = p.sbuf([128, T], BF16, "imch")
        sq1 = p.sbuf([128, T], F32, "sq1")
        sq2 = p.sbuf([128, T], F32, "sq2")
        vch = p.sbuf([128, T], F32, "vch")
        xcn = p.sbuf([128, T], F32, "xcn")
        zab = p.sbuf([128, 16], F32, "zab")
        s1 = p.sbuf([128, 1], F32, "s1")
        s2 = p.sbuf([128, 1], F32, "s2")
        mu = p.sbuf([128, 1], F32, "mu")
        lnv = p.sbuf([128, 1], F32, "lnv")
        rstd = p.sbuf([128, 1], F32, "rstd")
        p.inc(nc.scalar.activation(out=zab[:], in_=zmax[:], func=AF.Abs),
              tag + "zab")
        p.barrier()
        for kc in range(16):
            p.wait(nc.gpsimd, tag + "asq", p.tail(tag + "asq"))
            l1 = p.loadc(rech[:], X2r[kc * 128:(kc + 1) * 128, :], tag + "ar" + str(kc % 2))
            l2 = p.loadc(imch[:], X2i[kc * 128:(kc + 1) * 128, :], tag + "ai" + str(kc % 2))
            p.wait(nc.scalar, tag + "ar" + str(kc % 2), l1)
            p.wait(nc.scalar, tag + "ai" + str(kc % 2), l2)
            p.wait(nc.scalar, tag + "asq", p.tail(tag + "asq"))
            nc.scalar.activation(out=sq1[:], in_=rech[:], func=AF.Square)
            nc.scalar.drain()
            i2 = nc.scalar.activation(out=sq2[:], in_=imch[:], func=AF.Square)
            sqt = p.inc(i2, tag + "asq")
            p.wait(nc.vector, tag + "asq", sqt)
            nc.vector.drain()
            iv = nc.vector.tensor_add(vch[:], sq1[:], sq2[:])
            p.inc(iv, tag + "avv")
            p.wait(nc.scalar, tag + "avv", p.tail(tag + "avv"))
            isq = nc.scalar.activation(out=vch[:], in_=vch[:], func=AF.Sqrt)
            vt = p.inc(isq, tag + "avs")
            p.wait(nc.vector, tag + "avs", vt)
            nc.vector.drain()
            nc.vector.tensor_scalar(out=vch[:], in0=vch[:],
                                    scalar1=zab[:, kc:kc + 1],
                                    scalar2=1.0 / N2, op0=ALU.mult, op1=ALU.mult)
            nc.vector.drain()
            p.inc(nc.vector.reduce_sum(out=s1[:], in_=vch[:], axis=AX.X), tag + "as1")
            nc.vector.drain()
            nc.vector.tensor_scalar_mul(mu[:], s1[:], 1.0 / T)
            nc.vector.drain()
            ixc = nc.vector.tensor_single_scalar(out=xcn[:], in_=vch[:],
                                                 scalar=mu[:], op=ALU.subtract)
            xct = p.inc(ixc, tag + "axc")
            p.wait(nc.scalar, tag + "axc", xct)
            p.wait(nc.scalar, tag + "alw", p.tail(tag + "alw"))
            p.inc(nc.scalar.activation(out=sq1[:], in_=xcn[:], func=AF.Square,
                                       accum_out=s2[:]), tag + "asq2")
            p.wait(nc.scalar, tag + "asq2")
            nc.scalar.activation(out=lnv[:], in_=s2[:], func=AF.Ln,
                                 scale=1.0 / T, bias=p.constf(1e-5))
            nc.scalar.drain()
            ir = nc.scalar.activation(out=rstd[:], in_=lnv[:], func=AF.Exp,
                                      scale=-0.5)
            rt = p.inc(ir, tag + "arstd")
            p.wait(nc.vector, tag + "arstd", rt)
            nc.vector.drain()
            io = nc.vector.tensor_scalar_mul(vch[:], xcn[:], rstd[:])
            ot = p.inc(io, tag + "alw")
            p.wait(nc.gpsimd, tag + "alw", ot)
            p.inc(nc.gpsimd.dma_start(out=ocum_d[kc * 128:(kc + 1) * 128, :],
                                      in_=vch[:], accum_op=ALU.add), tag + "aoc", 16)
            p.wait(nc.vector, tag + "aoc", p.tail(tag + "aoc"))
    p._dma_sems.add(tag + "aoc")
    p.barrier()


def _host_consts():
    cosF, sinF, pi = rope_tables()
    ang = -2.0 * np.pi * np.outer(np.arange(T), np.arange(T)) / T
    FRE = np.cos(ang).astype(np.float32)
    FIM = np.sin(ang).astype(np.float32)
    return {
        "ident": np.eye(128, dtype=np.float32),
        "ident64s": np.ascontiguousarray(np.tile(np.eye(64, dtype=np.float32), (2, 1))),
        "rope_pi": pi, "rope_cos": cosF, "rope_sin": sinF,
        "fft_fre": FRE, "fft_fim": FIM, "fft_fimn": np.ascontiguousarray(-FIM),
    }


def _launch(build_fn, host, xs):
    p = Prog()
    nc = p.nc
    xd = nc.dram_tensor("x", [T, D], F32, kind="ExternalInput").ap()
    od = nc.dram_tensor("out", [T, D], F32, kind="ExternalOutput").ap()
    w = {k: nc.dram_tensor(k, list(np.asarray(a).shape), F32,
                           kind="ExternalInput").ap() for k, a in host.items()}
    build_fn(p, xd, od, w)
    p.es.close()
    hostc = dict(host)
    hostc["fconsts"] = p.const_array()
    in_maps = [dict(hostc, x=np.ascontiguousarray(xs[i])) for i in range(8)]
    res = run_bass_kernel_spmd(nc, in_maps, core_ids=list(range(8)))
    return [res.results[i]["out"] for i in range(8)]


FUSED = True


def kernel(**inputs):
    A = np.ascontiguousarray
    g = lambda k: np.asarray(inputs[k], dtype=np.float32)
    x = A(g("L0_attn_x"))
    C = _host_consts()

    h_l0f = {
        "ident": C["ident"], "ident64s": C["ident64s"],
        "We": A(g("L0_ffn_We").reshape(20 * 1024, 1024)),
        "be": A(g("L0_ffn_be").reshape(-1, 1)),
        "Wq": A(g("L0_ffn_Wq").reshape(6 * 1024, 1024)),
        "Wk": A(g("L0_ffn_Wk").reshape(6 * 1024, 1024)),
        "Wv": A(g("L0_ffn_Wv").reshape(6 * 1024, 1024)),
        "Wo": A(g("L0_ffn_Wo").reshape(6 * 1024, 1024)),
        "ew_bc": A(np.broadcast_to(g("L0_ffn_ew").reshape(1, 18), (128, 18))),
        "actw_bc": A(np.broadcast_to(g("L0_ffn_act_w").reshape(1, 6), (128, 6))),
        "lng_bc": A(np.broadcast_to(g("L0_ffn_ln_g"), (128, D))),
        "lnb_bc": A(np.broadcast_to(g("L0_ffn_ln_b"), (128, D))),
    }
    h_l0a = {
        "ident": C["ident"], "ident64s": C["ident64s"],
        "rope_pi": C["rope_pi"], "rope_cos": C["rope_cos"],
        "rope_sin": C["rope_sin"],
        "ln_gamma_bc": A(np.broadcast_to(g("L0_attn_ln_gamma"), (128, D))),
        "ln_beta_bc": A(np.broadcast_to(g("L0_attn_ln_beta"), (128, D))),
        "w_qkv": g("L0_attn_w_qkv"), "w_out": g("L0_attn_w_out"),
        "b_out": A(g("L0_attn_b_out").reshape(D, 1)),
    }
    h_l1f = {
        "ident": C["ident"], "gate_w": g("L1_ffn_gate_w"),
        "w1": A(g("L1_ffn_w1").reshape(8 * 1024, 512)),
        "w3": A(g("L1_ffn_w3").reshape(8 * 1024, 512)),
        "w2": A(g("L1_ffn_w2").reshape(8 * 512, 1024)),
    }
    h_l1a = {
        "ident": C["ident"],
        "in_projT": A(g("L1_attn_in_proj_w").T),
        "out_projT": A(g("L1_attn_out_proj_w").T),
        "f_conv_w": A(g("L1_attn_conv1d_w")[:, 0, :]),
        "f_conv_b": A(g("L1_attn_conv1d_bias").reshape(-1, 1)),
        "f_x_projT": A(g("L1_attn_x_proj_w").T),
        "f_dt_projT": A(g("L1_attn_dt_proj_w").T),
        "f_dt_b": A(g("L1_attn_dt_proj_b").reshape(-1, 1)),
        "f_A_log": g("L1_attn_A_log"),
        "f_Dp": A(g("L1_attn_D").reshape(-1, 1)),
        "b_conv_w": A(g("L1_attn_conv1db_w")[:, 0, :]),
        "b_conv_b": A(g("L1_attn_conv1db_bias").reshape(-1, 1)),
        "b_x_projT": A(g("L1_attn_x_projb_w").T),
        "b_dt_projT": A(g("L1_attn_dt_projb_w").T),
        "b_dt_b": A(g("L1_attn_dt_projb_b").reshape(-1, 1)),
        "b_A_log": A(g("L1_attn_A_b_log")),
        "b_Dp": A(g("L1_attn_D_b").reshape(-1, 1)),
        "fft_fre": C["fft_fre"], "fft_fim": C["fft_fim"],
        "fft_fimn": C["fft_fimn"],
        "fft_b": A(g("L1_attn_fft_conv_b").reshape(-1, 1)),
    }
    fcw = g("L1_attn_fft_conv_w")
    for t_ in range(5):
        h_l1a["fftw_t%d" % t_] = A(fcw[:, :, t_].T)

    xs = [x[i] for i in range(8)]
    if not FUSED:
        xs = _launch(lambda p, xd, od, w: sub_l0ffn(p, xd, od, w, tag="lf"),
                     h_l0f, xs)
        xs = _launch(sub_l0attn, h_l0a, xs)
        xs = _launch(lambda p, xd, od, w: sub_l0ffn(p, xd, od, w, tag="lf"),
                     h_l0f, xs)
        xs = _launch(lambda p, xd, od, w: sub_l1ffn(p, xd, od, w), h_l1f, xs)
        xs = _launch(lambda p, xd, od, w: sub_l1attn(p, xd, od, w), h_l1a, xs)
        xs = _launch(lambda p, xd, od, w: sub_l1ffn(p, xd, od, w), h_l1f, xs)
        return np.stack([np.asarray(o, dtype=np.float32) for o in xs])

    host = {}
    wmaps = {}
    for pre, h in (("A", h_l0f), ("B", h_l0a), ("C", h_l1f), ("D", h_l1a)):
        wmaps[pre] = {}
        for k, a in h.items():
            a = np.asarray(a)
            key = pre + "_" + k
            if k in ("ident", "ident64s", "rope_pi", "rope_cos", "rope_sin",
                     "fft_fre", "fft_fim", "fft_fimn"):
                key = "K_" + k
            if key not in host:
                host[key] = a
            wmaps[pre][k] = key

    p = Prog()
    nc = p.nc
    xd = nc.dram_tensor("x", [T, D], F32, kind="ExternalInput").ap()
    od = nc.dram_tensor("out", [T, D], F32, kind="ExternalOutput").ap()
    dr = {k: nc.dram_tensor(k, list(np.asarray(a).shape), F32,
                            kind="ExternalInput").ap() for k, a in host.items()}
    s0 = nc.dram_tensor("scr0", [T, D], F32, kind="Internal").ap()
    s1 = nc.dram_tensor("scr1", [T, D], F32, kind="Internal").ap()

    def wview(pre):
        return {k: dr[key] for k, key in wmaps[pre].items()}
    sub_l0ffn(p, xd, s0, wview("A"), tag="lf")
    sub_l0attn(p, s0, s1, wview("B"))
    sub_l0ffn(p, s1, s0, wview("A"), tag="lf")
    sub_l1ffn(p, s0, s1, wview("C"))
    sub_l1attn(p, s1, s0, wview("D"))
    sub_l1ffn(p, s0, od, wview("C"))
    p.es.close()
    hostc = dict(host)
    hostc["fconsts"] = p.const_array()
    in_maps = [dict(hostc, x=np.ascontiguousarray(x[i])) for i in range(8)]
    res = run_bass_kernel_spmd(nc, in_maps, core_ids=list(range(8)))
    return np.stack([np.asarray(res.results[i]["out"], dtype=np.float32)
                     for i in range(8)])
